# version 19
# baseline (speedup 1.0000x reference)
import math
import numpy as np
import ml_dtypes

import concourse.bass as bass
import concourse.mybir as mybir
from concourse.bass_utils import run_bass_kernel_spmd

F32 = mybir.dt.float32
BF16 = mybir.dt.bfloat16
I32 = mybir.dt.int32
AF = mybir.ActivationFunctionType
ALU = mybir.AluOpType
AX = mybir.AxisListType
bfdt = ml_dtypes.bfloat16

T = 2048
E = 2048
EC = E // 128
NSTRIP = T // 512
NTC = T // 128
H = 16
QL, KL = 1536, 512
DN, DR, DV = 128, 64, 128
HID = 5632
HC = HID // 128
DI = 4096
DST = 64
NH = 64
HD = 64
CONV = 4224
NCH = 16
LNEPS = 1e-5
RMSEPS_A = 1e-6
RMSEPS_B = 1e-5
NEG = -1e30
SCALE_ATTN = 1.0 / math.sqrt(DN + DR)
FCONSTS = [1e-5, 1e-6, 1.0, 0.0, -1.0, 0.5, 2.0, -0.5]


class Prog:
    ENGS = ("pe", "act", "dve", "pool", "sp")
    QSUBS = 6

    def __init__(self, nc):
        self.nc = nc
        self.ops = []
        self.cnt = {e: 0 for e in self.ENGS}
        self.qidx = {}
        for q in ("sp", "pool", "act"):
            self.qidx[q] = 0
            for i in range(self.QSUBS):
                self.cnt[f"d_{q}_{i}"] = 0

    def _fix(self, waits):
        return tuple((s, self.cnt[s]) if s.startswith("d_") else (s, v)
                     for (s, v) in waits)

    def do(self, eng, fn, waits=(), inc=True):
        incs = []
        if inc:
            self.cnt[eng] += 1
            incs.append((eng, 1))
        self.ops.append((eng, fn, self._fix(waits), tuple(incs)))
        return (eng, self.cnt[eng])

    def wait(self, eng, waits):
        if waits:
            self.ops.append((eng, None, self._fix(waits), ()))

    def dma(self, q, out, in_, waits=(), **kw):
        fixed = self._fix(waits)
        sem = f"d_{q}_{self.qidx[q] % self.QSUBS}"
        self.qidx[q] += 1
        self.cnt[sem] += 16
        def fn(e, out=out, in_=in_, kw=kw):
            return e.dma_start(out=out, in_=in_, **kw)
        self.ops.append((q if q != "sp" else "sp", fn, fixed, ((sem, 16),)))
        return (sem, self.cnt[sem])

    def tok(self, key):
        return (key, self.cnt[key])

    def barrier(self):
        snap = tuple((s, v) for s, v in self.cnt.items() if v > 0)
        for e in self.ENGS:
            self.ops.append((e, None, snap, ()))

    def emit(self):
        nc = self.nc
        sems = {name: nc.alloc_semaphore(name) for name in self.cnt}
        attr = {"pe": "tensor", "act": "scalar", "dve": "vector",
                "pool": "gpsimd", "sp": "sync"}
        with nc.Block() as block:
            for key in self.ENGS:
                def make(key):
                    def run(eng):
                        seen = {}
                        for (e, fn, waits, incs) in self.ops:
                            if e != key:
                                continue
                            for (s, v) in waits:
                                if seen.get(s, 0) < v:
                                    eng.wait_ge(sems[s], v)
                                    seen[s] = v
                            if fn is None:
                                continue
                            inst = fn(eng)
                            for (s, n) in incs:
                                inst.then_inc(sems[s], n)
                    return run
                getattr(block, attr[key])(make(key))
        return sems


class SB:
    BASE = 16512
    TOP = 229344

    def __init__(self, nc):
        self.nc = nc
        self.off = self.BASE

    def alloc(self, name, shape, dtype, off=None):
        sz = int(np.prod(shape[1:])) * mybir.dt.size(dtype)
        if off is None:
            off = self.off
            self.off = off + ((sz + 31) // 32) * 32
        assert off + sz <= self.TOP, f"SBUF overflow {name}: {off}+{sz}"
        return self.nc.alloc_sbuf_tensor_at(name, shape, dtype, offset=off)

    def mark(self):
        return self.off

    def reset(self, off):
        self.off = off


def _rope_tables():
    inv_freq = 1.0 / 10000.0 ** (np.arange(0, DR, 2, dtype=np.float32) / DR)
    t = np.arange(T, dtype=np.float32)
    freqs = np.outer(t, inv_freq)
    return np.cos(freqs).astype(np.float32), np.sin(freqs).astype(np.float32)


def _pe_perm():
    p = np.empty(64, dtype=np.int64)
    p[:32] = 2 * np.arange(32)
    p[32:] = 2 * np.arange(32) + 1
    return p


def prep_inputs(kw):
    f32 = np.float32
    pe = _pe_perm()
    d = {}
    d["wqa"] = np.ascontiguousarray(kw["L0_attn_a_w_q_a"], dtype=f32)
    wqb = np.asarray(kw["L0_attn_a_w_q_b"], dtype=f32)
    cols = []
    for h in range(H):
        cols.append(np.arange(h * 192, h * 192 + 128))
    for h in range(H):
        cols.append(h * 192 + 128 + pe)
    d["wqb"] = np.ascontiguousarray(wqb[:, np.concatenate(cols)])
    wkva = np.asarray(kw["L0_attn_a_w_kv_a"], dtype=f32)
    d["wkva"] = np.ascontiguousarray(
        np.concatenate([wkva[:, :KL], wkva[:, KL:][:, pe]], axis=1))
    wkvb = np.asarray(kw["L0_attn_a_w_kv_b"], dtype=f32)
    cols = [np.arange(h * 256, h * 256 + 128) for h in range(H)]
    cols += [np.arange(h * 256 + 128, h * 256 + 256) for h in range(H)]
    d["wkvb"] = np.ascontiguousarray(wkvb[:, np.concatenate(cols)])
    d["wout"] = np.ascontiguousarray(kw["L0_attn_a_w_out"], dtype=f32)
    d["qaln"] = np.ascontiguousarray(
        np.asarray(kw["L0_attn_a_q_a_ln"], f32).reshape(QL // 128, 128).T)
    d["kvln"] = np.ascontiguousarray(
        np.asarray(kw["L0_attn_a_kv_a_ln"], f32).reshape(KL // 128, 128).T)
    d["wg"] = np.ascontiguousarray(np.asarray(kw["L0_ffn_wq_gate"], np.int32).T)
    d["wu"] = np.ascontiguousarray(np.asarray(kw["L0_ffn_wq_up"], np.int32).T)
    d["wd"] = np.ascontiguousarray(np.asarray(kw["L0_ffn_wq_down"], np.int32).T)
    d["sg"] = np.ascontiguousarray(np.asarray(kw["L0_ffn_s_gate"], f32).reshape(HC, 128).T)
    d["su"] = np.ascontiguousarray(np.asarray(kw["L0_ffn_s_up"], f32).reshape(HC, 128).T)
    d["sd"] = np.ascontiguousarray(np.asarray(kw["L0_ffn_s_down"], f32).reshape(EC, 128).T)
    d["sdb"] = np.ascontiguousarray(np.broadcast_to(np.asarray(kw["L0_ffn_s_down"], f32), (128, E)))
    d["inpT"] = np.ascontiguousarray(np.asarray(kw["L0_attn_b_in_proj_w"], f32).T)
    d["outpT"] = np.ascontiguousarray(np.asarray(kw["L0_attn_b_out_proj_w"], f32).T)
    d["rmsw"] = np.ascontiguousarray(
        np.asarray(kw["L0_attn_b_rms_weight"], f32).reshape(EC, 128).T)
    d["rmsb"] = np.ascontiguousarray(
        np.broadcast_to(np.asarray(kw["L0_attn_b_rms_weight"], f32), (128, E)))
    d["normw"] = np.ascontiguousarray(
        np.asarray(kw["L0_attn_b_norm_w"], f32).reshape(DI // 128, 128).T)
    d["convw"] = np.ascontiguousarray(
        np.asarray(kw["L0_attn_b_conv_w"], f32).reshape(CONV // 128, 128, 4)
        .transpose(1, 0, 2).reshape(128, -1))
    d["convb"] = np.ascontiguousarray(
        np.asarray(kw["L0_attn_b_conv_b"], f32).reshape(CONV // 128, 128).T)
    d["dtb"] = np.ascontiguousarray(np.asarray(kw["L0_attn_b_dt_bias"], f32).reshape(NH, 1))
    d["alog"] = np.ascontiguousarray(np.asarray(kw["L0_attn_b_A_log"], f32).reshape(NH, 1))
    d["dbc"] = np.ascontiguousarray(
        np.broadcast_to(np.asarray(kw["L0_attn_b_D"], f32), (128, NH)))
    d["ident"] = np.eye(128, dtype=bfdt)
    d["identf"] = np.eye(128, dtype=f32)
    d["onesf"] = np.ones((1, 128), dtype=f32)
    d["negonesf"] = np.full((1, 128), -1.0, dtype=f32)
    d["fconst"] = np.ascontiguousarray(np.broadcast_to(np.array(FCONSTS, dtype=f32), (128, len(FCONSTS))))
    cos, sin = _rope_tables()
    d["costm"] = np.ascontiguousarray(cos.reshape(NTC, 128, 32).transpose(1, 0, 2).reshape(128, -1))
    d["sintm"] = np.ascontiguousarray(sin.reshape(NTC, 128, 32).transpose(1, 0, 2).reshape(128, -1))
    masks = np.zeros((4, 128, 512), dtype=np.float32)
    i = np.arange(128)[:, None]
    j = np.arange(512)[None, :]
    for v in range(4):
        masks[v] = np.where(j <= v * 128 + i, 0.0, NEG)
    d["masks"] = np.ascontiguousarray(masks.transpose(1, 0, 2).reshape(128, -1).astype(bfdt))
    mm = np.where(i <= j[:, :128], 0.0, NEG).astype(np.float32)
    d["mmask"] = np.ascontiguousarray(np.tile(mm, (1, 4)).astype(bfdt))
    return d


class MK:
    def __init__(self, taps=()):
        self.taps = set(taps)
        nc = bass.Bass("TRN2", target_bir_lowering=False)
        self.nc = nc
        self.p = Prog(nc)
        self.sb = SB(nc)
        dr = lambda n, s, t, k="ExternalInput": nc.dram_tensor(n, s, t, kind=k).ap()
        self.x_in = dr("x", [T, E], F32)
        self.wqa = dr("wqa", [E, QL], F32)
        self.wqb = dr("wqb", [QL, 3072], F32)
        self.wkva = dr("wkva", [E, 576], F32)
        self.wkvb = dr("wkvb", [KL, 4096], F32)
        self.wout = dr("wout", [E, E], F32)
        self.qaln = dr("qaln", [128, QL // 128], F32)
        self.kvln = dr("kvln", [128, KL // 128], F32)
        self.wg = dr("wg", [E, HID], I32)
        self.wu = dr("wu", [E, HID], I32)
        self.wd = dr("wd", [HID, E], I32)
        self.sg = dr("sg", [128, HC], F32)
        self.su = dr("su", [128, HC], F32)
        self.sd = dr("sd", [128, EC], F32)
        self.sdb = dr("sdb", [128, E], F32)
        self.inpT = dr("inpT", [E, 8384], F32)
        self.outpT = dr("outpT", [DI, E], F32)
        self.rmsw = dr("rmsw", [128, EC], F32)
        self.rmsb = dr("rmsb", [128, E], F32)
        self.normw = dr("normw", [128, DI // 128], F32)
        self.convw = dr("convw", [128, (CONV // 128) * 4], F32)
        self.convb = dr("convb", [128, CONV // 128], F32)
        self.dtb = dr("dtb", [NH, 1], F32)
        self.alog = dr("alog", [NH, 1], F32)
        self.dbc = dr("dbc", [128, NH], F32)
        self.ident = dr("ident", [128, 128], BF16)
        self.identf = dr("identf", [128, 128], F32)
        self.negonesf = dr("negonesf", [1, 128], F32)
        self.fconst = dr("fconst", [128, len(FCONSTS)], F32)
        self.onesf = dr("onesf", [1, 128], F32)
        self.costm = dr("costm", [128, NTC * 32], F32)
        self.sintm = dr("sintm", [128, NTC * 32], F32)
        self.masks = dr("masks", [128, 4 * 512], BF16)
        self.mmask = dr("mmask", [128, 512], BF16)
        self.out = dr("out", [T, E], F32, k="ExternalOutput")
        dri = lambda n, s, t: nc.dram_tensor(n, s, t, kind="Internal").ap()
        self.QN = dri("QN", [H * 128, T], BF16)
        self.QP = dri("QP", [H * 64, T], BF16)
        self.Vd = dri("Vd", [T, H * 128], BF16)
        self.OT = dri("OT", [H * 128, T], BF16)
        self.X1 = dri("X1", [T, E], F32)
        self.X2 = dri("X2", [T, E], F32)
        self.X3 = dri("X3", [T, E], F32)
        self.Zd = dri("Zd", [T, DI], BF16)
        self.XTd = dri("XTd", [T, DI], BF16)
        self.Yd = dri("Yd", [T, DI], BF16)
        self.YN = dri("YN", [DI // 128, NTC, 128, 128], BF16)
        self.tapd = {}
        for tname, shape, dt in [
            ("tap_h", [T, E], BF16), ("tap_qn", [H * 128, T], BF16),
            ("tap_qp", [H * 64, T], BF16), ("tap_kn", [H * 128, T], BF16),
            ("tap_kp", [64, T], BF16), ("tap_v", [T, H * 128], BF16),
            ("tap_ot", [H * 128, T], BF16), ("tap_x1", [T, E], F32),
            ("tap_x2", [T, E], F32), ("tap_z", [T, DI], BF16),
            ("tap_xt", [T, DI], BF16), ("tap_dt", [NH, T], F32),
            ("tap_y", [T, DI], BF16), ("tap_x3", [T, E], F32),
        ]:
            if tname in self.taps:
                self.tapd[tname] = dr(tname, shape, dt, k="ExternalOutput")

        sb = self.sb
        self.c_fc = sb.alloc("c_fc", [128, len(FCONSTS)], F32)
        self.c_ident = sb.alloc("c_ident", [128, 128], BF16)
        self.c_identf = sb.alloc("c_identf", [128, 128], F32)
        self.c_ones = sb.alloc("c_ones", [1, 128], F32)
        self.c_negones = sb.alloc("c_negones", [1, 128], F32)
        self.c_cos = sb.alloc("c_cos", [128, NTC * 32], F32)
        self.c_sin = sb.alloc("c_sin", [128, NTC * 32], F32)
        self.c_masks = sb.alloc("c_masks", [128, 4 * 512], BF16)
        self.c_mmask = sb.alloc("c_mmask", [128, 512], BF16)
        self.c_vec = sb.alloc("c_vec", [128, 256], F32)
        self.c_convw = sb.alloc("c_convw", [128, (CONV // 128) * 4], F32)
        self.c_dbc = sb.alloc("c_dbc", [128, NH], F32)
        self.c_small = sb.alloc("c_small", [NH, 8], F32)
        self.c_r2p1 = sb.alloc("c_r2p1", [128, NTC], F32)
        self.c_mr2 = sb.alloc("c_mr2", [128, NTC], F32)
        self.pers_end = sb.mark()

        self.PBF = nc.alloc_psum_tensor("PBF", [128, 2048], BF16)
        self.PSF = nc.alloc_psum_tensor("PSF", [128, 3072], F32)
        self._tr_war = {}
        self._load_consts()

    def _load_consts(self):
        p = self.p
        toks = []
        toks.append(p.dma("sp", self.c_fc[:], self.fconst[:, :]))
        for i, v in enumerate(FCONSTS):
            self.nc.const_aps.register(F32, v, self.c_fc[:, i:i + 1])
        toks.append(p.dma("sp", self.c_ident[:], self.ident[:, :]))
        toks.append(p.dma("sp", self.c_identf[:], self.identf[:, :]))
        toks.append(p.dma("sp", self.c_negones[:], self.negonesf[:, :]))
        toks.append(p.dma("sp", self.c_ones[:], self.onesf[:, :]))
        toks.append(p.dma("sp", self.c_cos[:], self.costm[:, :]))
        toks.append(p.dma("sp", self.c_sin[:], self.sintm[:, :]))
        toks.append(p.dma("sp", self.c_masks[:], self.masks[:, :]))
        toks.append(p.dma("sp", self.c_mmask[:], self.mmask[:, :]))
        toks.append(p.dma("sp", self.c_vec[:, 0:44], self.sg[:, :]))
        toks.append(p.dma("sp", self.c_vec[:, 44:88], self.su[:, :]))
        toks.append(p.dma("sp", self.c_vec[:, 88:104], self.sd[:, :]))
        toks.append(p.dma("sp", self.c_vec[:, 104:120], self.rmsw[:, :]))
        toks.append(p.dma("sp", self.c_vec[:, 120:152], self.normw[:, :]))
        toks.append(p.dma("sp", self.c_vec[:, 152:185], self.convb[:, :]))
        toks.append(p.dma("sp", self.c_vec[:, 185:197], self.qaln[:, :]))
        toks.append(p.dma("sp", self.c_vec[:, 197:201], self.kvln[:, :]))
        toks.append(p.dma("sp", self.c_convw[:], self.convw[:, :]))
        toks.append(p.dma("sp", self.c_dbc[:], self.dbc[:, :]))
        toks.append(p.dma("sp", self.c_small[0:NH, 0:1], self.dtb[:, :]))
        toks.append(p.dma("sp", self.c_small[0:NH, 1:2], self.alog[:, :]))
        self.const_tok = toks[-1]
        p.do("act", lambda e: e.activation(out=self.c_small[0:NH, 2:3],
                                           in_=self.c_small[0:NH, 1:2],
                                           func=AF.Exp, scale=1.0),
             waits=[self.const_tok])
        self.tok_A = p.do("dve", lambda e: e.tensor_scalar(
            out=self.c_small[0:NH, 2:3], in0=self.c_small[0:NH, 2:3],
            scalar1=-1.0, scalar2=None, op0=ALU.mult), waits=[p.tok("act")])
        p.barrier()

    def ln_stats(self, s_x, s_sum, s_ss, s_sc, s_bi, s_tmp, wait, width=E,
                 eps=LNEPS, sq_scr=None):
        p = self.p
        t1 = p.do("dve", lambda e: e.reduce_sum(s_sum, s_x, axis=AX.X), waits=[wait])
        t2 = p.do("act", lambda e: e.activation(out=sq_scr, in_=s_x, func=AF.Square,
                                                accum_out=s_ss), waits=[wait])
        t3 = p.do("dve", lambda e: e.tensor_scalar(out=s_sum, in0=s_sum,
                                                   scalar1=1.0 / width, scalar2=None,
                                                   op0=ALU.mult), waits=[t1])
        t4 = p.do("dve", lambda e: e.tensor_tensor(out=s_tmp, in0=s_sum, in1=s_sum,
                                                   op=ALU.mult), waits=[t3])
        t5 = p.do("dve", lambda e: e.scalar_tensor_tensor(out=s_ss, in0=s_ss,
                                                          scalar=1.0 / width, in1=s_tmp,
                                                          op0=ALU.mult, op1=ALU.subtract),
                  waits=[t2, t4])
        ta = p.do("act", lambda e: e.activation(out=s_tmp, in_=s_ss, func=AF.Sqrt,
                                                bias=eps, scale=1.0), waits=[t5])
        tr = p.do("dve", lambda e: e.reciprocal(out=s_sc, in_=s_tmp), waits=[ta])
        tb = p.do("dve", lambda e: e.scalar_tensor_tensor(out=s_bi, in0=s_sum,
                                                          scalar=-1.0, in1=s_sc,
                                                          op0=ALU.mult, op1=ALU.mult),
                  waits=[tr])
        return tb

    def transpose_tiles(self, src_fn, dst_fn, n, wait_src, batch=8,
                        evac_eng="dve"):
        p = self.p
        tok = None
        i = 0
        while i < n:
            cnt = min(batch, n - i)
            half = (self._tr_war.get("half", 0)) % 2
            self._tr_war["half"] = half + 1
            ps = self.PBF[:, half * 1024: half * 1024 + cnt * 128]
            waits = [wait_src] if wait_src else []
            if half in self._tr_war:
                waits.append(self._tr_war[half])
            for k in range(cnt):
                w = waits if k == 0 else []
                p.do("pe", lambda e, i=i, k=k, ps=ps: e.transpose(
                    ps[:, k * 128:(k + 1) * 128], src_fn(i + k), self.c_ident[:]),
                    waits=w, inc=(k == cnt - 1))
            tpe = p.tok("pe")
            tok = p.do(evac_eng, lambda e, i=i, cnt=cnt, ps=ps: (
                e.tensor_copy(dst_fn(i, cnt), ps[:, 0:cnt * 128])
                if evac_eng == "dve" else
                e.activation(out=dst_fn(i, cnt), in_=ps[:, 0:cnt * 128],
                             func=AF.Copy, scale=1.0)), waits=[tpe])
            self._tr_war[half] = tok
            i += cnt
        return tok

    def phase1a(self):
        sb, p = self.sb, self.p
        base = sb.mark()
        s_wqa = sb.alloc("s_wqa", [128, EC * QL], BF16)
        s_wqb = sb.alloc("s_wqb", [128, 12 * 3072], BF16)
        s_x = sb.alloc("p1_x", [128, E], F32)
        s_h = sb.alloc("p1_h", [128, E], BF16)
        s_sq = sb.alloc("p1_sq", [128, E], BF16)
        s_hfm = sb.alloc("p1_hfm", [128, EC * 512], BF16)
        s_qan = sb.alloc("p1_qan", [128, QL], BF16)
        s_qanfm = sb.alloc("p1_qanfm", [128, 12 * 512], BF16)
        s_stat = sb.alloc("p1_stat", [128, 16], F32)
        s_qev = sb.alloc("p1_qev", [128, 2 * 512], BF16)
        s_qpe = sb.alloc("p1_qpe", [128, 1024], BF16)
        s_rt1 = sb.alloc("p1_rt1", [128, 512], F32)
        s_rt2 = sb.alloc("p1_rt2", [128, 512], F32)
        ps_qa = self.PSF[:, 0:1536]
        ps_pe = self.PSF[:, 0:1024]
        ps_qn = self.PSF[:, 2048:3072]

        wqa_r = self.wqa.rearrange("(c p) n -> p c n", p=128)
        wqb_r = self.wqb.rearrange("(c p) n -> p c n", p=128)
        for half in range(2):
            p.dma("pool", s_wqa[:].rearrange("p (c n) -> p c n", c=EC)[:, half * 8:(half + 1) * 8, :],
                  wqa_r[:, half * 8:(half + 1) * 8, :])
        for half in range(2):
            wtok = p.dma("pool", s_wqb[:].rearrange("p (c n) -> p c n", c=12)[:, half * 6:(half + 1) * 6, :],
                         wqb_r[:, half * 6:(half + 1) * 6, :])
        for c in range(12):
            p.do("dve", lambda e, c=c: e.tensor_scalar(
                out=s_wqb[:, c * 3072:(c + 1) * 3072],
                in0=s_wqb[:, c * 3072:(c + 1) * 3072],
                scalar1=self.c_vec[:, 185 + c:186 + c], scalar2=None, op0=ALU.mult),
                waits=[wtok, self.const_tok] if c == 0 else [])
        wfold = p.tok("dve")

        last_qn_dma = {}
        last_qp_dma = {}
        for s in range(NSTRIP):
            for tl in range(4):
                tc = s * 4 + tl
                tx = p.dma("sp", s_x[:], self.x_in[tc * 128:(tc + 1) * 128, :],
                           waits=[p.tok("act"), p.tok("dve")])
                tb = self.ln_stats(s_x[:], s_stat[:, 0:1], s_stat[:, 1:2],
                                   s_stat[:, 2:3], s_stat[:, 3:4], s_stat[:, 4:5],
                                   tx, sq_scr=s_sq[:])
                th = p.do("dve", lambda e: e.tensor_scalar(
                    out=s_h[:], in0=s_x[:], scalar1=s_stat[:, 2:3],
                    scalar2=s_stat[:, 3:4], op0=ALU.mult, op1=ALU.add),
                    waits=[tb, p.tok("pe")])
                if "tap_h" in self.tapd:
                    p.dma("sp", self.tapd["tap_h"][tc * 128:(tc + 1) * 128, :], s_h[:],
                          waits=[th])
                self.transpose_tiles(
                    lambda i: s_h[:, i * 128:(i + 1) * 128],
                    lambda i0, cnt, tl=tl: bass.AP(s_hfm, i0 * 512 + tl * 128,
                                                   [[EC * 512, 128], [512, cnt], [1, 128]]),
                    EC, th, batch=8)
            tstrip = p.tok("dve")
            for tl in range(4):
                for seg in range(3):
                    for ec in range(EC):
                        p.do("pe", lambda e, tl=tl, seg=seg, ec=ec: e.matmul(
                            ps_qa[:, seg * 512:(seg + 1) * 512],
                            s_hfm[:, ec * 512 + tl * 128: ec * 512 + (tl + 1) * 128],
                            s_wqa[:, ec * QL + seg * 512: ec * QL + (seg + 1) * 512],
                            start=(ec == 0), stop=(ec == EC - 1)),
                            waits=([tstrip, p.tok("act")] if (seg == 0 and ec == 0) else []),
                            inc=(ec == EC - 1))
                tqa = p.tok("pe")
                tsq = None
                for seg in range(3):
                    tsq = p.do("act", lambda e, seg=seg: e.activation(
                        out=s_sq[:, 0:512], in_=ps_qa[:, seg * 512:(seg + 1) * 512],
                        func=AF.Square, accum_out=s_stat[:, 5 + seg:6 + seg]),
                        waits=[tqa] if seg == 0 else [tsq])
                ta1 = p.do("dve", lambda e: e.tensor_tensor(
                    out=s_stat[:, 8:9], in0=s_stat[:, 5:6], in1=s_stat[:, 6:7],
                    op=ALU.add), waits=[tsq])
                ta2 = p.do("dve", lambda e: e.tensor_tensor(
                    out=s_stat[:, 8:9], in0=s_stat[:, 8:9], in1=s_stat[:, 7:8],
                    op=ALU.add), waits=[ta1])
                ts2 = p.do("act", lambda e: e.activation(
                    out=s_stat[:, 9:10], in_=s_stat[:, 8:9], func=AF.Sqrt,
                    bias=RMSEPS_A, scale=1.0 / QL), waits=[ta2])
                tr = p.do("dve", lambda e: e.reciprocal(
                    out=s_stat[:, 10:11], in_=s_stat[:, 9:10]), waits=[ts2])
                for seg in range(3):
                    p.do("act", lambda e, seg=seg: e.activation(
                        out=s_qan[:, seg * 512:(seg + 1) * 512],
                        in_=ps_qa[:, seg * 512:(seg + 1) * 512],
                        func=AF.Copy, scale=s_stat[:, 10:11]), waits=[tr])
                tqan = p.tok("act")
                self.transpose_tiles(
                    lambda i: s_qan[:, i * 128:(i + 1) * 128],
                    lambda i0, cnt, tl=tl: bass.AP(s_qanfm, i0 * 512 + tl * 128,
                                                   [[12 * 512, 128], [512, cnt], [1, 128]]),
                    12, tqan, batch=4)
            tqanfm = p.tok("dve")
            for hc in range(H):
                bank = hc % 2
                waits = [tqanfm, wfold]
                if bank in last_qn_dma:
                    waits.append(last_qn_dma[bank])
                for qc in range(12):
                    p.do("pe", lambda e, hc=hc, qc=qc, bank=bank: e.matmul(
                        ps_qn[:, bank * 512:(bank + 1) * 512],
                        s_wqb[:, qc * 3072 + hc * 128: qc * 3072 + (hc + 1) * 128],
                        s_qanfm[:, qc * 512:(qc + 1) * 512],
                        start=(qc == 0), stop=(qc == 11)),
                        waits=(waits if qc == 0 else []), inc=(qc == 11))
                tq = p.tok("pe")
                tev = p.do("act", lambda e, hc=hc, bank=bank: e.activation(
                    out=s_qev[:, bank * 512:(bank + 1) * 512],
                    in_=ps_qn[:, bank * 512:(bank + 1) * 512],
                    func=AF.Copy, scale=SCALE_ATTN), waits=[tq])
                last_qn_dma[bank] = p.dma(
                    "sp", self.QN[hc * 128:(hc + 1) * 128, s * 512:(s + 1) * 512],
                    s_qev[:, bank * 512:(bank + 1) * 512], waits=[tev])
            tqn_act = p.tok("act")
            for tl in range(4):
                tc = s * 4 + tl
                for seg in range(2):
                    for qc in range(12):
                        p.do("pe", lambda e, tl=tl, seg=seg, qc=qc: e.matmul(
                            ps_pe[:, seg * 512:(seg + 1) * 512],
                            s_qanfm[:, qc * 512 + tl * 128: qc * 512 + (tl + 1) * 128],
                            s_wqb[:, qc * 3072 + 2048 + seg * 512: qc * 3072 + 2048 + (seg + 1) * 512],
                            start=(qc == 0), stop=(qc == 11)),
                            waits=([p.tok("dve"), tqn_act] if (seg == 0 and qc == 0) else []),
                            inc=(qc == 11))
                tpe = p.tok("pe")
                pp = ps_pe.rearrange("p (h d) -> p h d", d=64)
                cosap = bass.AP(self.c_cos, tc * 32, [[NTC * 32, 128], [0, 16], [1, 32]])
                sinap = bass.AP(self.c_sin, tc * 32, [[NTC * 32, 128], [0, 16], [1, 32]])
                qq = s_qpe[:].rearrange("p (h d) -> p h d", d=64)
                r1 = s_rt1[:].rearrange("p (h d) -> p h d", d=32)
                r2 = s_rt2[:].rearrange("p (h d) -> p h d", d=32)
                p.do("dve", lambda e, pp=pp, cosap=cosap: e.scalar_tensor_tensor(
                    out=s_rt1[:], in0=pp[:, :, 0:32], scalar=SCALE_ATTN, in1=cosap,
                    op0=ALU.mult, op1=ALU.mult), waits=[tpe])
                p.do("dve", lambda e, pp=pp, sinap=sinap: e.scalar_tensor_tensor(
                    out=s_rt2[:], in0=pp[:, :, 32:64], scalar=SCALE_ATTN, in1=sinap,
                    op0=ALU.mult, op1=ALU.mult))
                p.do("dve", lambda e, qq=qq, r1=r1, r2=r2: e.tensor_tensor(
                    out=qq[:, :, 0:32], in0=r1, in1=r2, op=ALU.subtract))
                p.do("dve", lambda e, pp=pp, cosap=cosap: e.scalar_tensor_tensor(
                    out=s_rt1[:], in0=pp[:, :, 32:64], scalar=SCALE_ATTN, in1=cosap,
                    op0=ALU.mult, op1=ALU.mult))
                p.do("dve", lambda e, pp=pp, sinap=sinap: e.scalar_tensor_tensor(
                    out=s_rt2[:], in0=pp[:, :, 0:32], scalar=SCALE_ATTN, in1=sinap,
                    op0=ALU.mult, op1=ALU.mult))
                trope = p.do("dve", lambda e, qq=qq, r1=r1, r2=r2: e.tensor_tensor(
                    out=qq[:, :, 32:64], in0=r1, in1=r2, op=ALU.add))
                for g in range(2):
                    waits = [trope]
                    if g in last_qp_dma:
                        waits.append(last_qp_dma[g])
                    tev = self.transpose_tiles(
                        lambda i, g=g: s_qpe[:, (g * 4 + i) * 128:(g * 4 + i + 1) * 128],
                        lambda i0, cnt, g=g: s_qev[:, g * 512 + i0 * 128: g * 512 + (i0 + cnt) * 128],
                        4, None, batch=4) if False else None
                    half = self._tr_war.get("half", 0) % 2
                    self._tr_war["half"] = half + 1
                    ps = self.PBF[:, half * 1024: half * 1024 + 512]
                    if half in self._tr_war:
                        waits.append(self._tr_war[half])
                    for k in range(4):
                        p.do("pe", lambda e, g=g, k=k, ps=ps: e.transpose(
                            ps[:, k * 128:(k + 1) * 128],
                            s_qpe[:, (g * 4 + k) * 128:(g * 4 + k + 1) * 128],
                            self.c_ident[:]), waits=(waits if k == 0 else []),
                            inc=(k == 3))
                    tev = p.do("dve", lambda e, g=g, ps=ps: e.tensor_copy(
                        s_qev[:, g * 512:(g + 1) * 512], ps[:, 0:512]),
                        waits=[p.tok("pe")])
                    self._tr_war[half] = tev
                    last_qp_dma[g] = p.dma(
                        "sp",
                        bass.AP(self.QP.tensor, (g * 4 * 128) * T + s * 512 + tl * 128,
                                [[T, 128], [128 * T, 4], [1, 128]]),
                        s_qev[:, g * 512:(g + 1) * 512].rearrange(
                            "p (c n) -> p c n", c=4), waits=[tev])
        sb.reset(base)
        p.barrier()




    def phase1b(self):
        sb, p = self.sb, self.p
        self.s_kn = sb.alloc("s_kn", [128, H * T], BF16)
        self.s_kp = sb.alloc("s_kp", [128, T], BF16)
        self.kv_base = sb.mark()
        base = sb.mark()
        s_wkva = sb.alloc("s_wkva", [128, EC * 576], BF16)
        s_wkvb = sb.alloc("s_wkvb", [128, 4 * 4096], BF16)
        s_x = sb.alloc("pb_x", [128, E], F32)
        s_h = sb.alloc("pb_h", [128, E], BF16)
        s_sq = sb.alloc("pb_sq", [128, E], BF16)
        s_hfm = sb.alloc("pb_hfm", [128, EC * 128], BF16)
        s_ckv = sb.alloc("pb_ckv", [128, KL], BF16)
        s_ckvfm = sb.alloc("pb_ckvfm", [128, 4 * 512], BF16)
        s_stat = sb.alloc("pb_stat", [128, 16], F32)
        s_kpe = sb.alloc("pb_kpe", [128, 64], BF16)
        s_vev = sb.alloc("pb_vev", [128, 2 * 512], BF16)
        s_rt1 = sb.alloc("pb_rt1", [128, 32], F32)
        s_rt2 = sb.alloc("pb_rt2", [128, 32], F32)
        ps_ckv = self.PSF[:, 0:512]
        ps_kpe = self.PSF[:, 512:576]
        ps_kn = self.PSF[:, 1024:2048]
        ps_v = self.PSF[:, 2048:3072]

        wkva_r = self.wkva.rearrange("(c p) n -> p c n", p=128)
        wkvb_r = self.wkvb.rearrange("(c p) n -> p c n", p=128)
        for half in range(2):
            p.dma("pool", s_wkva[:].rearrange("p (c n) -> p c n", c=EC)[:, half * 8:(half + 1) * 8, :],
                  wkva_r[:, half * 8:(half + 1) * 8, :])
        wtok = p.dma("pool", s_wkvb[:].rearrange("p (c n) -> p c n", c=4), wkvb_r)
        for c in range(4):
            p.do("dve", lambda e, c=c: e.tensor_scalar(
                out=s_wkvb[:, c * 4096:(c + 1) * 4096],
                in0=s_wkvb[:, c * 4096:(c + 1) * 4096],
                scalar1=self.c_vec[:, 197 + c:198 + c], scalar2=None, op0=ALU.mult),
                waits=[wtok, self.const_tok] if c == 0 else [])
        wfold = p.tok("dve")

        last_v_dma = {}
        last_kn = {}
        for s in range(NSTRIP):
            for tl in range(4):
                tc = s * 4 + tl
                tx = p.dma("sp", s_x[:], self.x_in[tc * 128:(tc + 1) * 128, :],
                           waits=[p.tok("act"), p.tok("dve")])
                tb = self.ln_stats(s_x[:], s_stat[:, 0:1], s_stat[:, 1:2],
                                   s_stat[:, 2:3], s_stat[:, 3:4], s_stat[:, 4:5],
                                   tx, sq_scr=s_sq[:])
                th = p.do("dve", lambda e: e.tensor_scalar(
                    out=s_h[:], in0=s_x[:], scalar1=s_stat[:, 2:3],
                    scalar2=s_stat[:, 3:4], op0=ALU.mult, op1=ALU.add),
                    waits=[tb, p.tok("pe")])
                self.transpose_tiles(
                    lambda i: s_h[:, i * 128:(i + 1) * 128],
                    lambda i0, cnt: bass.AP(s_hfm, i0 * 128,
                                            [[EC * 128, 128], [128, cnt], [1, 128]]),
                    EC, th, batch=8)
                thfm = p.tok("dve")
                for ec in range(EC):
                    p.do("pe", lambda e, ec=ec: e.matmul(
                        ps_ckv[:, 0:512],
                        s_hfm[:, ec * 128:(ec + 1) * 128],
                        s_wkva[:, ec * 576: ec * 576 + 512],
                        start=(ec == 0), stop=(ec == EC - 1)),
                        waits=([thfm, p.tok("act"), wtok] if ec == 0 else []),
                        inc=(ec == EC - 1))
                for ec in range(EC):
                    p.do("pe", lambda e, ec=ec: e.matmul(
                        ps_kpe[:, 0:64],
                        s_hfm[:, ec * 128:(ec + 1) * 128],
                        s_wkva[:, ec * 576 + 512: (ec + 1) * 576],
                        start=(ec == 0), stop=(ec == EC - 1)),
                        inc=(ec == EC - 1))
                tckv = p.tok("pe")
                p.do("act", lambda e: e.activation(
                    out=s_sq[:, 0:512], in_=ps_ckv[:, 0:512],
                    func=AF.Square, accum_out=s_stat[:, 5:6]), waits=[tckv])
                ts2 = p.do("act", lambda e: e.activation(
                    out=s_stat[:, 9:10], in_=s_stat[:, 5:6], func=AF.Sqrt,
                    bias=RMSEPS_A, scale=1.0 / KL), waits=[p.tok("act")])
                trr = p.do("dve", lambda e: e.reciprocal(
                    out=s_stat[:, 10:11], in_=s_stat[:, 9:10]), waits=[ts2])
                tckvn = p.do("act", lambda e: e.activation(
                    out=s_ckv[:, 0:512], in_=ps_ckv[:, 0:512],
                    func=AF.Copy, scale=s_stat[:, 10:11]), waits=[trr])
                cosap = bass.AP(self.c_cos, tc * 32, [[NTC * 32, 128], [1, 32]])
                sinap = bass.AP(self.c_sin, tc * 32, [[NTC * 32, 128], [1, 32]])
                p.do("dve", lambda e, cosap=cosap: e.tensor_tensor(
                    out=s_rt1[:], in0=ps_kpe[:, 0:32], in1=cosap, op=ALU.mult),
                    waits=[tckv])
                p.do("dve", lambda e, sinap=sinap: e.tensor_tensor(
                    out=s_rt2[:], in0=ps_kpe[:, 32:64], in1=sinap, op=ALU.mult))
                p.do("dve", lambda e: e.tensor_tensor(
                    out=s_kpe[:, 0:32], in0=s_rt1[:], in1=s_rt2[:], op=ALU.subtract))
                p.do("dve", lambda e, cosap=cosap: e.tensor_tensor(
                    out=s_rt1[:], in0=ps_kpe[:, 32:64], in1=cosap, op=ALU.mult))
                p.do("dve", lambda e, sinap=sinap: e.tensor_tensor(
                    out=s_rt2[:], in0=ps_kpe[:, 0:32], in1=sinap, op=ALU.mult))
                tkpe = p.do("dve", lambda e: e.tensor_tensor(
                    out=s_kpe[:, 32:64], in0=s_rt1[:], in1=s_rt2[:], op=ALU.add))
                half = self._tr_war.get("half", 0) % 2
                self._tr_war["half"] = half + 1
                pstr = self.PBF[:, half * 1024: half * 1024 + 128]
                w = [tkpe] + ([self._tr_war[half]] if half in self._tr_war else [])
                p.do("pe", lambda e, pstr=pstr: e.transpose(
                    pstr[0:64, 0:128], s_kpe[:, 0:64], self.c_ident[:]), waits=w)
                tkp = p.do("dve", lambda e, pstr=pstr, tc=tc: e.tensor_copy(
                    self.s_kp[0:64, tc * 128:(tc + 1) * 128], pstr[0:64, 0:128]),
                    waits=[p.tok("pe")])
                self._tr_war[half] = tkp
                self.transpose_tiles(
                    lambda i: s_ckv[:, i * 128:(i + 1) * 128],
                    lambda i0, cnt, tl=tl: bass.AP(s_ckvfm, i0 * 512 + tl * 128,
                                                   [[4 * 512, 128], [512, cnt], [1, 128]]),
                    4, tckvn, batch=4)
            tstrip = p.tok("dve")
            for hc in range(H):
                bank = hc % 2
                for kc in range(4):
                    p.do("pe", lambda e, hc=hc, kc=kc, bank=bank: e.matmul(
                        ps_kn[:, bank * 512:(bank + 1) * 512],
                        s_wkvb[:, kc * 4096 + hc * 128: kc * 4096 + (hc + 1) * 128],
                        s_ckvfm[:, kc * 512:(kc + 1) * 512],
                        start=(kc == 0), stop=(kc == 3)),
                        waits=([tstrip, wfold] + ([last_kn[bank]] if bank in last_kn else [])
                               if kc == 0 else []),
                        inc=(kc == 3))
                last_kn[bank] = p.do("dve", lambda e, hc=hc, bank=bank, s=s: e.tensor_copy(
                    self.s_kn[:, hc * T + s * 512:(hc * T) + (s + 1) * 512],
                    ps_kn[:, bank * 512:(bank + 1) * 512]), waits=[p.tok("pe")])
            for tl in range(4):
                tc = s * 4 + tl
                for seg in range(4):
                    bank = seg % 2
                    w0 = [p.tok("dve")]
                    if (bank,) in last_v_dma:
                        w0.append(last_v_dma[(bank,)])
                    for kc in range(4):
                        p.do("pe", lambda e, tl=tl, seg=seg, kc=kc, bank=bank: e.matmul(
                            ps_v[:, bank * 512:(bank + 1) * 512],
                            s_ckvfm[:, kc * 512 + tl * 128: kc * 512 + (tl + 1) * 128],
                            s_wkvb[:, kc * 4096 + 2048 + seg * 512: kc * 4096 + 2048 + (seg + 1) * 512],
                            start=(kc == 0), stop=(kc == 3)),
                            waits=(w0 if kc == 0 else []), inc=(kc == 3))
                    tev = p.do("act", lambda e, bank=bank: e.activation(
                        out=s_vev[:, bank * 512: bank * 512 + 512],
                        in_=ps_v[:, bank * 512:(bank + 1) * 512],
                        func=AF.Copy, scale=1.0), waits=[p.tok("pe")])
                    last_v_dma[(bank,)] = p.dma(
                        "sp", self.Vd[tc * 128:(tc + 1) * 128, seg * 512:(seg + 1) * 512],
                        s_vev[:, bank * 512: bank * 512 + 512], waits=[tev])
        sb.reset(base)
        p.barrier()

    def phase2(self):
        sb, p = self.sb, self.p
        base = sb.mark()
        s_v = sb.alloc("p2_v", [128, NTC * 2048], BF16)
        s_qn = sb.alloc("p2_qn", [128, H * 512], BF16)
        s_qp = sb.alloc("p2_qp", [128, 8 * 512], BF16)
        s_p = sb.alloc("p2_p", [128, 2 * 4 * 2048], BF16)
        s_pt = sb.alloc("p2_pt", [128, 2 * 1024], BF16)
        s_otev = sb.alloc("p2_otev", [128, 2 * 512], BF16)
        s_st = sb.alloc("p2_st", [128, 64], F32)
        ps_O = self.PSF[:, 2048:2560]

        vr = self.Vd.rearrange("(c p) n -> p c n", p=128)
        vv = s_v[:].rearrange("p (c n) -> p c n", c=NTC)
        tv = []
        for qtr in range(2):
            tv.append(p.dma("sp", vv[:, qtr * 8:(qtr + 1) * 8, :], vr[:, qtr * 8:(qtr + 1) * 8, :]))
        tkp2 = p.dma("pool", self.s_kp[64:128, :], self.s_kp[0:64, :])
        qnr = self.QN.rearrange("(c p) t -> p c t", p=128)
        qpr = self.QP.rearrange("(c p) t -> p c t", p=128)

        pending = None
        state = {"oev": {}, "odma": {}, "gi": 0}

        def emit_ptpv(info):
            s, h, par, nseg, tnorm = info
            nkt = nseg * 4
            opar = state["gi"] % 2
            state["gi"] += 1
            psO = self.PSF[:, 2048 + opar * 512: 2048 + (opar + 1) * 512]
            for kp in range(nkt // 2):
                kt0 = kp * 2
                half = self._tr_war.get("half", 0) % 2
                self._tr_war["half"] = half + 1
                psb = self.PBF[:, half * 1024: half * 1024 + 1024]
                w0 = list(tnorm)
                if half in self._tr_war:
                    w0.append(self._tr_war[half])
                for j in range(8):
                    kt, qt = kt0 + j // 4, j % 4
                    p.do("pe", lambda e, kt=kt, qt=qt, j=j, psb=psb, par=par: e.transpose(
                        psb[:, j * 128:(j + 1) * 128],
                        s_p[:, (par * 4 + qt) * 2048 + kt * 128:
                            (par * 4 + qt) * 2048 + (kt + 1) * 128],
                        self.c_ident[:]),
                        waits=(w0 if j == 0 else []), inc=(j == 7))
                tev = p.do("dve", lambda e, half=half, psb=psb: e.tensor_copy(
                    s_pt[:, half * 1024:(half + 1) * 1024], psb[:]),
                    waits=[p.tok("pe")])
                self._tr_war[half] = tev
                for j2 in range(2):
                    kt = kt0 + j2
                    w1 = [tev]
                    if kt == 0 and opar in state["oev"]:
                        w1.append(state["oev"][opar])
                    p.do("pe", lambda e, h=h, kt=kt, half=half, j2=j2, nkt=nkt, psO=psO: e.matmul(
                        psO, s_v[:, kt * 2048 + h * 128: kt * 2048 + (h + 1) * 128],
                        s_pt[:, half * 1024 + j2 * 512: half * 1024 + (j2 + 1) * 512],
                        start=(kt == 0), stop=(kt == nkt - 1)),
                        waits=(w1 if j2 == 0 else []), inc=(kt == nkt - 1))
            tO = p.tok("pe")
            w2 = [tO] + ([state["odma"][opar]] if opar in state["odma"] else [])
            tevo = p.do("dve", lambda e, opar=opar, psO=psO: e.tensor_copy(
                s_otev[:, opar * 512:(opar + 1) * 512], psO), waits=w2)
            state["oev"][opar] = tevo
            state["odma"][opar] = p.dma(
                "sp", self.OT[h * 128:(h + 1) * 128, s * 512:(s + 1) * 512],
                s_otev[:, opar * 512:(opar + 1) * 512], waits=[tevo])

        for s in range(NSTRIP):
            nseg = s + 1
            tq1 = p.dma("sp", s_qn[:].rearrange("p (c t) -> p c t", c=H),
                        qnr[:, :, s * 512:(s + 1) * 512], waits=[p.tok("pe")])
            tq2 = p.dma("sp", s_qp[:].rearrange("p (c t) -> p c t", c=8),
                        qpr[:, :, s * 512:(s + 1) * 512])
            bank_free = {}
            strip_start = p.tok("act")
            first_mm = True
            for h in range(H):
                par = (s * H + h) % 2
                texp = []
                bsets = {1: 4, 2: 2, 3: 1, 4: 1}[nseg]
                for qt in range(4):
                    bset = qt % bsets
                    b0 = bset * nseg * 512
                    for seg in range(nseg):
                        w0 = [bank_free.get((bset, seg), strip_start)]
                        if first_mm:
                            w0 += [tq1, tq2, tkp2] + tv
                            first_mm = False
                        p.do("pe", lambda e, h=h, qt=qt, seg=seg, b0=b0: e.matmul(
                            self.PSF[:, b0 + seg * 512: b0 + (seg + 1) * 512],
                            s_qn[:, h * 512 + qt * 128: h * 512 + (qt + 1) * 128],
                            self.s_kn[:, h * T + seg * 512: h * T + (seg + 1) * 512],
                            start=True, stop=False),
                            waits=w0, inc=False)
                        p.do("pe", lambda e, h=h, qt=qt, seg=seg, nseg=nseg, b0=b0: e.matmul(
                            self.PSF[:, b0 + seg * 512: b0 + (seg + 1) * 512],
                            s_qp[(h % 2) * 64:(h % 2) * 64 + 64,
                                 (h // 2) * 512 + qt * 128: (h // 2) * 512 + (qt + 1) * 128],
                            self.s_kp[(h % 2) * 64:(h % 2) * 64 + 64,
                                      seg * 512:(seg + 1) * 512],
                            start=False, stop=(seg != nseg - 1)),
                            inc=(seg != nseg - 1))
                        if seg == nseg - 1:
                            p.do("pe", lambda e, qt=qt, nseg=nseg, b0=b0: e.matmul(
                                self.PSF[:, b0 + (nseg - 1) * 512: b0 + nseg * 512],
                                self.c_ident[:], self.c_masks[:, qt * 512:(qt + 1) * 512],
                                start=False, stop=True), inc=True)
                        tSseg = p.tok("pe")
                        pcol = par * 32 + 16 + qt * 4 + seg
                        te = p.do("act", lambda e, qt=qt, seg=seg, par=par, b0=b0, pcol=pcol: e.activation(
                            out=s_p[:, (par * 4 + qt) * 2048 + seg * 512:
                                    (par * 4 + qt) * 2048 + (seg + 1) * 512],
                            in_=self.PSF[:, b0 + seg * 512: b0 + (seg + 1) * 512],
                            func=AF.Exp, scale=1.0,
                            accum_out=s_st[:, pcol:pcol + 1]),
                            waits=[tSseg])
                        bank_free[(bset, seg)] = te
                        texp.append(te)
                    p.do("dve", lambda e, qt=qt, nseg=nseg, par=par: e.reduce_sum(
                        s_st[:, par * 32 + 4 + qt:par * 32 + 5 + qt],
                        s_st[:, par * 32 + 16 + qt * 4: par * 32 + 16 + qt * 4 + nseg],
                        axis=AX.X), waits=[texp[-1]])
                tsum = p.tok("dve")
                trec = p.do("dve", lambda e, par=par: e.reciprocal(
                    out=s_st[:, par * 32 + 8:par * 32 + 12],
                    in_=s_st[:, par * 32 + 4:par * 32 + 8]), waits=[tsum] + texp)
                tnorm = []
                for qt in range(4):
                    tn = p.do("dve", lambda e, qt=qt, nseg=nseg, par=par: e.tensor_scalar(
                        out=s_p[:, (par * 4 + qt) * 2048: (par * 4 + qt) * 2048 + nseg * 512],
                        in0=s_p[:, (par * 4 + qt) * 2048: (par * 4 + qt) * 2048 + nseg * 512],
                        scalar1=s_st[:, par * 32 + 8 + qt:par * 32 + 9 + qt],
                        scalar2=None, op0=ALU.mult), waits=[trec])
                    tnorm.append(tn)
                info = (s, h, par, nseg, tnorm)
                if pending is not None:
                    emit_ptpv(pending)
                pending = info
        if pending is not None:
            emit_ptpv(pending)
        sb.reset(base)
        p.barrier()

    def phase2b(self):
        sb, p = self.sb, self.p
        base = self.sb.BASE
        sb.reset(max(base, self.pers_end))
        base = sb.mark()
        s_otf = sb.alloc("p2b_ot", [128, H * T], BF16)
        s_wo = sb.alloc("p2b_wo", [128, EC * E], BF16)
        s_x = sb.alloc("p2b_x", [128, E], F32)
        s_x1 = sb.alloc("p2b_x1", [128, E], F32)
        otr = self.OT.rearrange("(c p) t -> p c t", p=128)
        ov = s_otf[:].rearrange("p (c t) -> p c t", c=H)
        for qtr in range(2):
            tot = p.dma("sp", ov[:, qtr * 8:(qtr + 1) * 8, :], otr[:, qtr * 8:(qtr + 1) * 8, :])
        wor = self.wout.rearrange("(c p) n -> p c n", p=128)
        wv = s_wo[:].rearrange("p (c n) -> p c n", c=EC)
        for qtr in range(2):
            tw = p.dma("pool", wv[:, qtr * 8:(qtr + 1) * 8, :], wor[:, qtr * 8:(qtr + 1) * 8, :])
        for tc in range(NTC):
            tx = p.dma("sp", s_x[:], self.x_in[tc * 128:(tc + 1) * 128, :],
                       waits=[p.tok("dve")])
            for eseg in range(4):
                bank = eseg % 2
                for dvc in range(EC):
                    p.do("pe", lambda e, tc=tc, eseg=eseg, dvc=dvc, bank=bank: e.matmul(
                        self.PSF[:, bank * 512:(bank + 1) * 512],
                        s_otf[:, dvc * T + tc * 128: dvc * T + (tc + 1) * 128],
                        s_wo[:, dvc * E + eseg * 512: dvc * E + (eseg + 1) * 512],
                        start=(dvc == 0), stop=(dvc == EC - 1)),
                        waits=([tot, tw, p.tok("dve")] if dvc == 0 else []),
                        inc=(dvc == EC - 1))
                tmm = p.tok("pe")
                p.do("dve", lambda e, eseg=eseg, bank=bank: e.tensor_tensor(
                    out=s_x1[:, eseg * 512:(eseg + 1) * 512],
                    in0=self.PSF[:, bank * 512:(bank + 1) * 512],
                    in1=s_x[:, eseg * 512:(eseg + 1) * 512], op=ALU.add),
                    waits=[tmm, tx])
            tev = p.tok("dve")
            p.dma("sp", self.X1[tc * 128:(tc + 1) * 128, :], s_x1[:], waits=[tev])
            if "tap_x1" in self.tapd:
                p.dma("sp", self.tapd["tap_x1"][tc * 128:(tc + 1) * 128, :], s_x1[:],
                      waits=[tev])
        sb.reset(base)
        p.barrier()

    def phase_ffn(self, x_src, x_dst, tap=None):
        sb, p = self.sb, self.p
        base = sb.mark()
        off_hfm = sb.mark()
        s_hfm = sb.alloc("f_hfm", [128, EC * 1024], BF16)
        s_dfm = sb.alloc("f_dfm", [128, EC * 1024], BF16, off=off_hfm)
        s_act = sb.alloc("f_act", [128, HC * 1024], BF16)
        s_wgu = sb.alloc("f_wgu", [128, 4 * EC * 128], BF16)
        s_wd = sb.alloc("f_wd", [128, 2 * HC * 128], BF16, off=None)
        off_x = sb.mark()
        s_x = sb.alloc("f_x", [128, 2 * E], F32)
        off_h = sb.mark()
        s_h = sb.alloc("f_h", [128, 2 * E], BF16)
        s_sq = sb.alloc("f_sq", [128, E], BF16)
        s_stat = sb.alloc("f_stat", [128, 16], F32)
        s_gt = sb.alloc("f_gt", [128, 2 * 512], BF16)
        s_xr = sb.alloc("f_xr", [128, E], F32, off=off_x)
        s_xo = sb.alloc("f_xo", [128, E], F32, off=off_h)
        NSL = HC
        wg_r = self.wg.rearrange("(c p) n -> p c n", p=128)
        wu_r = self.wu.rearrange("(c p) n -> p c n", p=128)
        wd_r = self.wd.rearrange("(c p) n -> p c n", p=128)

        def load_gu_slab(si, war):
            toks = []
            for j, (src_r, bufo) in enumerate([(wg_r, 0), (wu_r, 2)]):
                buf = (bufo + si % 2) * EC * 128
                dst = bass.AP(s_wgu, buf, [[4 * EC * 128, 128], [128, EC], [1, 128]])
                tl_ = p.dma("pool", dst, src_r[:, :, si * 128:(si + 1) * 128],
                            waits=war)
                tk = p.do("dve", lambda e, buf=buf: e.tensor_scalar(
                    out=s_wgu[:, buf:buf + EC * 128], in0=s_wgu[:, buf:buf + EC * 128],
                    scalar1=-128.0, scalar2=None, op0=ALU.add), waits=[tl_])
                toks.append(tk)
            return toks

        def load_wd_blk(ec, war):
            buf = (ec % 2) * HC * 128
            dst = bass.AP(s_wd, buf, [[2 * HC * 128, 128], [128, HC], [1, 128]])
            for cq in range(2):
                tl_ = p.dma("pool",
                            bass.AP(s_wd, buf + cq * 22 * 128,
                                    [[2 * HC * 128, 128], [128, 22], [1, 128]]),
                            wd_r[:, cq * 22:(cq + 1) * 22, ec * 128:(ec + 1) * 128],
                            waits=war)
            tk = p.do("dve", lambda e, buf=buf: e.tensor_scalar(
                out=s_wd[:, buf:buf + HC * 128], in0=s_wd[:, buf:buf + HC * 128],
                scalar1=-128.0, scalar2=None, op0=ALU.add), waits=[tl_])
            return tk

        ln_h_tok = {}
        ln_tr_tok = {}
        for hf in range(2):
            for tl in range(8):
                tc = hf * 8 + tl
                par = tc % 2
                sc = s_stat[:, par * 8: par * 8 + 8]
                xb = s_x[:, par * E:(par + 1) * E]
                hb = s_h[:, par * E:(par + 1) * E]
                tx = p.dma("sp", xb, x_src[tc * 128:(tc + 1) * 128, :],
                           waits=[ln_h_tok.get(par, ("pe", 0))])
                tb = self.ln_stats(xb, sc[:, 0:1], sc[:, 1:2],
                                   sc[:, 2:3], sc[:, 3:4], sc[:, 4:5],
                                   tx, sq_scr=s_sq[:])
                th = p.do("dve", lambda e, xb=xb, hb=hb, sc=sc: e.tensor_scalar(
                    out=hb, in0=xb, scalar1=sc[:, 2:3],
                    scalar2=sc[:, 3:4], op0=ALU.mult, op1=ALU.add),
                    waits=[tb, ln_tr_tok.get(par, ("pe", 0))])
                ln_h_tok[par] = th
                ln_tr_tok[par] = self.transpose_tiles(
                    lambda i, hb=hb: hb[:, i * 128:(i + 1) * 128],
                    lambda i0, cnt, tl=tl: bass.AP(s_hfm, i0 * 1024 + tl * 128,
                                                   [[EC * 1024, 128], [1024, cnt], [1, 128]]),
                    EC, th, batch=8)
            thalf = p.tok("dve")
            wtoks = {0: load_gu_slab(0, [p.tok("pe")])}
            last_ev = {}
            slab_war = {}
            for hc in range(HC):
                si = hc
                if si + 1 < NSL and (si + 1) not in wtoks:
                    wtoks[si + 1] = load_gu_slab(
                        si + 1, [slab_war.get((si + 1) % 2, ("pe", 0))])
                gbuf = (si % 2) * EC * 128
                ubuf = (2 + si % 2) * EC * 128
                for seg in range(2):
                    w0 = [thalf] + wtoks[si] + [last_ev.get(("g", seg), ("pe", 0)),
                                                last_ev.get(("u", seg), ("pe", 0))]
                    for ec in range(EC):
                        p.do("pe", lambda e, seg=seg, ec=ec, gbuf=gbuf: e.matmul(
                            self.PSF[:, seg * 512:(seg + 1) * 512],
                            s_wgu[:, gbuf + ec * 128: gbuf + ec * 128 + 128],
                            s_hfm[:, ec * 1024 + seg * 512: ec * 1024 + (seg + 1) * 512],
                            start=(ec == 0), stop=(ec == EC - 1)),
                            waits=(w0 if ec == 0 else []), inc=False)
                        p.do("pe", lambda e, seg=seg, ec=ec, ubuf=ubuf: e.matmul(
                            self.PSF[:, 1024 + seg * 512: 1024 + (seg + 1) * 512],
                            s_wgu[:, ubuf + ec * 128: ubuf + ec * 128 + 128],
                            s_hfm[:, ec * 1024 + seg * 512: ec * 1024 + (seg + 1) * 512],
                            start=(ec == 0), stop=(ec == EC - 1)),
                            inc=(ec == EC - 1))
                    tmm = p.tok("pe")
                    slab_war[si % 2] = tmm
                    tg = p.do("act", lambda e, hc=hc, seg=seg: e.activation(
                        out=s_gt[:, seg * 512:(seg + 1) * 512],
                        in_=self.PSF[:, seg * 512:(seg + 1) * 512],
                        func=AF.Silu, scale=self.c_vec[:, hc:hc + 1]), waits=[tmm])
                    last_ev[("g", seg)] = tg
                    tu = p.do("dve", lambda e, hc=hc, seg=seg: e.scalar_tensor_tensor(
                        out=s_act[:, hc * 1024 + seg * 512: hc * 1024 + (seg + 1) * 512],
                        in0=self.PSF[:, 1024 + seg * 512: 1024 + (seg + 1) * 512],
                        scalar=self.c_vec[:, 44 + hc:45 + hc],
                        in1=s_gt[:, seg * 512:(seg + 1) * 512],
                        op0=ALU.mult, op1=ALU.mult), waits=[tg])
                    last_ev[("u", seg)] = tu
            tact_done = [p.tok("dve"), p.tok("act")]
            dtok = {0: load_wd_blk(0, [p.tok("pe")])}
            last_dev = {}
            for ec in range(EC):
                if ec + 1 < EC:
                    dtok[ec + 1] = load_wd_blk(ec + 1, [last_dev.get((ec + 1) % 2, ("pe", 0))])
                wbuf = (ec % 2) * HC * 128
                for seg in range(2):
                    w0 = tact_done + [dtok[ec], last_dev.get(("e", seg), ("pe", 0))]
                    for c in range(HC):
                        p.do("pe", lambda e, seg=seg, c=c, wbuf=wbuf: e.matmul(
                            self.PSF[:, seg * 512:(seg + 1) * 512],
                            s_wd[:, wbuf + c * 128: wbuf + (c + 1) * 128],
                            s_act[:, c * 1024 + seg * 512: c * 1024 + (seg + 1) * 512],
                            start=(c == 0), stop=(c == HC - 1)),
                            waits=(w0 if c == 0 else []), inc=(c == HC - 1))
                    tmm = p.tok("pe")
                    last_dev[(ec % 2)] = tmm
                    td = p.do("act", lambda e, ec=ec, seg=seg: e.activation(
                        out=s_dfm[:, ec * 1024 + seg * 512: ec * 1024 + (seg + 1) * 512],
                        in_=self.PSF[:, seg * 512:(seg + 1) * 512],
                        func=AF.Copy, scale=self.c_vec[:, 88 + ec:89 + ec]), waits=[tmm])
                    last_dev[("e", seg)] = td
            tdown = p.tok("act")
            last_store = ("pe", 0)
            for tl in range(8):
                tc = hf * 8 + tl
                txr = p.dma("sp", s_xr[:], x_src[tc * 128:(tc + 1) * 128, :],
                            waits=[p.tok("dve")])
                for ecg in range(2):
                    half = self._tr_war.get("half", 0) % 2
                    self._tr_war["half"] = half + 1
                    psb = self.PBF[:, half * 1024: half * 1024 + 1024]
                    w0 = [tdown] + ([self._tr_war[half]] if half in self._tr_war else [])
                    for k in range(8):
                        p.do("pe", lambda e, k=k, tl=tl, ecg=ecg, psb=psb: e.transpose(
                            psb[:, k * 128:(k + 1) * 128],
                            s_dfm[:, (ecg * 8 + k) * 1024 + tl * 128:
                                  (ecg * 8 + k) * 1024 + (tl + 1) * 128],
                            self.c_ident[:]), waits=(w0 if k == 0 else []),
                            inc=(k == 7))
                    tev = p.do("dve", lambda e, ecg=ecg, psb=psb: e.tensor_tensor(
                        out=s_xo[:, ecg * 1024:(ecg + 1) * 1024],
                        in0=psb[:, 0:1024],
                        in1=s_xr[:, ecg * 1024:(ecg + 1) * 1024], op=ALU.add),
                        waits=[p.tok("pe"), txr, last_store])
                    self._tr_war[half] = tev
                tevd = p.tok("dve")
                last_store = p.dma("sp", x_dst[tc * 128:(tc + 1) * 128, :], s_xo[:],
                                   waits=[tevd])
                if tap and tap in self.tapd:
                    p.dma("sp", self.tapd[tap][tc * 128:(tc + 1) * 128, :], s_xo[:],
                          waits=[tevd])
        sb.reset(base)
        p.barrier()

    def phase4a(self):
        sb, p = self.sb, self.p
        self.s_bc = sb.alloc("m_bc", [128, T], BF16)
        self.s_cfm = sb.alloc("m_cfm", [64, T], BF16)
        self.s_btm = sb.alloc("m_btm", [128, NCH * 64], BF16)
        self.s_stm = sb.alloc("m_stm", [128, NCH * 64], F32)
        self.s_estm = sb.alloc("m_estm", [128, NCH * 64], F32)
        self.s_nstm = sb.alloc("m_nstm", [128, NCH * 64], F32)
        self.s_dttm = sb.alloc("m_dttm", [128, NCH * 64], F32)
        self.s_wtm = sb.alloc("m_wtm", [128, NCH * 64], F32)
        self.s_state = sb.alloc("m_state", [64, NH * 64], F32)
        self.s_stateb = sb.alloc("m_stateb", [64, NH * 64], BF16)
        self.off_dt = sb.mark()
        self.s_dt = sb.alloc("m_dt", [64, T], F32)
        self.off_ss = sb.mark()
        self.s_s = sb.alloc("m_s", [64, T], F32)
        self.s_z64 = sb.alloc("m_z64", [64, 128], F32)
        self.mpers_end = sb.mark()
        base = sb.mark()
        s_ufm = sb.alloc("m_ufm", [128, EC * T], BF16)
        s_w = sb.alloc("m_w", [128, 2 * EC * 512], BF16)
        off_mx = sb.mark()
        s_x = sb.alloc("m_x", [128, 2 * E], F32)
        off_u = sb.mark()
        s_u = sb.alloc("m_u", [128, 2 * E], BF16)
        off_sq = sb.mark()
        s_sq = sb.alloc("m_sq", [128, E], BF16)
        s_stat = sb.alloc("m_stat", [128, 32], F32)
        s_zev = sb.alloc("m_zev", [128, 2 * 512], BF16, off=self.off_ss)
        s_rmsb = sb.alloc("m_rmsb", [128, E], F32, off=self.off_dt)
        s_xbc = sb.alloc("m_xbc", [128, 3 + T + 1], BF16, off=off_mx)
        off_cv = off_mx + 4128
        s_cv = sb.alloc("m_cv", [128, T], F32, off=off_cv)
        s_cvb = sb.alloc("m_cvb", [128, T], BF16, off=off_u + 2048)
        s_xtev = sb.alloc("m_xtev", [128, 2 * 1024], BF16, off=off_sq)
        s_dtmp = sb.alloc("m_dtmp", [64, T], F32, off=off_cv)
        p.do("pool", lambda e: e.memset(self.s_z64[:], 0.0))
        trb = p.dma("pool", s_rmsb[:], self.rmsb[:, :])
        inp_r = self.inpT.rearrange("(c p) n -> p c n", p=128)

        def load_slab(sl, war):
            cols = 512 if sl < 16 else 192
            buf = (sl % 2) * EC * 512
            tl_ = None
            for cq in range(2):
                tl_ = p.dma("pool",
                            bass.AP(s_w, buf + cq * 8 * 512,
                                    [[2 * EC * 512, 128], [512, 8], [1, cols]]),
                            inp_r[:, cq * 8:(cq + 1) * 8, sl * 512: sl * 512 + cols],
                            waits=war)
            return tl_

        slab_war = {}
        wtoks = {0: load_slab(0, []), 1: load_slab(1, [])}
        zr = self.Zd.rearrange("(c p) n -> p c n", p=128)
        last_zev = {}
        cc_done = {}

        x_war = {}
        u_war = {}
        for tc in range(NTC):
            par = tc % 2
            st = s_stat[:, par * 16: par * 16 + 16]
            xb = s_x[:, par * E:(par + 1) * E]
            ub = s_u[:, par * E:(par + 1) * E]
            tx = p.dma("sp", xb, self.X2[tc * 128:(tc + 1) * 128, :],
                       waits=[x_war.get(par, ("pe", 0))])
            tb = self.ln_stats(xb, st[:, 0:1], st[:, 1:2], st[:, 2:3],
                               self.c_mr2[:, tc:tc + 1], st[:, 4:5], tx,
                               sq_scr=s_sq[:])
            t1 = p.do("dve", lambda e, tc=tc, st=st: e.tensor_scalar(
                out=self.c_r2p1[:, tc:tc + 1], in0=st[:, 2:3], scalar1=1.0,
                scalar2=None, op0=ALU.add), waits=[tb])
            t2 = p.do("dve", lambda e, st=st: e.tensor_tensor(
                out=st[:, 5:6], in0=st[:, 2:3], in1=st[:, 2:3],
                op=ALU.mult), waits=[tb])
            t3 = p.do("dve", lambda e, st=st: e.tensor_tensor(
                out=st[:, 5:6], in0=st[:, 5:6], in1=st[:, 1:2],
                op=ALU.mult), waits=[t2])
            t4 = p.do("act", lambda e, st=st: e.activation(
                out=st[:, 6:7], in_=st[:, 5:6], func=AF.Sqrt,
                bias=RMSEPS_B, scale=1.0), waits=[t3])
            t5 = p.do("dve", lambda e, st=st: e.reciprocal(
                out=st[:, 7:8], in_=st[:, 6:7]), waits=[t4])
            t6 = p.do("dve", lambda e, st=st: e.tensor_tensor(
                out=st[:, 8:9], in0=st[:, 2:3], in1=st[:, 7:8],
                op=ALU.mult), waits=[t5])
            t7 = p.do("dve", lambda e, st=st: e.scalar_tensor_tensor(
                out=st[:, 9:10], in0=st[:, 0:1], scalar=-1.0,
                in1=st[:, 8:9], op0=ALU.mult, op1=ALU.mult), waits=[t6])
            tu = p.do("dve", lambda e, xb=xb, ub=ub, st=st: e.tensor_scalar(
                out=ub, in0=xb, scalar1=st[:, 8:9],
                scalar2=st[:, 9:10], op0=ALU.mult, op1=ALU.add),
                waits=[t7, u_war.get(par, ("pe", 0))])
            x_war[par] = tu
            tur = p.do("dve", lambda e, ub=ub: e.tensor_tensor(
                out=ub, in0=ub, in1=s_rmsb[:], op=ALU.mult),
                waits=[tu] + ([trb] if tc == 0 else []))
            u_war[par] = self.transpose_tiles(
                lambda i, ub=ub: ub[:, i * 128:(i + 1) * 128],
                lambda i0, cnt, tc=tc: bass.AP(s_ufm, i0 * T + tc * 128,
                                               [[EC * T, 128], [T, cnt], [1, 128]]),
                EC, tur, batch=8)
            bank = tc % 2
            w0 = [u_war[par], wtoks[0], last_zev.get(bank, ("pe", 0))]
            for ec in range(EC):
                p.do("pe", lambda e, tc=tc, ec=ec, bank=bank: e.matmul(
                    self.PSF[:, bank * 512:(bank + 1) * 512],
                    s_ufm[:, ec * T + tc * 128: ec * T + (tc + 1) * 128],
                    s_w[:, ec * 512:(ec + 1) * 512],
                    start=(ec == 0), stop=(ec == EC - 1)),
                    waits=(w0 if ec == 0 else []), inc=(ec == EC - 1))
            slab_war[0] = p.tok("pe")
            tz = p.do("act", lambda e, bank=bank: e.activation(
                out=s_zev[:, bank * 512:(bank + 1) * 512],
                in_=self.PSF[:, bank * 512:(bank + 1) * 512],
                func=AF.Copy, scale=1.0), waits=[p.tok("pe")])
            last_zev[bank] = p.dma(
                "sp", zr[:, tc:tc + 1, 0:512],
                s_zev[:, bank * 512: bank * 512 + 512].rearrange(
                    "p (a n) -> p a n", a=1), waits=[tz])
        tuf = p.tok("dve")

        p.do("pool", lambda e: e.memset(s_xbc[:, 0:3], 0.0), waits=[tuf])
        for sl in range(1, 17):
            if sl + 1 < 17:
                wtoks[sl + 1] = load_slab(sl + 1, [slab_war.get((sl + 1) % 2, ("pe", 0))])
            buf = (sl % 2) * EC * 512
            if sl < 8:
                for tc in range(NTC):
                    bank = tc % 2
                    w0 = [tuf, wtoks[sl], last_zev.get(bank, ("pe", 0))]
                    for ec in range(EC):
                        p.do("pe", lambda e, tc=tc, ec=ec, bank=bank, buf=buf: e.matmul(
                            self.PSF[:, bank * 512:(bank + 1) * 512],
                            s_ufm[:, ec * T + tc * 128: ec * T + (tc + 1) * 128],
                            s_w[:, buf + ec * 512: buf + (ec + 1) * 512],
                            start=(ec == 0), stop=(ec == EC - 1)),
                            waits=(w0 if ec == 0 else []), inc=(ec == EC - 1))
                    slab_war[sl % 2] = p.tok("pe")
                    tz = p.do("act", lambda e, bank=bank: e.activation(
                        out=s_zev[:, bank * 512:(bank + 1) * 512],
                        in_=self.PSF[:, bank * 512:(bank + 1) * 512],
                        func=AF.Copy, scale=1.0), waits=[p.tok("pe")])
                    last_zev[bank] = p.dma(
                        "sp", zr[:, tc:tc + 1, sl * 512:(sl + 1) * 512],
                        s_zev[:, bank * 512: bank * 512 + 512].rearrange(
                            "p (a n) -> p a n", a=1), waits=[tz])
            else:
                ncc = 4 if sl < 16 else 1
                for ccl in range(ncc):
                    cc = (sl - 8) * 4 + ccl
                    w0 = [tuf, wtoks[sl], cc_done.get("psum", ("pe", 0)),
                          cc_done.get("psum_d", ("pe", 0))]
                    if cc == 0:
                        w0.append(p.tok("act"))
                    for st4 in range(4):
                        for ec in range(EC):
                            p.do("pe", lambda e, ccl=ccl, st4=st4, ec=ec, buf=buf: e.matmul(
                                self.PSF[:, st4 * 512:(st4 + 1) * 512],
                                s_w[:, buf + ec * 512 + ccl * 128: buf + ec * 512 + (ccl + 1) * 128],
                                s_ufm[:, ec * T + st4 * 512: ec * T + (st4 + 1) * 512],
                                start=(ec == 0), stop=(ec == EC - 1)),
                                waits=(w0 if (st4 == 0 and ec == 0) else []),
                                inc=(ec == EC - 1 and st4 == 3))
                    slab_war[sl % 2] = p.tok("pe")
                    tmm = p.tok("pe")
                    tcp1 = p.do("dve", lambda e: e.tensor_copy(
                        s_xbc[:, 3:3 + 1024], self.PSF[:, 0:1024]), waits=[tmm])
                    tcp2 = p.do("act", lambda e: e.activation(
                        out=s_xbc[:, 3 + 1024:3 + 2048], in_=self.PSF[:, 1024:2048],
                        func=AF.Copy, scale=1.0), waits=[tmm])
                    cc_done["psum"] = tcp2
                    cc_done["psum_d"] = tcp1
                    tdv, tpl = [tcp1, tcp2], [tcp1, tcp2]
                    for k in range(4):
                        wcol = self.c_convw[:, cc * 4 + k: cc * 4 + k + 1]
                        if k == 0:
                            tdv = [p.do("dve", lambda e, k=k, wcol=wcol: e.tensor_scalar(
                                out=s_cv[:, 0:1280], in0=s_xbc[:, k:k + 1280],
                                scalar1=wcol, scalar2=None, op0=ALU.mult),
                                waits=tdv + [cc_done.get("cv", ("pe", 0))])]
                            tpl = [p.do("dve", lambda e, k=k, wcol=wcol: e.tensor_scalar(
                                out=s_cv[:, 1280:2048], in0=s_xbc[:, 1280 + k:2048 + k],
                                scalar1=wcol, scalar2=None, op0=ALU.mult),
                                waits=tpl + [cc_done.get("cv", ("pe", 0))])]
                        else:
                            tdv = [p.do("dve", lambda e, k=k, wcol=wcol: e.scalar_tensor_tensor(
                                out=s_cv[:, 0:1280], in0=s_xbc[:, k:k + 1280],
                                scalar=wcol, in1=s_cv[:, 0:1280],
                                op0=ALU.mult, op1=ALU.add), waits=tdv)]
                            tpl = [p.do("dve", lambda e, k=k, wcol=wcol: e.scalar_tensor_tensor(
                                out=s_cv[:, 1280:2048], in0=s_xbc[:, 1280 + k:2048 + k],
                                scalar=wcol, in1=s_cv[:, 1280:2048],
                                op0=ALU.mult, op1=ALU.add), waits=tpl)]
                    tsil = p.do("act", lambda e, cc=cc: e.activation(
                        out=s_cvb[:], in_=s_cv[:], func=AF.Silu,
                        bias=self.c_vec[:, 152 + cc:153 + cc], scale=1.0),
                        waits=tdv + tpl)
                    cc_done["cv"] = tsil
                    if cc < 32:
                        for g in range(2):
                            half = self._tr_war.get("half", 0) % 2
                            self._tr_war["half"] = half + 1
                            psb = self.PBF[:, half * 1024: half * 1024 + 1024]
                            w0t = [tsil] + ([self._tr_war[half]] if half in self._tr_war else [])
                            for k in range(8):
                                p.do("pe", lambda e, g=g, k=k, psb=psb: e.transpose(
                                    psb[:, k * 128:(k + 1) * 128],
                                    s_cvb[:, (g * 8 + k) * 128:(g * 8 + k + 1) * 128],
                                    self.c_ident[:]), waits=(w0t if k == 0 else []),
                                    inc=(k == 7))
                            tev = p.do("dve", lambda e, g=g, psb=psb: e.tensor_copy(
                                s_xtev[:, g * 1024:(g + 1) * 1024], psb[:, 0:1024]),
                                waits=[p.tok("pe"), cc_done.get(("xt", g), ("pe", 0))])
                            self._tr_war[half] = tev
                            cc_done[("xt", g)] = p.dma(
                                "sp",
                                self.XTd.rearrange("(tc p) n -> p tc n", p=128)[
                                    :, g * 8:(g + 1) * 8, cc * 128:(cc + 1) * 128],
                                s_xtev[:, g * 1024:(g + 1) * 1024].rearrange(
                                    "p (a n) -> p a n", a=8), waits=[tev])
                    else:
                        tbc = p.do("dve", lambda e: e.tensor_copy(self.s_bc[:], s_cvb[:]),
                                   waits=[tsil])
                        tcd = p.dma("pool", self.s_cfm[0:64, :], self.s_bc[64:128, :],
                                    waits=[tbc])
                        for c in range(NCH):
                            half = self._tr_war.get("half", 0) % 2
                            self._tr_war["half"] = half + 1
                            psb = self.PBF[:, half * 1024: half * 1024 + 128]
                            w0t = [tbc] + ([self._tr_war[half]] if half in self._tr_war else [])
                            p.do("pe", lambda e, c=c, psb=psb: e.transpose(
                                psb[0:128, 0:64],
                                self.s_bc[0:64, c * 128:(c + 1) * 128],
                                self.c_ident[0:64, 0:64]), waits=w0t)
                            tev = p.do("dve", lambda e, c=c, psb=psb: e.tensor_copy(
                                self.s_btm[:, c * 64:(c + 1) * 64], psb[0:128, 0:64]),
                                waits=[p.tok("pe")])
                            self._tr_war[half] = tev
        buf = (16 % 2) * EC * 512
        for st4 in range(4):
            for ec in range(EC):
                p.do("pe", lambda e, st4=st4, ec=ec, buf=buf: e.matmul(
                    self.PSF[0:64, st4 * 512:(st4 + 1) * 512][:, 0:512],
                    s_w[:, buf + ec * 512 + 128: buf + ec * 512 + 192],
                    s_ufm[:, ec * T + st4 * 512: ec * T + (st4 + 1) * 512],
                    start=(ec == 0), stop=(ec == EC - 1)),
                    waits=([wtoks[16], cc_done.get("psum", ("pe", 0)),
                            p.tok("dve"), p.tok("act"), p.tok("pool")]
                           if (st4 == 0 and ec == 0) else []),
                    inc=(ec == EC - 1 and st4 == 3))
        tdt = p.tok("pe")
        for st4 in range(4):
            p.do("dve", lambda e, st4=st4: e.tensor_copy(
                s_dtmp[0:64, st4 * 512:(st4 + 1) * 512],
                self.PSF[0:64, st4 * 512:(st4 + 1) * 512]),
                waits=[tdt] if st4 == 0 else [])
        tev = p.do("dve", lambda e: e.tensor_scalar(
            out=s_dtmp[0:64, :], in0=s_dtmp[0:64, :],
            scalar1=self.c_small[0:NH, 0:1], scalar2=None, op0=ALU.add),
            waits=[p.tok("dve")])
        tab = p.do("act", lambda e: e.activation(
            out=self.s_dt[0:64, :], in_=s_dtmp[0:64, :], func=AF.Abs, scale=1.0),
            waits=[tev])
        te1 = p.do("act", lambda e: e.activation(
            out=self.s_s[0:64, :], in_=self.s_dt[0:64, :], func=AF.Exp, scale=-1.0),
            waits=[tab])
        te2 = p.do("act", lambda e: e.activation(
            out=self.s_s[0:64, :], in_=self.s_s[0:64, :], func=AF.Ln, bias=1.0,
            scale=1.0), waits=[te1])
        tr1 = p.do("dve", lambda e: e.tensor_scalar(
            out=s_dtmp[0:64, :], in0=s_dtmp[0:64, :], scalar1=0.0, scalar2=None,
            op0=ALU.max), waits=[te1])
        tsp = p.do("dve", lambda e: e.tensor_tensor(
            out=self.s_dt[0:64, :], in0=s_dtmp[0:64, :], in1=self.s_s[0:64, :],
            op=ALU.add), waits=[te2, tr1])
        tL = p.do("dve", lambda e: e.tensor_scalar(
            out=s_dtmp[0:64, :], in0=self.s_dt[0:64, :],
            scalar1=self.c_small[0:NH, 2:3], scalar2=None, op0=ALU.mult),
            waits=[tsp, self.tok_A])
        tsc = tL
        for c in range(NCH):
            tsc = p.do("dve", lambda e, c=c: e.tensor_tensor_scan(
                out=self.s_s[0:64, c * 128:(c + 1) * 128],
                data0=s_dtmp[0:64, c * 128:(c + 1) * 128],
                data1=self.s_z64[0:64, :], initial=0.0,
                op0=ALU.add, op1=ALU.add), waits=[tsc])
        tw = []
        for c in range(NCH):
            t1 = p.do("act", lambda e, c=c: e.activation(
                out=s_dtmp[0:64, c * 128:(c + 1) * 128],
                in_=self.s_s[0:64, c * 128:(c + 1) * 128], func=AF.Exp, scale=-1.0,
                bias=self.s_s[0:64, c * 128 + 127: c * 128 + 128]), waits=[tsc])
            tw.append(t1)
        tw2 = p.do("dve", lambda e: e.tensor_tensor(
            out=s_dtmp[0:64, :], in0=s_dtmp[0:64, :], in1=self.s_dt[0:64, :],
            op=ALU.mult), waits=tw)
        for (srcbuf, dstbuf, wtok) in [(self.s_s, self.s_stm, tsc),
                                       (self.s_dt, self.s_dttm, tsp),
                                       (s_dtmp, self.s_wtm, tw2)]:
            for c in range(NCH):
                tp = p.do("pe", lambda e, c=c, srcbuf=srcbuf: e.transpose(
                    self.PSF[0:128, 2048:2048 + 64],
                    srcbuf[0:64, c * 128:(c + 1) * 128], self.c_identf[0:64, 0:64]),
                    waits=[wtok, p.tok("dve")])
                p.do("dve", lambda e, c=c, dstbuf=dstbuf: e.tensor_copy(
                    dstbuf[:, c * 64:(c + 1) * 64], self.PSF[0:128, 2048:2048 + 64]),
                    waits=[tp])
        p.do("act", lambda e: e.activation(
            out=self.s_estm[:], in_=self.s_stm[:], func=AF.Exp, scale=1.0),
            waits=[p.tok("dve")])
        p.do("dve", lambda e: e.tensor_scalar(
            out=self.s_nstm[:], in0=self.s_stm[:], scalar1=-1.0, scalar2=None,
            op0=ALU.mult), waits=[p.tok("dve")])
        sb.reset(base)
        p.barrier()

    def phase4b(self):
        sb, p = self.sb, self.p
        base = sb.mark()
        s_xt = sb.alloc("b_xt", [128, 2 * DI], BF16)
        s_srow = sb.alloc("b_srow", [1, 2 * NH * 128], F32)
        s_y = sb.alloc("b_y", [128, DI], BF16)
        s_zc = sb.alloc("b_zc", [128, DI], BF16)
        s_sig = sb.alloc("b_sig", [128, DI // 2], F32)
        s_mt = sb.alloc("b_mt", [128, 2 * 512], BF16)
        s_ed = sb.alloc("b_ed", [128, 2 * 512], BF16)
        s_yb = sb.alloc("b_yb", [128, 256], F32)
        s_t1 = sb.alloc("b_t1", [128, 256], F32)
        s_t2 = sb.alloc("b_t2", [128, 256], F32)
        s_st = sb.alloc("b_st", [64, 256], F32)
        s_esq = sb.alloc("b_esq", [64, 4], F32)
        s_esqr = sb.alloc("b_esqr", [1, 64], F32)
        s_decay = sb.alloc("b_decay", [64, 64], F32)
        ps_G = self.PSF[:, 2048:2176]
        ps_esq = self.PSF[:, 2560:2624]
        xt_r = self.XTd.rearrange("(tc p) n -> p tc n", p=128)

        tz = p.do("pool", lambda e: e.memset(self.s_state[:], 0.0))
        tz = p.do("pool", lambda e: e.memset(self.s_stateb[:], 0.0), waits=[tz])
        yr = self.Yd.rearrange("(tc p) n -> p tc n", p=128)
        zr4 = self.Zd.rearrange("(tc p) n -> p tc n", p=128)
        zc_tok = {0: p.dma("pool", s_zc[:], zr4[:, 0:1, :].rearrange("p a n -> p (a n)"))}
        xt_tok = {0: p.dma("sp", s_xt[:, 0:DI], xt_r[:, 0:1, :].rearrange("p a n -> p (a n)"))}
        last_y_dma = ("pe", 0)
        ywar = {"t": ("pe", 0)}
        st_evac = [tz]
        xt_mm = {}
        srow_mm = {}
        for c in range(NCH):
            if c + 1 < NCH:
                xt_tok[c + 1] = p.dma(
                    "sp", s_xt[:, ((c + 1) % 2) * DI: ((c + 1) % 2 + 1) * DI],
                    xt_r[:, c + 1:c + 2, :].rearrange("p a n -> p (a n)"),
                    waits=[("pe", 0) if c == 0 else xt_mm.get((c + 1) % 2, ("pe", 0))])
            xbuf = (c % 2) * DI
            srb = (c % 2) * NH * 128
            if c == 0:
                srow_tok = {0: p.dma("pool", s_srow[0:1, 0:NH * 128],
                                     self.s_s[0:64, 0:128], waits=[p.tok("pe")])}
            if c + 1 < NCH:
                srow_tok[c + 1] = p.dma(
                    "pool", s_srow[0:1, ((c + 1) % 2) * NH * 128: ((c + 1) % 2 + 1) * NH * 128],
                    self.s_s[0:64, (c + 1) * 128:(c + 2) * 128],
                    waits=[srow_mm.get((c + 1) % 2, ("pe", 0))])
            tst = srow_tok[c]
            te0 = p.do("act", lambda e, c=c: e.activation(
                out=s_esq[0:64, 0:1], in_=self.s_s[0:64, c * 128 + 127: c * 128 + 128],
                func=AF.Exp, scale=1.0), waits=[p.tok("dve")])
            tt0 = p.do("pe", lambda e: e.transpose(
                ps_esq[0:1, 0:64], s_esq[0:64, 0:1], self.c_identf[0:64, 0:64]),
                waits=[te0, st_evac[-1]])
            tt1 = p.do("dve", lambda e: e.tensor_copy(s_esqr[0:1, 0:64],
                                                      ps_esq[0:1, 0:64]), waits=[tt0])
            tt2 = p.do("pe", lambda e: e.matmul(
                self.PSF[0:64, 2560:2624], self.c_ones[0:1, 0:64], s_esqr[0:1, 0:64],
                start=True, stop=True), waits=[tt1])
            tdec = p.do("dve", lambda e: e.tensor_copy(s_decay[:], self.PSF[0:64, 2560:2624]),
                        waits=[tt2])
            tG = p.do("pe", lambda e, c=c: e.matmul(
                ps_G, self.s_bc[0:64, c * 128:(c + 1) * 128],
                self.s_cfm[0:64, c * 128:(c + 1) * 128], start=True, stop=True),
                waits=[p.tok("dve")])
            pend = None
            bank_tok = {}
            def stage_b(hg, par, tm2):
                for hh in range(4):
                    h = hg * 4 + hh
                    w_in = [tm2, xt_tok[c]] + [bank_tok.get("y", ("pe", 0))]
                    p.do("pe", lambda e, h=h, hh=hh, par=par, xbuf=xbuf: e.matmul(
                        self.PSF[:, 1024 + hh * 64: 1024 + (hh + 1) * 64],
                        s_mt[:, par * 512 + hh * 128: par * 512 + (hh + 1) * 128],
                        s_xt[:, xbuf + h * 64: xbuf + (h + 1) * 64],
                        start=True, stop=True),
                        waits=(w_in if hh == 0 else []), inc=False)
                    p.do("pe", lambda e, h=h, hh=hh, c=c: e.matmul(
                        self.PSF[:, 1536 + hh * 64: 1536 + (hh + 1) * 64],
                        self.s_cfm[0:64, c * 128:(c + 1) * 128],
                        self.s_stateb[0:64, h * 64:(h + 1) * 64],
                        start=True, stop=True),
                        waits=(st_evac[-1:] if hh == 0 else []), inc=(hh == 3))
                btm_rep = bass.AP(self.s_btm, c * 64,
                                  [[NCH * 64, 128], [0, 4], [1, 64]])
                wtm_b = bass.AP(self.s_wtm, c * 64 + hg * 4,
                                [[NCH * 64, 128], [1, 4], [0, 64]])
                tb2 = p.do("dve", lambda e, btm_rep=btm_rep, wtm_b=wtm_b, par=par: e.tensor_tensor(
                    out=s_ed[:, par * 512:par * 512 + 256], in0=btm_rep, in1=wtm_b,
                    op=ALU.mult), waits=[tm2])
                for hh in range(4):
                    h = hg * 4 + hh
                    p.do("pe", lambda e, hh=hh, h=h, par=par, xbuf=xbuf: e.matmul(
                        self.PSF[0:64, 2624 + hh * 64: 2624 + (hh + 1) * 64],
                        s_ed[:, par * 512 + hh * 64: par * 512 + hh * 64 + 64],
                        s_xt[:, xbuf + h * 64: xbuf + (h + 1) * 64],
                        start=True, stop=True),
                        waits=([tb2, tdec, st_evac[-1]] if hh == 0 else []),
                        inc=(hh == 3))
                tpe_all = p.tok("pe")
                esb = bass.AP(self.s_estm, c * 64 + hg * 4,
                              [[NCH * 64, 128], [1, 4], [0, 64]])
                ty1 = p.do("dve", lambda e, esb=esb: e.tensor_tensor(
                    out=s_yb[:], in0=self.PSF[:, 1536:1792], in1=esb, op=ALU.mult),
                    waits=[tpe_all])
                dbc = bass.AP(self.c_dbc, hg * 4, [[NH, 128], [1, 4], [0, 64]])
                ty2 = p.do("dve", lambda e, dbc=dbc, xbuf=xbuf, hg=hg: e.tensor_tensor(
                    out=s_t1[:], in0=s_xt[:, xbuf + hg * 256: xbuf + (hg + 1) * 256],
                    in1=dbc, op=ALU.mult), waits=[tpe_all])
                ty3 = p.do("dve", lambda e: e.tensor_tensor(
                    out=s_t2[:], in0=s_t1[:], in1=self.PSF[:, 1024:1280], op=ALU.add),
                    waits=[ty2])
                p.do("dve", lambda e, hg=hg: e.tensor_tensor(
                    out=s_y[:, hg * 256:(hg + 1) * 256], in0=s_t2[:], in1=s_yb[:],
                    op=ALU.add),
                    waits=[ty3, ty1] + ([ywar["t"]] if hg == 0 else []))
                bank_tok["y"] = ty3
                dcb = bass.AP(s_decay, hg * 4, [[64, 64], [1, 4], [0, 64]])
                ts1 = p.do("dve", lambda e, dcb=dcb, hg=hg: e.tensor_tensor(
                    out=s_st[0:64, :], in0=self.s_state[0:64, hg * 256:(hg + 1) * 256],
                    in1=dcb, op=ALU.mult), waits=[tdec, tpe_all])
                ts2 = p.do("dve", lambda e, hg=hg: e.tensor_tensor(
                    out=self.s_state[0:64, hg * 256:(hg + 1) * 256], in0=s_st[0:64, :],
                    in1=self.PSF[0:64, 2624:2880], op=ALU.add), waits=[ts1])
                ts3 = p.do("dve", lambda e, hg=hg: e.tensor_copy(
                    self.s_stateb[0:64, hg * 256:(hg + 1) * 256],
                    self.s_state[0:64, hg * 256:(hg + 1) * 256]), waits=[ts2])
                st_evac.append(ts3)

            for hg in range(NH // 4):
                bank = hg % 2
                par = hg % 2
                psD = self.PSF[:, bank * 512:(bank + 1) * 512]
                w0 = [tst, p.tok("act")]
                p.do("pe", lambda e, hg=hg, psD=psD, srb=srb: e.matmul(
                    psD, self.c_ones[0:1, :], s_srow[0:1, srb + hg * 512: srb + (hg + 1) * 512],
                    start=True, stop=False), waits=w0, inc=False)
                for hh in range(4):
                    h = hg * 4 + hh
                    p.do("pe", lambda e, h=h, hh=hh, psD=psD, srb=srb: e.matmul(
                        psD[:, hh * 128:(hh + 1) * 128],
                        s_srow[0:1, srb + h * 128: srb + (h + 1) * 128],
                        self.c_negones[0:1, :], start=False, stop=False), inc=False)
                p.do("pe", lambda e, psD=psD: e.matmul(
                    psD, self.c_ident[:], self.c_mmask[:], start=False, stop=True),
                    inc=True)
                tD = p.tok("pe")
                ted = p.do("act", lambda e, par=par, psD=psD: e.activation(
                    out=s_ed[:, par * 512:(par + 1) * 512], in_=psD, func=AF.Exp),
                    waits=[tD])
                dtb = bass.AP(self.s_dttm, c * 64 + hg * 4,
                              [[NCH * 64, 128], [1, 4], [0, 128]])
                tm1 = p.do("dve", lambda e, dtb=dtb, par=par: e.tensor_tensor(
                    out=s_mt[:, par * 512:(par + 1) * 512],
                    in0=s_ed[:, par * 512:(par + 1) * 512], in1=dtb, op=ALU.mult),
                    waits=[ted, p.tok("pe")])
                gb = bass.AP(self.PSF, 2048, [[3072, 128], [0, 4], [1, 128]])
                tm2 = p.do("dve", lambda e, gb=gb, par=par: e.tensor_tensor(
                    out=s_mt[:, par * 512:(par + 1) * 512],
                    in0=s_mt[:, par * 512:(par + 1) * 512], in1=gb, op=ALU.mult),
                    waits=[tm1, tG])
                if pend is not None:
                    stage_b(*pend)
                pend = (hg, par, tm2)
            stage_b(*pend)
            tyd = p.tok("dve")
            xt_mm[c % 2] = p.tok("pe")
            srow_mm[c % 2] = p.tok("pe")
            tw = p.do("dve", lambda e: e.tensor_tensor(
                out=s_y[:], in0=s_y[:], in1=s_zc[:], op=ALU.mult),
                waits=[tyd, zc_tok[c]])
            tys = tw
            for hv in range(2):
                HW = DI // 2
                te = p.do("act", lambda e, hv=hv, HW=HW: e.activation(
                    out=s_sig[:], in_=s_zc[:, hv * HW:(hv + 1) * HW],
                    func=AF.Exp, scale=-1.0), waits=[tw, tys])
                td = p.do("dve", lambda e: e.tensor_scalar(
                    out=s_sig[:], in0=s_sig[:], scalar1=1.0, scalar2=None,
                    op0=ALU.add), waits=[te])
                trc = p.do("dve", lambda e: e.reciprocal(
                    out=s_sig[:], in_=s_sig[:]), waits=[td])
                tys = p.do("dve", lambda e, hv=hv, HW=HW: e.tensor_tensor(
                    out=s_y[:, hv * HW:(hv + 1) * HW],
                    in0=s_y[:, hv * HW:(hv + 1) * HW], in1=s_sig[:],
                    op=ALU.mult), waits=[trc])
            tsq = p.do("dve", lambda e: e.tensor_tensor(
                out=s_zc[:], in0=s_y[:], in1=s_y[:], op=ALU.mult), waits=[tys])
            tss = p.do("dve", lambda e, c=c: e.reduce_sum(
                self.c_vec[:, 201 + c:202 + c], s_zc[:], axis=AX.X), waits=[tsq])
            if c + 1 < NCH:
                zc_tok[c + 1] = p.dma(
                    "pool", s_zc[:],
                    zr4[:, c + 1:c + 2, :].rearrange("p a n -> p (a n)"),
                    waits=[tss])
            last_y_dma = p.dma("sp", yr[:, c:c + 1, :].rearrange("p a n -> p (a n)"),
                               s_y[:], waits=[tss, last_y_dma])
            ywar["t"] = last_y_dma
        sb.reset(base)
        p.barrier()

    def phase4c(self):
        sb, p = self.sb, self.p
        sb.reset(self.pers_end)
        self.s_opt = sb.alloc("d_opt", [128, 32 * E], BF16)
        opt_r = self.outpT.rearrange("(c p) n -> p c n", p=128)
        ov = self.s_opt[:].rearrange("p (c n) -> p c n", c=32)
        for q in range(4):
            tw = p.dma("pool", ov[:, q * 8:(q + 1) * 8, :], opt_r[:, q * 8:(q + 1) * 8, :])
        twf = tw
        for cfold in range(32):
            twf = p.do("dve", lambda e, cfold=cfold: e.tensor_scalar(
                out=self.s_opt[:, cfold * E:(cfold + 1) * E],
                in0=self.s_opt[:, cfold * E:(cfold + 1) * E],
                scalar1=self.c_vec[:, 120 + cfold:121 + cfold], scalar2=None,
                op0=ALU.mult), waits=([tw] if cfold == 0 else []))
        self.opt_tok = twf
        base = sb.mark()
        s_opt = self.s_opt
        s_y = sb.alloc("c_y", [128, DI], BF16)
        s_ys = sb.alloc("c_ys", [128, 2 * DI], BF16)
        s_rst = sb.alloc("c_rst", [128, NTC], F32)
        s_ynt = sb.alloc("d_ynt", [128, 2 * 32 * 128], BF16)
        s_x2 = sb.alloc("d_x2", [128, E], F32)
        s_x3 = sb.alloc("d_x3", [128, E], F32)
        yr = self.Yd.rearrange("(tc p) n -> p tc n", p=128)
        tq = p.do("act", lambda e: e.activation(
            out=s_rst[:], in_=self.c_vec[:, 201:201 + NTC], func=AF.Sqrt,
            bias=RMSEPS_B, scale=1.0 / DI), waits=[p.tok("act")])
        tr = p.do("dve", lambda e: e.reciprocal(out=s_rst[:], in_=s_rst[:]), waits=[tq])
        last_ev = {}
        yn_mm = {}
        ys_war = {}
        y_war = ("pe", 0)
        for tc in range(NTC):
            par = tc % 2
            buf = par * 32 * 128
            ysb = s_ys[:, par * DI:(par + 1) * DI]
            ty = p.dma("sp", s_y[:], yr[:, tc:tc + 1, :].rearrange("p a n -> p (a n)"),
                       waits=[y_war])
            tyn = p.do("dve", lambda e, tc=tc, ysb=ysb: e.tensor_scalar(
                out=ysb, in0=s_y[:], scalar1=s_rst[:, tc:tc + 1], scalar2=None,
                op0=ALU.mult), waits=[ty, tr, ys_war.get(par, ("pe", 0))])
            y_war = tyn
            for g in range(4):
                half = self._tr_war.get("half", 0) % 2
                self._tr_war["half"] = half + 1
                psb = self.PBF[:, half * 1024: half * 1024 + 1024]
                w0 = [tyn] + ([self._tr_war[half]] if half in self._tr_war else [])
                for k in range(8):
                    p.do("pe", lambda e, g=g, k=k, psb=psb, ysb=ysb: e.transpose(
                        psb[:, k * 128:(k + 1) * 128],
                        ysb[:, (g * 8 + k) * 128:(g * 8 + k + 1) * 128],
                        self.c_ident[:]), waits=(w0 if k == 0 else []), inc=(k == 7))
                tev = p.do("dve", lambda e, g=g, psb=psb, buf=buf: e.tensor_copy(
                    s_ynt[:, buf + g * 1024: buf + (g + 1) * 1024], psb[:, 0:1024]),
                    waits=[p.tok("pe"), yn_mm.get(par, ("pe", 0))])
                self._tr_war[half] = tev
            tyn_done = p.tok("dve")
            ys_war[par] = tyn_done
            tx = p.dma("sp", s_x2[:], self.X2[tc * 128:(tc + 1) * 128, :],
                       waits=[p.tok("dve")])
            for eseg in range(4):
                bank = eseg % 2
                w0 = [tyn_done, twf, tx, last_ev.get(bank, ("pe", 0))]
                for xc in range(32):
                    p.do("pe", lambda e, xc=xc, eseg=eseg, bank=bank, buf=buf: e.matmul(
                        self.PSF[:, bank * 512:(bank + 1) * 512],
                        s_ynt[:, buf + xc * 128: buf + (xc + 1) * 128],
                        s_opt[:, xc * E + eseg * 512: xc * E + (eseg + 1) * 512],
                        start=(xc == 0), stop=(xc == 31)),
                        waits=(w0 if xc == 0 else []), inc=(xc == 31))
                tmm = p.tok("pe")
                yn_mm[par] = tmm
                t1 = p.do("dve", lambda e, eseg=eseg, bank=bank, tc=tc: e.scalar_tensor_tensor(
                    out=s_x3[:, eseg * 512:(eseg + 1) * 512],
                    in0=s_x2[:, eseg * 512:(eseg + 1) * 512],
                    scalar=self.c_r2p1[:, tc:tc + 1],
                    in1=self.PSF[:, bank * 512:(bank + 1) * 512],
                    op0=ALU.mult, op1=ALU.add), waits=[tmm])
                t2 = p.do("dve", lambda e, eseg=eseg, tc=tc: e.tensor_scalar(
                    out=s_x3[:, eseg * 512:(eseg + 1) * 512],
                    in0=s_x3[:, eseg * 512:(eseg + 1) * 512],
                    scalar1=self.c_mr2[:, tc:tc + 1], scalar2=None, op0=ALU.add),
                    waits=[t1])
                last_ev[bank] = t2
            tevd = p.tok("dve")
            p.dma("sp", self.X3[tc * 128:(tc + 1) * 128, :], s_x3[:], waits=[tevd])
            if "tap_x3" in self.tapd:
                p.dma("sp", self.tapd["tap_x3"][tc * 128:(tc + 1) * 128, :], s_x3[:],
                      waits=[tevd])
        sb.reset(self.pers_end)
        p.barrier()

    def tap_copy(self, name, src_ap):
        if name in self.tapd:
            t = p_split_dma(self.p, "sp", self.tapd[name][:, :], src_ap)
            return t
        return None

    def finish(self):
        self.p.barrier()


def p_split_dma(p, q, out_ap, in_ap, parts=2):
    n = out_ap.shape[0]
    tok = None
    step = (n + parts - 1) // parts
    for i in range(0, n, step):
        j = min(n, i + step)
        tok = p.dma(q, out_ap[i:j], in_ap[i:j])
    return tok


def build_mk(taps=(), phases=("p1a", "p1b", "p2", "p2b", "p3", "p4", "p5")):
    mk = MK(taps=taps)
    if "p1a" in phases:
        mk.phase1a()
    if "p1b" in phases:
        mk.phase1b()
    if "p2" in phases:
        mk.phase2()
    if "p2b" in phases:
        mk.phase2b()
    if "p3" in phases:
        mk.phase_ffn(mk.X1, mk.X2, tap="tap_x2")
    if "p4" in phases:
        mk.phase4a()
        mk.tap_copy("tap_z", mk.Zd)
        mk.tap_copy("tap_xt", mk.XTd)
        if "tap_dt" in mk.tapd:
            mk.p.dma("sp", mk.tapd["tap_dt"][0:NH, :], mk.s_dt[0:64, :])
        mk.phase4b()
        mk.tap_copy("tap_y", mk.Yd)
        mk.phase4c()
    if "p5" in phases:
        mk.phase_ffn(mk.X3, mk.out, tap=None)
    mk.tap_copy("tap_qn", mk.QN)
    mk.tap_copy("tap_qp", mk.QP)
    mk.tap_copy("tap_v", mk.Vd)
    if "tap_kn" in mk.tapd:
        mk.p.dma("sp", mk.tapd["tap_kn"].rearrange("(c p) t -> p c t", p=128),
                 mk.s_kn[:].rearrange("p (c t) -> p c t", c=H))
    if "tap_kp" in mk.tapd:
        mk.p.dma("sp", mk.tapd["tap_kp"][:, :], mk.s_kp[0:64, :])
    mk.finish()
    mk.p.emit()
    return mk


def kernel(**inputs):
    shared = prep_inputs(inputs)
    x = np.asarray(inputs["L0_attn_a_x"], np.float32)
    mk = build_mk()
    in_maps = []
    for b in range(8):
        m = dict(shared)
        m["x"] = np.ascontiguousarray(x[b])
        in_maps.append(m)
    res = run_bass_kernel_spmd(mk.nc, in_maps, core_ids=list(range(8)))
    return np.stack([r["out"] for r in res.results], axis=0)



# version 20
# speedup vs baseline: 1.0287x; 1.0287x over previous
import math
import numpy as np
import ml_dtypes

import concourse.bass as bass
import concourse.mybir as mybir
from concourse.bass_utils import run_bass_kernel_spmd

F32 = mybir.dt.float32
BF16 = mybir.dt.bfloat16
I32 = mybir.dt.int32
AF = mybir.ActivationFunctionType
ALU = mybir.AluOpType
AX = mybir.AxisListType
bfdt = ml_dtypes.bfloat16

T = 2048
E = 2048
EC = E // 128
NSTRIP = T // 512
NTC = T // 128
H = 16
QL, KL = 1536, 512
DN, DR, DV = 128, 64, 128
HID = 5632
HC = HID // 128
DI = 4096
DST = 64
NH = 64
HD = 64
CONV = 4224
NCH = 16
LNEPS = 1e-5
RMSEPS_A = 1e-6
RMSEPS_B = 1e-5
NEG = -1e30
SCALE_ATTN = 1.0 / math.sqrt(DN + DR)
FCONSTS = [1e-5, 1e-6, 1.0, 0.0, -1.0, 0.5, 2.0, -0.5]


class Prog:
    ENGS = ("pe", "act", "dve", "pool", "sp")
    QSUBS = 6

    def __init__(self, nc):
        self.nc = nc
        self.ops = []
        self.cnt = {e: 0 for e in self.ENGS}
        self.qidx = {}
        for q in ("sp", "pool", "act"):
            self.qidx[q] = 0
            for i in range(self.QSUBS):
                self.cnt[f"d_{q}_{i}"] = 0

    def _fix(self, waits):
        return tuple((s, self.cnt[s]) if s.startswith("d_") else (s, v)
                     for (s, v) in waits)

    def do(self, eng, fn, waits=(), inc=True):
        incs = []
        if inc:
            self.cnt[eng] += 1
            incs.append((eng, 1))
        self.ops.append((eng, fn, self._fix(waits), tuple(incs)))
        return (eng, self.cnt[eng])

    def wait(self, eng, waits):
        if waits:
            self.ops.append((eng, None, self._fix(waits), ()))

    def dma(self, q, out, in_, waits=(), **kw):
        fixed = self._fix(waits)
        sem = f"d_{q}_{self.qidx[q] % self.QSUBS}"
        self.qidx[q] += 1
        self.cnt[sem] += 16
        def fn(e, out=out, in_=in_, kw=kw):
            return e.dma_start(out=out, in_=in_, **kw)
        self.ops.append((q if q != "sp" else "sp", fn, fixed, ((sem, 16),)))
        return (sem, self.cnt[sem])

    def tok(self, key):
        return (key, self.cnt[key])

    def barrier(self):
        snap = tuple((s, v) for s, v in self.cnt.items() if v > 0)
        for e in self.ENGS:
            self.ops.append((e, None, snap, ()))

    def emit(self):
        nc = self.nc
        sems = {name: nc.alloc_semaphore(name) for name in self.cnt}
        attr = {"pe": "tensor", "act": "scalar", "dve": "vector",
                "pool": "gpsimd", "sp": "sync"}
        with nc.Block() as block:
            for key in self.ENGS:
                def make(key):
                    def run(eng):
                        seen = {}
                        for (e, fn, waits, incs) in self.ops:
                            if e != key:
                                continue
                            for (s, v) in waits:
                                if seen.get(s, 0) < v:
                                    eng.wait_ge(sems[s], v)
                                    seen[s] = v
                            if fn is None:
                                continue
                            inst = fn(eng)
                            for (s, n) in incs:
                                inst.then_inc(sems[s], n)
                    return run
                getattr(block, attr[key])(make(key))
        return sems


class SB:
    BASE = 16512
    TOP = 229344

    def __init__(self, nc):
        self.nc = nc
        self.off = self.BASE

    def alloc(self, name, shape, dtype, off=None):
        sz = int(np.prod(shape[1:])) * mybir.dt.size(dtype)
        if off is None:
            off = self.off
            self.off = off + ((sz + 31) // 32) * 32
        assert off + sz <= self.TOP, f"SBUF overflow {name}: {off}+{sz}"
        return self.nc.alloc_sbuf_tensor_at(name, shape, dtype, offset=off)

    def mark(self):
        return self.off

    def reset(self, off):
        self.off = off


def _rope_tables():
    inv_freq = 1.0 / 10000.0 ** (np.arange(0, DR, 2, dtype=np.float32) / DR)
    t = np.arange(T, dtype=np.float32)
    freqs = np.outer(t, inv_freq)
    return np.cos(freqs).astype(np.float32), np.sin(freqs).astype(np.float32)


def _pe_perm():
    p = np.empty(64, dtype=np.int64)
    p[:32] = 2 * np.arange(32)
    p[32:] = 2 * np.arange(32) + 1
    return p


def prep_inputs(kw):
    f32 = np.float32
    pe = _pe_perm()
    d = {}
    d["wqa"] = np.ascontiguousarray(kw["L0_attn_a_w_q_a"], dtype=f32)
    wqb = np.asarray(kw["L0_attn_a_w_q_b"], dtype=f32)
    cols = []
    for h in range(H):
        cols.append(np.arange(h * 192, h * 192 + 128))
    for h in range(H):
        cols.append(h * 192 + 128 + pe)
    d["wqb"] = np.ascontiguousarray(wqb[:, np.concatenate(cols)])
    wkva = np.asarray(kw["L0_attn_a_w_kv_a"], dtype=f32)
    d["wkva"] = np.ascontiguousarray(
        np.concatenate([wkva[:, :KL], wkva[:, KL:][:, pe]], axis=1))
    wkvb = np.asarray(kw["L0_attn_a_w_kv_b"], dtype=f32)
    cols = [np.arange(h * 256, h * 256 + 128) for h in range(H)]
    cols += [np.arange(h * 256 + 128, h * 256 + 256) for h in range(H)]
    d["wkvb"] = np.ascontiguousarray(wkvb[:, np.concatenate(cols)])
    d["wout"] = np.ascontiguousarray(kw["L0_attn_a_w_out"], dtype=f32)
    d["qaln"] = np.ascontiguousarray(
        np.asarray(kw["L0_attn_a_q_a_ln"], f32).reshape(QL // 128, 128).T)
    d["kvln"] = np.ascontiguousarray(
        np.asarray(kw["L0_attn_a_kv_a_ln"], f32).reshape(KL // 128, 128).T)
    d["wg"] = np.ascontiguousarray(np.asarray(kw["L0_ffn_wq_gate"], np.int32).T)
    d["wu"] = np.ascontiguousarray(np.asarray(kw["L0_ffn_wq_up"], np.int32).T)
    d["wd"] = np.ascontiguousarray(np.asarray(kw["L0_ffn_wq_down"], np.int32).T)
    d["sg"] = np.ascontiguousarray(np.asarray(kw["L0_ffn_s_gate"], f32).reshape(HC, 128).T)
    d["su"] = np.ascontiguousarray(np.asarray(kw["L0_ffn_s_up"], f32).reshape(HC, 128).T)
    d["sd"] = np.ascontiguousarray(np.asarray(kw["L0_ffn_s_down"], f32).reshape(EC, 128).T)
    d["sdb"] = np.ascontiguousarray(np.broadcast_to(np.asarray(kw["L0_ffn_s_down"], f32), (128, E)))
    d["inpT"] = np.ascontiguousarray(np.asarray(kw["L0_attn_b_in_proj_w"], f32).T)
    d["outpT"] = np.ascontiguousarray(np.asarray(kw["L0_attn_b_out_proj_w"], f32).T)
    d["rmsw"] = np.ascontiguousarray(
        np.asarray(kw["L0_attn_b_rms_weight"], f32).reshape(EC, 128).T)
    d["rmsb"] = np.ascontiguousarray(
        np.broadcast_to(np.asarray(kw["L0_attn_b_rms_weight"], f32), (128, E)))
    d["normw"] = np.ascontiguousarray(
        np.asarray(kw["L0_attn_b_norm_w"], f32).reshape(DI // 128, 128).T)
    d["convw"] = np.ascontiguousarray(
        np.asarray(kw["L0_attn_b_conv_w"], f32).reshape(CONV // 128, 128, 4)
        .transpose(1, 0, 2).reshape(128, -1))
    d["convb"] = np.ascontiguousarray(
        np.asarray(kw["L0_attn_b_conv_b"], f32).reshape(CONV // 128, 128).T)
    d["dtb"] = np.ascontiguousarray(np.asarray(kw["L0_attn_b_dt_bias"], f32).reshape(NH, 1))
    d["alog"] = np.ascontiguousarray(np.asarray(kw["L0_attn_b_A_log"], f32).reshape(NH, 1))
    d["dbc"] = np.ascontiguousarray(
        np.broadcast_to(np.asarray(kw["L0_attn_b_D"], f32), (128, NH)))
    d["ident"] = np.eye(128, dtype=bfdt)
    d["identf"] = np.eye(128, dtype=f32)
    d["onesf"] = np.ones((1, 128), dtype=f32)
    d["negonesf"] = np.full((1, 128), -1.0, dtype=f32)
    d["fconst"] = np.ascontiguousarray(np.broadcast_to(np.array(FCONSTS, dtype=f32), (128, len(FCONSTS))))
    cos, sin = _rope_tables()
    d["costm"] = np.ascontiguousarray(cos.reshape(NTC, 128, 32).transpose(1, 0, 2).reshape(128, -1))
    d["sintm"] = np.ascontiguousarray(sin.reshape(NTC, 128, 32).transpose(1, 0, 2).reshape(128, -1))
    masks = np.zeros((4, 128, 512), dtype=np.float32)
    i = np.arange(128)[:, None]
    j = np.arange(512)[None, :]
    for v in range(4):
        masks[v] = np.where(j <= v * 128 + i, 0.0, NEG)
    d["masks"] = np.ascontiguousarray(masks.transpose(1, 0, 2).reshape(128, -1).astype(bfdt))
    mm = np.where(i <= j[:, :128], 0.0, NEG).astype(np.float32)
    d["mmask"] = np.ascontiguousarray(np.tile(mm, (1, 4)).astype(bfdt))
    return d


class MK:
    def __init__(self, taps=()):
        self.taps = set(taps)
        nc = bass.Bass("TRN2", target_bir_lowering=False)
        self.nc = nc
        self.p = Prog(nc)
        self.sb = SB(nc)
        dr = lambda n, s, t, k="ExternalInput": nc.dram_tensor(n, s, t, kind=k).ap()
        self.x_in = dr("x", [T, E], F32)
        self.wqa = dr("wqa", [E, QL], F32)
        self.wqb = dr("wqb", [QL, 3072], F32)
        self.wkva = dr("wkva", [E, 576], F32)
        self.wkvb = dr("wkvb", [KL, 4096], F32)
        self.wout = dr("wout", [E, E], F32)
        self.qaln = dr("qaln", [128, QL // 128], F32)
        self.kvln = dr("kvln", [128, KL // 128], F32)
        self.wg = dr("wg", [E, HID], I32)
        self.wu = dr("wu", [E, HID], I32)
        self.wd = dr("wd", [HID, E], I32)
        self.sg = dr("sg", [128, HC], F32)
        self.su = dr("su", [128, HC], F32)
        self.sd = dr("sd", [128, EC], F32)
        self.sdb = dr("sdb", [128, E], F32)
        self.inpT = dr("inpT", [E, 8384], F32)
        self.outpT = dr("outpT", [DI, E], F32)
        self.rmsw = dr("rmsw", [128, EC], F32)
        self.rmsb = dr("rmsb", [128, E], F32)
        self.normw = dr("normw", [128, DI // 128], F32)
        self.convw = dr("convw", [128, (CONV // 128) * 4], F32)
        self.convb = dr("convb", [128, CONV // 128], F32)
        self.dtb = dr("dtb", [NH, 1], F32)
        self.alog = dr("alog", [NH, 1], F32)
        self.dbc = dr("dbc", [128, NH], F32)
        self.ident = dr("ident", [128, 128], BF16)
        self.identf = dr("identf", [128, 128], F32)
        self.negonesf = dr("negonesf", [1, 128], F32)
        self.fconst = dr("fconst", [128, len(FCONSTS)], F32)
        self.onesf = dr("onesf", [1, 128], F32)
        self.costm = dr("costm", [128, NTC * 32], F32)
        self.sintm = dr("sintm", [128, NTC * 32], F32)
        self.masks = dr("masks", [128, 4 * 512], BF16)
        self.mmask = dr("mmask", [128, 512], BF16)
        self.out = dr("out", [T, E], F32, k="ExternalOutput")
        dri = lambda n, s, t: nc.dram_tensor(n, s, t, kind="Internal").ap()
        self.QN = dri("QN", [H * 128, T], BF16)
        self.QP = dri("QP", [H * 64, T], BF16)
        self.Vd = dri("Vd", [T, H * 128], BF16)
        self.OT = dri("OT", [H * 128, T], BF16)
        self.X1 = dri("X1", [T, E], F32)
        self.X2 = dri("X2", [T, E], F32)
        self.X3 = dri("X3", [T, E], F32)
        self.Zd = dri("Zd", [T, DI], BF16)
        self.XTd = dri("XTd", [T, DI], BF16)
        self.Yd = dri("Yd", [T, DI], BF16)
        self.YN = dri("YN", [DI // 128, NTC, 128, 128], BF16)
        self.tapd = {}
        for tname, shape, dt in [
            ("tap_h", [T, E], BF16), ("tap_qn", [H * 128, T], BF16),
            ("tap_qp", [H * 64, T], BF16), ("tap_kn", [H * 128, T], BF16),
            ("tap_kp", [64, T], BF16), ("tap_v", [T, H * 128], BF16),
            ("tap_ot", [H * 128, T], BF16), ("tap_x1", [T, E], F32),
            ("tap_x2", [T, E], F32), ("tap_z", [T, DI], BF16),
            ("tap_xt", [T, DI], BF16), ("tap_dt", [NH, T], F32),
            ("tap_y", [T, DI], BF16), ("tap_x3", [T, E], F32),
        ]:
            if tname in self.taps:
                self.tapd[tname] = dr(tname, shape, dt, k="ExternalOutput")

        sb = self.sb
        self.c_fc = sb.alloc("c_fc", [128, len(FCONSTS)], F32)
        self.c_ident = sb.alloc("c_ident", [128, 128], BF16)
        self.c_identf = sb.alloc("c_identf", [128, 128], F32)
        self.c_ones = sb.alloc("c_ones", [1, 128], F32)
        self.c_negones = sb.alloc("c_negones", [1, 128], F32)
        self.c_cos = sb.alloc("c_cos", [128, NTC * 32], F32)
        self.c_sin = sb.alloc("c_sin", [128, NTC * 32], F32)
        self.c_masks = sb.alloc("c_masks", [128, 4 * 512], BF16)
        self.c_mmask = sb.alloc("c_mmask", [128, 512], BF16)
        self.c_vec = sb.alloc("c_vec", [128, 256], F32)
        self.c_convw = sb.alloc("c_convw", [128, (CONV // 128) * 4], F32)
        self.c_dbc = sb.alloc("c_dbc", [128, NH], F32)
        self.c_small = sb.alloc("c_small", [NH, 8], F32)
        self.c_r2p1 = sb.alloc("c_r2p1", [128, NTC], F32)
        self.c_mr2 = sb.alloc("c_mr2", [128, NTC], F32)
        self.pers_end = sb.mark()

        self.PBF = nc.alloc_psum_tensor("PBF", [128, 2048], BF16)
        self.PSF = nc.alloc_psum_tensor("PSF", [128, 3072], F32)
        self._tr_war = {}
        self._load_consts()

    def _load_consts(self):
        p = self.p
        toks = []
        toks.append(p.dma("sp", self.c_fc[:], self.fconst[:, :]))
        for i, v in enumerate(FCONSTS):
            self.nc.const_aps.register(F32, v, self.c_fc[:, i:i + 1])
        toks.append(p.dma("sp", self.c_ident[:], self.ident[:, :]))
        toks.append(p.dma("sp", self.c_identf[:], self.identf[:, :]))
        toks.append(p.dma("sp", self.c_negones[:], self.negonesf[:, :]))
        toks.append(p.dma("sp", self.c_ones[:], self.onesf[:, :]))
        toks.append(p.dma("sp", self.c_cos[:], self.costm[:, :]))
        toks.append(p.dma("sp", self.c_sin[:], self.sintm[:, :]))
        toks.append(p.dma("sp", self.c_masks[:], self.masks[:, :]))
        toks.append(p.dma("sp", self.c_mmask[:], self.mmask[:, :]))
        toks.append(p.dma("sp", self.c_vec[:, 0:44], self.sg[:, :]))
        toks.append(p.dma("sp", self.c_vec[:, 44:88], self.su[:, :]))
        toks.append(p.dma("sp", self.c_vec[:, 88:104], self.sd[:, :]))
        toks.append(p.dma("sp", self.c_vec[:, 104:120], self.rmsw[:, :]))
        toks.append(p.dma("sp", self.c_vec[:, 120:152], self.normw[:, :]))
        toks.append(p.dma("sp", self.c_vec[:, 152:185], self.convb[:, :]))
        toks.append(p.dma("sp", self.c_vec[:, 185:197], self.qaln[:, :]))
        toks.append(p.dma("sp", self.c_vec[:, 197:201], self.kvln[:, :]))
        toks.append(p.dma("sp", self.c_convw[:], self.convw[:, :]))
        toks.append(p.dma("sp", self.c_dbc[:], self.dbc[:, :]))
        toks.append(p.dma("sp", self.c_small[0:NH, 0:1], self.dtb[:, :]))
        toks.append(p.dma("sp", self.c_small[0:NH, 1:2], self.alog[:, :]))
        self.const_tok = toks[-1]
        p.do("act", lambda e: e.activation(out=self.c_small[0:NH, 2:3],
                                           in_=self.c_small[0:NH, 1:2],
                                           func=AF.Exp, scale=1.0),
             waits=[self.const_tok])
        self.tok_A = p.do("dve", lambda e: e.tensor_scalar(
            out=self.c_small[0:NH, 2:3], in0=self.c_small[0:NH, 2:3],
            scalar1=-1.0, scalar2=None, op0=ALU.mult), waits=[p.tok("act")])
        p.barrier()

    def ln_stats(self, s_x, s_sum, s_ss, s_sc, s_bi, s_tmp, wait, width=E,
                 eps=LNEPS, sq_scr=None):
        p = self.p
        t1 = p.do("dve", lambda e: e.reduce_sum(s_sum, s_x, axis=AX.X), waits=[wait])
        t2 = p.do("act", lambda e: e.activation(out=sq_scr, in_=s_x, func=AF.Square,
                                                accum_out=s_ss), waits=[wait])
        t3 = p.do("dve", lambda e: e.tensor_scalar(out=s_sum, in0=s_sum,
                                                   scalar1=1.0 / width, scalar2=None,
                                                   op0=ALU.mult), waits=[t1])
        t4 = p.do("dve", lambda e: e.tensor_tensor(out=s_tmp, in0=s_sum, in1=s_sum,
                                                   op=ALU.mult), waits=[t3])
        t5 = p.do("dve", lambda e: e.scalar_tensor_tensor(out=s_ss, in0=s_ss,
                                                          scalar=1.0 / width, in1=s_tmp,
                                                          op0=ALU.mult, op1=ALU.subtract),
                  waits=[t2, t4])
        ta = p.do("act", lambda e: e.activation(out=s_tmp, in_=s_ss, func=AF.Sqrt,
                                                bias=eps, scale=1.0), waits=[t5])
        tr = p.do("dve", lambda e: e.reciprocal(out=s_sc, in_=s_tmp), waits=[ta])
        tb = p.do("dve", lambda e: e.scalar_tensor_tensor(out=s_bi, in0=s_sum,
                                                          scalar=-1.0, in1=s_sc,
                                                          op0=ALU.mult, op1=ALU.mult),
                  waits=[tr])
        return tb

    def transpose_tiles(self, src_fn, dst_fn, n, wait_src, batch=8,
                        evac_eng="dve"):
        p = self.p
        tok = None
        i = 0
        while i < n:
            cnt = min(batch, n - i)
            half = (self._tr_war.get("half", 0)) % 2
            self._tr_war["half"] = half + 1
            ps = self.PBF[:, half * 1024: half * 1024 + cnt * 128]
            waits = [wait_src] if wait_src else []
            if half in self._tr_war:
                waits.append(self._tr_war[half])
            for k in range(cnt):
                w = waits if k == 0 else []
                p.do("pe", lambda e, i=i, k=k, ps=ps: e.transpose(
                    ps[:, k * 128:(k + 1) * 128], src_fn(i + k), self.c_ident[:]),
                    waits=w, inc=(k == cnt - 1))
            tpe = p.tok("pe")
            tok = p.do(evac_eng, lambda e, i=i, cnt=cnt, ps=ps: (
                e.tensor_copy(dst_fn(i, cnt), ps[:, 0:cnt * 128])
                if evac_eng == "dve" else
                e.activation(out=dst_fn(i, cnt), in_=ps[:, 0:cnt * 128],
                             func=AF.Copy, scale=1.0)), waits=[tpe])
            self._tr_war[half] = tok
            i += cnt
        return tok

    def phase1a(self):
        sb, p = self.sb, self.p
        base = sb.mark()
        s_wqa = sb.alloc("s_wqa", [128, EC * QL], BF16)
        s_wqb = sb.alloc("s_wqb", [128, 12 * 3072], BF16)
        s_x = sb.alloc("p1_x", [128, E], F32)
        s_h = sb.alloc("p1_h", [128, E], BF16)
        s_sq = sb.alloc("p1_sq", [128, E], BF16)
        s_hfm = sb.alloc("p1_hfm", [128, EC * 512], BF16)
        s_qan = sb.alloc("p1_qan", [128, QL], BF16)
        s_qanfm = sb.alloc("p1_qanfm", [128, 12 * 512], BF16)
        s_stat = sb.alloc("p1_stat", [128, 16], F32)
        s_qev = sb.alloc("p1_qev", [128, 2 * 512], BF16)
        s_qpe = sb.alloc("p1_qpe", [128, 1024], BF16)
        s_rt1 = sb.alloc("p1_rt1", [128, 512], F32)
        s_rt2 = sb.alloc("p1_rt2", [128, 512], F32)
        ps_qa = self.PSF[:, 0:1536]
        ps_pe = self.PSF[:, 0:1024]
        ps_qn = self.PSF[:, 2048:3072]

        wqa_r = self.wqa.rearrange("(c p) n -> p c n", p=128)
        wqb_r = self.wqb.rearrange("(c p) n -> p c n", p=128)
        for half in range(2):
            p.dma("pool", s_wqa[:].rearrange("p (c n) -> p c n", c=EC)[:, half * 8:(half + 1) * 8, :],
                  wqa_r[:, half * 8:(half + 1) * 8, :])
        for half in range(2):
            wtok = p.dma("pool", s_wqb[:].rearrange("p (c n) -> p c n", c=12)[:, half * 6:(half + 1) * 6, :],
                         wqb_r[:, half * 6:(half + 1) * 6, :])
        for c in range(12):
            p.do("dve", lambda e, c=c: e.tensor_scalar(
                out=s_wqb[:, c * 3072:(c + 1) * 3072],
                in0=s_wqb[:, c * 3072:(c + 1) * 3072],
                scalar1=self.c_vec[:, 185 + c:186 + c], scalar2=None, op0=ALU.mult),
                waits=[wtok, self.const_tok] if c == 0 else [])
        wfold = p.tok("dve")

        last_qn_dma = {}
        last_qp_dma = {}
        for s in range(NSTRIP):
            for tl in range(4):
                tc = s * 4 + tl
                tx = p.dma("sp", s_x[:], self.x_in[tc * 128:(tc + 1) * 128, :],
                           waits=[p.tok("act"), p.tok("dve")])
                tb = self.ln_stats(s_x[:], s_stat[:, 0:1], s_stat[:, 1:2],
                                   s_stat[:, 2:3], s_stat[:, 3:4], s_stat[:, 4:5],
                                   tx, sq_scr=s_sq[:])
                th = p.do("dve", lambda e: e.tensor_scalar(
                    out=s_h[:], in0=s_x[:], scalar1=s_stat[:, 2:3],
                    scalar2=s_stat[:, 3:4], op0=ALU.mult, op1=ALU.add),
                    waits=[tb, p.tok("pe")])
                if "tap_h" in self.tapd:
                    p.dma("sp", self.tapd["tap_h"][tc * 128:(tc + 1) * 128, :], s_h[:],
                          waits=[th])
                self.transpose_tiles(
                    lambda i: s_h[:, i * 128:(i + 1) * 128],
                    lambda i0, cnt, tl=tl: bass.AP(s_hfm, i0 * 512 + tl * 128,
                                                   [[EC * 512, 128], [512, cnt], [1, 128]]),
                    EC, th, batch=8)
            tstrip = p.tok("dve")
            for tl in range(4):
                for seg in range(3):
                    for ec in range(EC):
                        p.do("pe", lambda e, tl=tl, seg=seg, ec=ec: e.matmul(
                            ps_qa[:, seg * 512:(seg + 1) * 512],
                            s_hfm[:, ec * 512 + tl * 128: ec * 512 + (tl + 1) * 128],
                            s_wqa[:, ec * QL + seg * 512: ec * QL + (seg + 1) * 512],
                            start=(ec == 0), stop=(ec == EC - 1)),
                            waits=([tstrip, p.tok("act")] if (seg == 0 and ec == 0) else []),
                            inc=(ec == EC - 1))
                tqa = p.tok("pe")
                tsq = None
                for seg in range(3):
                    tsq = p.do("act", lambda e, seg=seg: e.activation(
                        out=s_sq[:, 0:512], in_=ps_qa[:, seg * 512:(seg + 1) * 512],
                        func=AF.Square, accum_out=s_stat[:, 5 + seg:6 + seg]),
                        waits=[tqa] if seg == 0 else [tsq])
                ta1 = p.do("dve", lambda e: e.tensor_tensor(
                    out=s_stat[:, 8:9], in0=s_stat[:, 5:6], in1=s_stat[:, 6:7],
                    op=ALU.add), waits=[tsq])
                ta2 = p.do("dve", lambda e: e.tensor_tensor(
                    out=s_stat[:, 8:9], in0=s_stat[:, 8:9], in1=s_stat[:, 7:8],
                    op=ALU.add), waits=[ta1])
                ts2 = p.do("act", lambda e: e.activation(
                    out=s_stat[:, 9:10], in_=s_stat[:, 8:9], func=AF.Sqrt,
                    bias=RMSEPS_A, scale=1.0 / QL), waits=[ta2])
                tr = p.do("dve", lambda e: e.reciprocal(
                    out=s_stat[:, 10:11], in_=s_stat[:, 9:10]), waits=[ts2])
                for seg in range(3):
                    p.do("act", lambda e, seg=seg: e.activation(
                        out=s_qan[:, seg * 512:(seg + 1) * 512],
                        in_=ps_qa[:, seg * 512:(seg + 1) * 512],
                        func=AF.Copy, scale=s_stat[:, 10:11]), waits=[tr])
                tqan = p.tok("act")
                self.transpose_tiles(
                    lambda i: s_qan[:, i * 128:(i + 1) * 128],
                    lambda i0, cnt, tl=tl: bass.AP(s_qanfm, i0 * 512 + tl * 128,
                                                   [[12 * 512, 128], [512, cnt], [1, 128]]),
                    12, tqan, batch=4)
            tqanfm = p.tok("dve")
            for hc in range(H):
                bank = hc % 2
                waits = [tqanfm, wfold]
                if bank in last_qn_dma:
                    waits.append(last_qn_dma[bank])
                for qc in range(12):
                    p.do("pe", lambda e, hc=hc, qc=qc, bank=bank: e.matmul(
                        ps_qn[:, bank * 512:(bank + 1) * 512],
                        s_wqb[:, qc * 3072 + hc * 128: qc * 3072 + (hc + 1) * 128],
                        s_qanfm[:, qc * 512:(qc + 1) * 512],
                        start=(qc == 0), stop=(qc == 11)),
                        waits=(waits if qc == 0 else []), inc=(qc == 11))
                tq = p.tok("pe")
                tev = p.do("act", lambda e, hc=hc, bank=bank: e.activation(
                    out=s_qev[:, bank * 512:(bank + 1) * 512],
                    in_=ps_qn[:, bank * 512:(bank + 1) * 512],
                    func=AF.Copy, scale=SCALE_ATTN), waits=[tq])
                last_qn_dma[bank] = p.dma(
                    "sp", self.QN[hc * 128:(hc + 1) * 128, s * 512:(s + 1) * 512],
                    s_qev[:, bank * 512:(bank + 1) * 512], waits=[tev])
            tqn_act = p.tok("act")
            for tl in range(4):
                tc = s * 4 + tl
                for seg in range(2):
                    for qc in range(12):
                        p.do("pe", lambda e, tl=tl, seg=seg, qc=qc: e.matmul(
                            ps_pe[:, seg * 512:(seg + 1) * 512],
                            s_qanfm[:, qc * 512 + tl * 128: qc * 512 + (tl + 1) * 128],
                            s_wqb[:, qc * 3072 + 2048 + seg * 512: qc * 3072 + 2048 + (seg + 1) * 512],
                            start=(qc == 0), stop=(qc == 11)),
                            waits=([p.tok("dve"), tqn_act] if (seg == 0 and qc == 0) else []),
                            inc=(qc == 11))
                tpe = p.tok("pe")
                pp = ps_pe.rearrange("p (h d) -> p h d", d=64)
                cosap = bass.AP(self.c_cos, tc * 32, [[NTC * 32, 128], [0, 16], [1, 32]])
                sinap = bass.AP(self.c_sin, tc * 32, [[NTC * 32, 128], [0, 16], [1, 32]])
                qq = s_qpe[:].rearrange("p (h d) -> p h d", d=64)
                r1 = s_rt1[:].rearrange("p (h d) -> p h d", d=32)
                r2 = s_rt2[:].rearrange("p (h d) -> p h d", d=32)
                p.do("dve", lambda e, pp=pp, cosap=cosap: e.scalar_tensor_tensor(
                    out=s_rt1[:], in0=pp[:, :, 0:32], scalar=SCALE_ATTN, in1=cosap,
                    op0=ALU.mult, op1=ALU.mult), waits=[tpe])
                p.do("dve", lambda e, pp=pp, sinap=sinap: e.scalar_tensor_tensor(
                    out=s_rt2[:], in0=pp[:, :, 32:64], scalar=SCALE_ATTN, in1=sinap,
                    op0=ALU.mult, op1=ALU.mult))
                p.do("dve", lambda e, qq=qq, r1=r1, r2=r2: e.tensor_tensor(
                    out=qq[:, :, 0:32], in0=r1, in1=r2, op=ALU.subtract))
                p.do("dve", lambda e, pp=pp, cosap=cosap: e.scalar_tensor_tensor(
                    out=s_rt1[:], in0=pp[:, :, 32:64], scalar=SCALE_ATTN, in1=cosap,
                    op0=ALU.mult, op1=ALU.mult))
                p.do("dve", lambda e, pp=pp, sinap=sinap: e.scalar_tensor_tensor(
                    out=s_rt2[:], in0=pp[:, :, 0:32], scalar=SCALE_ATTN, in1=sinap,
                    op0=ALU.mult, op1=ALU.mult))
                trope = p.do("dve", lambda e, qq=qq, r1=r1, r2=r2: e.tensor_tensor(
                    out=qq[:, :, 32:64], in0=r1, in1=r2, op=ALU.add))
                for g in range(2):
                    waits = [trope]
                    if g in last_qp_dma:
                        waits.append(last_qp_dma[g])
                    tev = self.transpose_tiles(
                        lambda i, g=g: s_qpe[:, (g * 4 + i) * 128:(g * 4 + i + 1) * 128],
                        lambda i0, cnt, g=g: s_qev[:, g * 512 + i0 * 128: g * 512 + (i0 + cnt) * 128],
                        4, None, batch=4) if False else None
                    half = self._tr_war.get("half", 0) % 2
                    self._tr_war["half"] = half + 1
                    ps = self.PBF[:, half * 1024: half * 1024 + 512]
                    if half in self._tr_war:
                        waits.append(self._tr_war[half])
                    for k in range(4):
                        p.do("pe", lambda e, g=g, k=k, ps=ps: e.transpose(
                            ps[:, k * 128:(k + 1) * 128],
                            s_qpe[:, (g * 4 + k) * 128:(g * 4 + k + 1) * 128],
                            self.c_ident[:]), waits=(waits if k == 0 else []),
                            inc=(k == 3))
                    tev = p.do("dve", lambda e, g=g, ps=ps: e.tensor_copy(
                        s_qev[:, g * 512:(g + 1) * 512], ps[:, 0:512]),
                        waits=[p.tok("pe")])
                    self._tr_war[half] = tev
                    last_qp_dma[g] = p.dma(
                        "sp",
                        bass.AP(self.QP.tensor, (g * 4 * 128) * T + s * 512 + tl * 128,
                                [[T, 128], [128 * T, 4], [1, 128]]),
                        s_qev[:, g * 512:(g + 1) * 512].rearrange(
                            "p (c n) -> p c n", c=4), waits=[tev])
        sb.reset(base)
        p.barrier()




    def phase1b(self):
        sb, p = self.sb, self.p
        self.s_kn = sb.alloc("s_kn", [128, H * T], BF16)
        self.s_kp = sb.alloc("s_kp", [128, T], BF16)
        self.kv_base = sb.mark()
        base = sb.mark()
        s_wkva = sb.alloc("s_wkva", [128, EC * 576], BF16)
        s_wkvb = sb.alloc("s_wkvb", [128, 4 * 4096], BF16)
        s_x = sb.alloc("pb_x", [128, E], F32)
        s_h = sb.alloc("pb_h", [128, E], BF16)
        s_sq = sb.alloc("pb_sq", [128, E], BF16)
        s_hfm = sb.alloc("pb_hfm", [128, EC * 128], BF16)
        s_ckv = sb.alloc("pb_ckv", [128, KL], BF16)
        s_ckvfm = sb.alloc("pb_ckvfm", [128, 4 * 512], BF16)
        s_stat = sb.alloc("pb_stat", [128, 16], F32)
        s_kpe = sb.alloc("pb_kpe", [128, 64], BF16)
        s_vev = sb.alloc("pb_vev", [128, 2 * 512], BF16)
        s_rt1 = sb.alloc("pb_rt1", [128, 32], F32)
        s_rt2 = sb.alloc("pb_rt2", [128, 32], F32)
        ps_ckv = self.PSF[:, 0:512]
        ps_kpe = self.PSF[:, 512:576]
        ps_kn = self.PSF[:, 1024:2048]
        ps_v = self.PSF[:, 2048:3072]

        wkva_r = self.wkva.rearrange("(c p) n -> p c n", p=128)
        wkvb_r = self.wkvb.rearrange("(c p) n -> p c n", p=128)
        for half in range(2):
            p.dma("pool", s_wkva[:].rearrange("p (c n) -> p c n", c=EC)[:, half * 8:(half + 1) * 8, :],
                  wkva_r[:, half * 8:(half + 1) * 8, :])
        wtok = p.dma("pool", s_wkvb[:].rearrange("p (c n) -> p c n", c=4), wkvb_r)
        for c in range(4):
            p.do("dve", lambda e, c=c: e.tensor_scalar(
                out=s_wkvb[:, c * 4096:(c + 1) * 4096],
                in0=s_wkvb[:, c * 4096:(c + 1) * 4096],
                scalar1=self.c_vec[:, 197 + c:198 + c], scalar2=None, op0=ALU.mult),
                waits=[wtok, self.const_tok] if c == 0 else [])
        wfold = p.tok("dve")

        last_v_dma = {}
        last_kn = {}
        for s in range(NSTRIP):
            for tl in range(4):
                tc = s * 4 + tl
                tx = p.dma("sp", s_x[:], self.x_in[tc * 128:(tc + 1) * 128, :],
                           waits=[p.tok("act"), p.tok("dve")])
                tb = self.ln_stats(s_x[:], s_stat[:, 0:1], s_stat[:, 1:2],
                                   s_stat[:, 2:3], s_stat[:, 3:4], s_stat[:, 4:5],
                                   tx, sq_scr=s_sq[:])
                th = p.do("dve", lambda e: e.tensor_scalar(
                    out=s_h[:], in0=s_x[:], scalar1=s_stat[:, 2:3],
                    scalar2=s_stat[:, 3:4], op0=ALU.mult, op1=ALU.add),
                    waits=[tb, p.tok("pe")])
                self.transpose_tiles(
                    lambda i: s_h[:, i * 128:(i + 1) * 128],
                    lambda i0, cnt: bass.AP(s_hfm, i0 * 128,
                                            [[EC * 128, 128], [128, cnt], [1, 128]]),
                    EC, th, batch=8)
                thfm = p.tok("dve")
                for ec in range(EC):
                    p.do("pe", lambda e, ec=ec: e.matmul(
                        ps_ckv[:, 0:512],
                        s_hfm[:, ec * 128:(ec + 1) * 128],
                        s_wkva[:, ec * 576: ec * 576 + 512],
                        start=(ec == 0), stop=(ec == EC - 1)),
                        waits=([thfm, p.tok("act"), wtok] if ec == 0 else []),
                        inc=(ec == EC - 1))
                for ec in range(EC):
                    p.do("pe", lambda e, ec=ec: e.matmul(
                        ps_kpe[:, 0:64],
                        s_hfm[:, ec * 128:(ec + 1) * 128],
                        s_wkva[:, ec * 576 + 512: (ec + 1) * 576],
                        start=(ec == 0), stop=(ec == EC - 1)),
                        inc=(ec == EC - 1))
                tckv = p.tok("pe")
                p.do("act", lambda e: e.activation(
                    out=s_sq[:, 0:512], in_=ps_ckv[:, 0:512],
                    func=AF.Square, accum_out=s_stat[:, 5:6]), waits=[tckv])
                ts2 = p.do("act", lambda e: e.activation(
                    out=s_stat[:, 9:10], in_=s_stat[:, 5:6], func=AF.Sqrt,
                    bias=RMSEPS_A, scale=1.0 / KL), waits=[p.tok("act")])
                trr = p.do("dve", lambda e: e.reciprocal(
                    out=s_stat[:, 10:11], in_=s_stat[:, 9:10]), waits=[ts2])
                tckvn = p.do("act", lambda e: e.activation(
                    out=s_ckv[:, 0:512], in_=ps_ckv[:, 0:512],
                    func=AF.Copy, scale=s_stat[:, 10:11]), waits=[trr])
                cosap = bass.AP(self.c_cos, tc * 32, [[NTC * 32, 128], [1, 32]])
                sinap = bass.AP(self.c_sin, tc * 32, [[NTC * 32, 128], [1, 32]])
                p.do("dve", lambda e, cosap=cosap: e.tensor_tensor(
                    out=s_rt1[:], in0=ps_kpe[:, 0:32], in1=cosap, op=ALU.mult),
                    waits=[tckv])
                p.do("dve", lambda e, sinap=sinap: e.tensor_tensor(
                    out=s_rt2[:], in0=ps_kpe[:, 32:64], in1=sinap, op=ALU.mult))
                p.do("dve", lambda e: e.tensor_tensor(
                    out=s_kpe[:, 0:32], in0=s_rt1[:], in1=s_rt2[:], op=ALU.subtract))
                p.do("dve", lambda e, cosap=cosap: e.tensor_tensor(
                    out=s_rt1[:], in0=ps_kpe[:, 32:64], in1=cosap, op=ALU.mult))
                p.do("dve", lambda e, sinap=sinap: e.tensor_tensor(
                    out=s_rt2[:], in0=ps_kpe[:, 0:32], in1=sinap, op=ALU.mult))
                tkpe = p.do("dve", lambda e: e.tensor_tensor(
                    out=s_kpe[:, 32:64], in0=s_rt1[:], in1=s_rt2[:], op=ALU.add))
                half = self._tr_war.get("half", 0) % 2
                self._tr_war["half"] = half + 1
                pstr = self.PBF[:, half * 1024: half * 1024 + 128]
                w = [tkpe] + ([self._tr_war[half]] if half in self._tr_war else [])
                p.do("pe", lambda e, pstr=pstr: e.transpose(
                    pstr[0:64, 0:128], s_kpe[:, 0:64], self.c_ident[:]), waits=w)
                tkp = p.do("dve", lambda e, pstr=pstr, tc=tc: e.tensor_copy(
                    self.s_kp[0:64, tc * 128:(tc + 1) * 128], pstr[0:64, 0:128]),
                    waits=[p.tok("pe")])
                self._tr_war[half] = tkp
                self.transpose_tiles(
                    lambda i: s_ckv[:, i * 128:(i + 1) * 128],
                    lambda i0, cnt, tl=tl: bass.AP(s_ckvfm, i0 * 512 + tl * 128,
                                                   [[4 * 512, 128], [512, cnt], [1, 128]]),
                    4, tckvn, batch=4)
            tstrip = p.tok("dve")
            for hc in range(H):
                bank = hc % 2
                for kc in range(4):
                    p.do("pe", lambda e, hc=hc, kc=kc, bank=bank: e.matmul(
                        ps_kn[:, bank * 512:(bank + 1) * 512],
                        s_wkvb[:, kc * 4096 + hc * 128: kc * 4096 + (hc + 1) * 128],
                        s_ckvfm[:, kc * 512:(kc + 1) * 512],
                        start=(kc == 0), stop=(kc == 3)),
                        waits=([tstrip, wfold] + ([last_kn[bank]] if bank in last_kn else [])
                               if kc == 0 else []),
                        inc=(kc == 3))
                last_kn[bank] = p.do("dve", lambda e, hc=hc, bank=bank, s=s: e.tensor_copy(
                    self.s_kn[:, hc * T + s * 512:(hc * T) + (s + 1) * 512],
                    ps_kn[:, bank * 512:(bank + 1) * 512]), waits=[p.tok("pe")])
            for tl in range(4):
                tc = s * 4 + tl
                for seg in range(4):
                    bank = seg % 2
                    w0 = [p.tok("dve")]
                    if (bank,) in last_v_dma:
                        w0.append(last_v_dma[(bank,)])
                    for kc in range(4):
                        p.do("pe", lambda e, tl=tl, seg=seg, kc=kc, bank=bank: e.matmul(
                            ps_v[:, bank * 512:(bank + 1) * 512],
                            s_ckvfm[:, kc * 512 + tl * 128: kc * 512 + (tl + 1) * 128],
                            s_wkvb[:, kc * 4096 + 2048 + seg * 512: kc * 4096 + 2048 + (seg + 1) * 512],
                            start=(kc == 0), stop=(kc == 3)),
                            waits=(w0 if kc == 0 else []), inc=(kc == 3))
                    tev = p.do("act", lambda e, bank=bank: e.activation(
                        out=s_vev[:, bank * 512: bank * 512 + 512],
                        in_=ps_v[:, bank * 512:(bank + 1) * 512],
                        func=AF.Copy, scale=1.0), waits=[p.tok("pe")])
                    last_v_dma[(bank,)] = p.dma(
                        "sp", self.Vd[tc * 128:(tc + 1) * 128, seg * 512:(seg + 1) * 512],
                        s_vev[:, bank * 512: bank * 512 + 512], waits=[tev])
        sb.reset(base)
        p.barrier()

    def phase2(self):
        sb, p = self.sb, self.p
        base = sb.mark()
        s_v = sb.alloc("p2_v", [128, NTC * 2048], BF16)
        s_qn = sb.alloc("p2_qn", [128, H * 512], BF16)
        s_qp = sb.alloc("p2_qp", [128, 8 * 512], BF16)
        s_p = sb.alloc("p2_p", [128, 2 * 4 * 2048], BF16)
        s_pt = sb.alloc("p2_pt", [128, 2 * 1024], BF16)
        s_otev = sb.alloc("p2_otev", [128, 2 * 512], BF16)
        s_st = sb.alloc("p2_st", [128, 64], F32)
        ps_O = self.PSF[:, 2048:2560]

        vr = self.Vd.rearrange("(c p) n -> p c n", p=128)
        vv = s_v[:].rearrange("p (c n) -> p c n", c=NTC)
        tv = []
        for qtr in range(2):
            tv.append(p.dma("sp", vv[:, qtr * 8:(qtr + 1) * 8, :], vr[:, qtr * 8:(qtr + 1) * 8, :]))
        tkp2 = p.dma("pool", self.s_kp[64:128, :], self.s_kp[0:64, :])
        qnr = self.QN.rearrange("(c p) t -> p c t", p=128)
        qpr = self.QP.rearrange("(c p) t -> p c t", p=128)

        pending = None
        state = {"oev": {}, "odma": {}, "gi": 0}

        def emit_ptpv(info):
            s, h, par, nseg, tnorm = info
            nkt = nseg * 4
            opar = state["gi"] % 2
            state["gi"] += 1
            psO = self.PSF[:, 2048 + opar * 512: 2048 + (opar + 1) * 512]
            for kp in range(nkt // 2):
                kt0 = kp * 2
                half = self._tr_war.get("half", 0) % 2
                self._tr_war["half"] = half + 1
                psb = self.PBF[:, half * 1024: half * 1024 + 1024]
                w0 = list(tnorm)
                if half in self._tr_war:
                    w0.append(self._tr_war[half])
                for j in range(8):
                    kt, qt = kt0 + j // 4, j % 4
                    p.do("pe", lambda e, kt=kt, qt=qt, j=j, psb=psb, par=par: e.transpose(
                        psb[:, j * 128:(j + 1) * 128],
                        s_p[:, (par * 4 + qt) * 2048 + kt * 128:
                            (par * 4 + qt) * 2048 + (kt + 1) * 128],
                        self.c_ident[:]),
                        waits=(w0 if j == 0 else []), inc=(j == 7))
                tev = p.do("dve", lambda e, half=half, psb=psb: e.tensor_copy(
                    s_pt[:, half * 1024:(half + 1) * 1024], psb[:]),
                    waits=[p.tok("pe")])
                self._tr_war[half] = tev
                for j2 in range(2):
                    kt = kt0 + j2
                    w1 = [tev]
                    if kt == 0 and opar in state["oev"]:
                        w1.append(state["oev"][opar])
                    p.do("pe", lambda e, h=h, kt=kt, half=half, j2=j2, nkt=nkt, psO=psO: e.matmul(
                        psO, s_v[:, kt * 2048 + h * 128: kt * 2048 + (h + 1) * 128],
                        s_pt[:, half * 1024 + j2 * 512: half * 1024 + (j2 + 1) * 512],
                        start=(kt == 0), stop=(kt == nkt - 1)),
                        waits=(w1 if j2 == 0 else []), inc=(kt == nkt - 1))
            tO = p.tok("pe")
            w2 = [tO] + ([state["odma"][opar]] if opar in state["odma"] else [])
            tevo = p.do("dve", lambda e, opar=opar, psO=psO: e.tensor_copy(
                s_otev[:, opar * 512:(opar + 1) * 512], psO), waits=w2)
            state["oev"][opar] = tevo
            state["odma"][opar] = p.dma(
                "sp", self.OT[h * 128:(h + 1) * 128, s * 512:(s + 1) * 512],
                s_otev[:, opar * 512:(opar + 1) * 512], waits=[tevo])

        for s in range(NSTRIP):
            nseg = s + 1
            tq1 = p.dma("sp", s_qn[:].rearrange("p (c t) -> p c t", c=H),
                        qnr[:, :, s * 512:(s + 1) * 512], waits=[p.tok("pe")])
            tq2 = p.dma("sp", s_qp[:].rearrange("p (c t) -> p c t", c=8),
                        qpr[:, :, s * 512:(s + 1) * 512])
            bank_free = {}
            strip_start = p.tok("act")
            first_mm = True
            for h in range(H):
                par = (s * H + h) % 2
                texp = []
                bsets = {1: 4, 2: 2, 3: 1, 4: 1}[nseg]
                for qt in range(4):
                    bset = qt % bsets
                    b0 = bset * nseg * 512
                    for seg in range(nseg):
                        w0 = [bank_free.get((bset, seg), strip_start)]
                        if first_mm:
                            w0 += [tq1, tq2, tkp2] + tv
                            first_mm = False
                        p.do("pe", lambda e, h=h, qt=qt, seg=seg, b0=b0: e.matmul(
                            self.PSF[:, b0 + seg * 512: b0 + (seg + 1) * 512],
                            s_qn[:, h * 512 + qt * 128: h * 512 + (qt + 1) * 128],
                            self.s_kn[:, h * T + seg * 512: h * T + (seg + 1) * 512],
                            start=True, stop=False),
                            waits=w0, inc=False)
                        p.do("pe", lambda e, h=h, qt=qt, seg=seg, nseg=nseg, b0=b0: e.matmul(
                            self.PSF[:, b0 + seg * 512: b0 + (seg + 1) * 512],
                            s_qp[(h % 2) * 64:(h % 2) * 64 + 64,
                                 (h // 2) * 512 + qt * 128: (h // 2) * 512 + (qt + 1) * 128],
                            self.s_kp[(h % 2) * 64:(h % 2) * 64 + 64,
                                      seg * 512:(seg + 1) * 512],
                            start=False, stop=(seg != nseg - 1)),
                            inc=(seg != nseg - 1))
                        if seg == nseg - 1:
                            p.do("pe", lambda e, qt=qt, nseg=nseg, b0=b0: e.matmul(
                                self.PSF[:, b0 + (nseg - 1) * 512: b0 + nseg * 512],
                                self.c_ident[:], self.c_masks[:, qt * 512:(qt + 1) * 512],
                                start=False, stop=True), inc=True)
                        tSseg = p.tok("pe")
                        pcol = par * 32 + 16 + qt * 4 + seg
                        te = p.do("act", lambda e, qt=qt, seg=seg, par=par, b0=b0, pcol=pcol: e.activation(
                            out=s_p[:, (par * 4 + qt) * 2048 + seg * 512:
                                    (par * 4 + qt) * 2048 + (seg + 1) * 512],
                            in_=self.PSF[:, b0 + seg * 512: b0 + (seg + 1) * 512],
                            func=AF.Exp, scale=1.0,
                            accum_out=s_st[:, pcol:pcol + 1]),
                            waits=[tSseg])
                        bank_free[(bset, seg)] = te
                        texp.append(te)
                    p.do("dve", lambda e, qt=qt, nseg=nseg, par=par: e.reduce_sum(
                        s_st[:, par * 32 + 4 + qt:par * 32 + 5 + qt],
                        s_st[:, par * 32 + 16 + qt * 4: par * 32 + 16 + qt * 4 + nseg],
                        axis=AX.X), waits=[texp[-1]])
                tsum = p.tok("dve")
                trec = p.do("dve", lambda e, par=par: e.reciprocal(
                    out=s_st[:, par * 32 + 8:par * 32 + 12],
                    in_=s_st[:, par * 32 + 4:par * 32 + 8]), waits=[tsum] + texp)
                tnorm = []
                for qt in range(4):
                    tn = p.do("dve", lambda e, qt=qt, nseg=nseg, par=par: e.tensor_scalar(
                        out=s_p[:, (par * 4 + qt) * 2048: (par * 4 + qt) * 2048 + nseg * 512],
                        in0=s_p[:, (par * 4 + qt) * 2048: (par * 4 + qt) * 2048 + nseg * 512],
                        scalar1=s_st[:, par * 32 + 8 + qt:par * 32 + 9 + qt],
                        scalar2=None, op0=ALU.mult), waits=[trec])
                    tnorm.append(tn)
                info = (s, h, par, nseg, tnorm)
                if pending is not None:
                    emit_ptpv(pending)
                pending = info
        if pending is not None:
            emit_ptpv(pending)
        sb.reset(base)
        p.barrier()

    def phase2b(self):
        sb, p = self.sb, self.p
        base = self.sb.BASE
        sb.reset(max(base, self.pers_end))
        base = sb.mark()
        s_otf = sb.alloc("p2b_ot", [128, H * T], BF16)
        s_wo = sb.alloc("p2b_wo", [128, EC * E], BF16)
        s_x = sb.alloc("p2b_x", [128, E], F32)
        s_x1 = sb.alloc("p2b_x1", [128, E], F32)
        otr = self.OT.rearrange("(c p) t -> p c t", p=128)
        ov = s_otf[:].rearrange("p (c t) -> p c t", c=H)
        for qtr in range(2):
            tot = p.dma("sp", ov[:, qtr * 8:(qtr + 1) * 8, :], otr[:, qtr * 8:(qtr + 1) * 8, :])
        wor = self.wout.rearrange("(c p) n -> p c n", p=128)
        wv = s_wo[:].rearrange("p (c n) -> p c n", c=EC)
        for qtr in range(2):
            tw = p.dma("pool", wv[:, qtr * 8:(qtr + 1) * 8, :], wor[:, qtr * 8:(qtr + 1) * 8, :])
        for tc in range(NTC):
            tx = p.dma("sp", s_x[:], self.x_in[tc * 128:(tc + 1) * 128, :],
                       waits=[p.tok("dve")])
            for eseg in range(4):
                bank = eseg % 2
                for dvc in range(EC):
                    p.do("pe", lambda e, tc=tc, eseg=eseg, dvc=dvc, bank=bank: e.matmul(
                        self.PSF[:, bank * 512:(bank + 1) * 512],
                        s_otf[:, dvc * T + tc * 128: dvc * T + (tc + 1) * 128],
                        s_wo[:, dvc * E + eseg * 512: dvc * E + (eseg + 1) * 512],
                        start=(dvc == 0), stop=(dvc == EC - 1)),
                        waits=([tot, tw, p.tok("dve")] if dvc == 0 else []),
                        inc=(dvc == EC - 1))
                tmm = p.tok("pe")
                p.do("dve", lambda e, eseg=eseg, bank=bank: e.tensor_tensor(
                    out=s_x1[:, eseg * 512:(eseg + 1) * 512],
                    in0=self.PSF[:, bank * 512:(bank + 1) * 512],
                    in1=s_x[:, eseg * 512:(eseg + 1) * 512], op=ALU.add),
                    waits=[tmm, tx])
            tev = p.tok("dve")
            p.dma("sp", self.X1[tc * 128:(tc + 1) * 128, :], s_x1[:], waits=[tev])
            if "tap_x1" in self.tapd:
                p.dma("sp", self.tapd["tap_x1"][tc * 128:(tc + 1) * 128, :], s_x1[:],
                      waits=[tev])
        sb.reset(base)
        p.barrier()

    def phase_ffn(self, x_src, x_dst, tap=None):
        sb, p = self.sb, self.p
        base = sb.mark()
        off_hfm = sb.mark()
        s_hfm = sb.alloc("f_hfm", [128, EC * 1024], BF16)
        s_dfm = sb.alloc("f_dfm", [128, EC * 1024], BF16, off=off_hfm)
        s_act = sb.alloc("f_act", [128, HC * 1024], BF16)
        s_wgu = sb.alloc("f_wgu", [128, 4 * EC * 128], BF16)
        s_wd = sb.alloc("f_wd", [128, 2 * HC * 128], BF16, off=None)
        off_x = sb.mark()
        s_x = sb.alloc("f_x", [128, 2 * E], F32)
        off_h = sb.mark()
        s_h = sb.alloc("f_h", [128, 2 * E], BF16)
        s_sq = sb.alloc("f_sq", [128, E], BF16)
        s_stat = sb.alloc("f_stat", [128, 16], F32)
        s_gt = sb.alloc("f_gt", [128, 2 * 512], BF16)
        s_xr = sb.alloc("f_xr", [128, E], F32, off=off_x)
        s_xo = sb.alloc("f_xo", [128, E], F32, off=off_h)
        NSL = HC
        wg_r = self.wg.rearrange("(c p) n -> p c n", p=128)
        wu_r = self.wu.rearrange("(c p) n -> p c n", p=128)
        wd_r = self.wd.rearrange("(c p) n -> p c n", p=128)

        def load_gu_slab(si, war):
            toks = []
            for j, (src_r, bufo) in enumerate([(wg_r, 0), (wu_r, 2)]):
                buf = (bufo + si % 2) * EC * 128
                dst = bass.AP(s_wgu, buf, [[4 * EC * 128, 128], [128, EC], [1, 128]])
                tl_ = p.dma("pool", dst, src_r[:, :, si * 128:(si + 1) * 128],
                            waits=war)
                tk = p.do("dve", lambda e, buf=buf: e.tensor_scalar(
                    out=s_wgu[:, buf:buf + EC * 128], in0=s_wgu[:, buf:buf + EC * 128],
                    scalar1=-128.0, scalar2=None, op0=ALU.add), waits=[tl_])
                toks.append(tk)
            return toks

        def load_wd_blk(ec, war):
            buf = (ec % 2) * HC * 128
            dst = bass.AP(s_wd, buf, [[2 * HC * 128, 128], [128, HC], [1, 128]])
            for cq in range(2):
                tl_ = p.dma("pool",
                            bass.AP(s_wd, buf + cq * 22 * 128,
                                    [[2 * HC * 128, 128], [128, 22], [1, 128]]),
                            wd_r[:, cq * 22:(cq + 1) * 22, ec * 128:(ec + 1) * 128],
                            waits=war)
            tk = p.do("dve", lambda e, buf=buf: e.tensor_scalar(
                out=s_wd[:, buf:buf + HC * 128], in0=s_wd[:, buf:buf + HC * 128],
                scalar1=-128.0, scalar2=None, op0=ALU.add), waits=[tl_])
            return tk

        ln_h_tok = {}
        ln_tr_tok = {}
        for hf in range(2):
            for tl in range(8):
                tc = hf * 8 + tl
                par = tc % 2
                sc = s_stat[:, par * 8: par * 8 + 8]
                xb = s_x[:, par * E:(par + 1) * E]
                hb = s_h[:, par * E:(par + 1) * E]
                tx = p.dma("sp", xb, x_src[tc * 128:(tc + 1) * 128, :],
                           waits=[ln_h_tok.get(par, ("pe", 0))])
                tb = self.ln_stats(xb, sc[:, 0:1], sc[:, 1:2],
                                   sc[:, 2:3], sc[:, 3:4], sc[:, 4:5],
                                   tx, sq_scr=s_sq[:])
                th = p.do("dve", lambda e, xb=xb, hb=hb, sc=sc: e.tensor_scalar(
                    out=hb, in0=xb, scalar1=sc[:, 2:3],
                    scalar2=sc[:, 3:4], op0=ALU.mult, op1=ALU.add),
                    waits=[tb, ln_tr_tok.get(par, ("pe", 0))])
                ln_h_tok[par] = th
                ln_tr_tok[par] = self.transpose_tiles(
                    lambda i, hb=hb: hb[:, i * 128:(i + 1) * 128],
                    lambda i0, cnt, tl=tl: bass.AP(s_hfm, i0 * 1024 + tl * 128,
                                                   [[EC * 1024, 128], [1024, cnt], [1, 128]]),
                    EC, th, batch=8)
            thalf = p.tok("dve")
            wtoks = {0: load_gu_slab(0, [p.tok("pe")])}
            last_ev = {}
            slab_war = {}
            for hc in range(HC):
                si = hc
                if si + 1 < NSL and (si + 1) not in wtoks:
                    wtoks[si + 1] = load_gu_slab(
                        si + 1, [slab_war.get((si + 1) % 2, ("pe", 0))])
                gbuf = (si % 2) * EC * 128
                ubuf = (2 + si % 2) * EC * 128
                for seg in range(2):
                    w0 = [thalf] + wtoks[si] + [last_ev.get(("g", seg), ("pe", 0)),
                                                last_ev.get(("u", seg), ("pe", 0))]
                    for ec in range(EC):
                        p.do("pe", lambda e, seg=seg, ec=ec, gbuf=gbuf: e.matmul(
                            self.PSF[:, seg * 512:(seg + 1) * 512],
                            s_wgu[:, gbuf + ec * 128: gbuf + ec * 128 + 128],
                            s_hfm[:, ec * 1024 + seg * 512: ec * 1024 + (seg + 1) * 512],
                            start=(ec == 0), stop=(ec == EC - 1)),
                            waits=(w0 if ec == 0 else []), inc=False)
                        p.do("pe", lambda e, seg=seg, ec=ec, ubuf=ubuf: e.matmul(
                            self.PSF[:, 1024 + seg * 512: 1024 + (seg + 1) * 512],
                            s_wgu[:, ubuf + ec * 128: ubuf + ec * 128 + 128],
                            s_hfm[:, ec * 1024 + seg * 512: ec * 1024 + (seg + 1) * 512],
                            start=(ec == 0), stop=(ec == EC - 1)),
                            inc=(ec == EC - 1))
                    tmm = p.tok("pe")
                    slab_war[si % 2] = tmm
                    tg = p.do("act", lambda e, hc=hc, seg=seg: e.activation(
                        out=s_gt[:, seg * 512:(seg + 1) * 512],
                        in_=self.PSF[:, seg * 512:(seg + 1) * 512],
                        func=AF.Silu, scale=self.c_vec[:, hc:hc + 1]), waits=[tmm])
                    last_ev[("g", seg)] = tg
                    tu = p.do("dve", lambda e, hc=hc, seg=seg: e.scalar_tensor_tensor(
                        out=s_act[:, hc * 1024 + seg * 512: hc * 1024 + (seg + 1) * 512],
                        in0=self.PSF[:, 1024 + seg * 512: 1024 + (seg + 1) * 512],
                        scalar=self.c_vec[:, 44 + hc:45 + hc],
                        in1=s_gt[:, seg * 512:(seg + 1) * 512],
                        op0=ALU.mult, op1=ALU.mult), waits=[tg])
                    last_ev[("u", seg)] = tu
            tact_done = [p.tok("dve"), p.tok("act")]
            dtok = {0: load_wd_blk(0, [p.tok("pe")])}
            last_dev = {}
            for ec in range(EC):
                if ec + 1 < EC:
                    dtok[ec + 1] = load_wd_blk(ec + 1, [last_dev.get((ec + 1) % 2, ("pe", 0))])
                wbuf = (ec % 2) * HC * 128
                for seg in range(2):
                    w0 = tact_done + [dtok[ec], last_dev.get(("e", seg), ("pe", 0))]
                    for c in range(HC):
                        p.do("pe", lambda e, seg=seg, c=c, wbuf=wbuf: e.matmul(
                            self.PSF[:, seg * 512:(seg + 1) * 512],
                            s_wd[:, wbuf + c * 128: wbuf + (c + 1) * 128],
                            s_act[:, c * 1024 + seg * 512: c * 1024 + (seg + 1) * 512],
                            start=(c == 0), stop=(c == HC - 1)),
                            waits=(w0 if c == 0 else []), inc=(c == HC - 1))
                    tmm = p.tok("pe")
                    last_dev[(ec % 2)] = tmm
                    td = p.do("act", lambda e, ec=ec, seg=seg: e.activation(
                        out=s_dfm[:, ec * 1024 + seg * 512: ec * 1024 + (seg + 1) * 512],
                        in_=self.PSF[:, seg * 512:(seg + 1) * 512],
                        func=AF.Copy, scale=self.c_vec[:, 88 + ec:89 + ec]), waits=[tmm])
                    last_dev[("e", seg)] = td
            tdown = p.tok("act")
            last_store = ("pe", 0)
            for tl in range(8):
                tc = hf * 8 + tl
                txr = p.dma("sp", s_xr[:], x_src[tc * 128:(tc + 1) * 128, :],
                            waits=[p.tok("dve")])
                for ecg in range(2):
                    half = self._tr_war.get("half", 0) % 2
                    self._tr_war["half"] = half + 1
                    psb = self.PBF[:, half * 1024: half * 1024 + 1024]
                    w0 = [tdown] + ([self._tr_war[half]] if half in self._tr_war else [])
                    for k in range(8):
                        p.do("pe", lambda e, k=k, tl=tl, ecg=ecg, psb=psb: e.transpose(
                            psb[:, k * 128:(k + 1) * 128],
                            s_dfm[:, (ecg * 8 + k) * 1024 + tl * 128:
                                  (ecg * 8 + k) * 1024 + (tl + 1) * 128],
                            self.c_ident[:]), waits=(w0 if k == 0 else []),
                            inc=(k == 7))
                    tev = p.do("dve", lambda e, ecg=ecg, psb=psb: e.tensor_tensor(
                        out=s_xo[:, ecg * 1024:(ecg + 1) * 1024],
                        in0=psb[:, 0:1024],
                        in1=s_xr[:, ecg * 1024:(ecg + 1) * 1024], op=ALU.add),
                        waits=[p.tok("pe"), txr, last_store])
                    self._tr_war[half] = tev
                tevd = p.tok("dve")
                last_store = p.dma("sp", x_dst[tc * 128:(tc + 1) * 128, :], s_xo[:],
                                   waits=[tevd])
                if tap and tap in self.tapd:
                    p.dma("sp", self.tapd[tap][tc * 128:(tc + 1) * 128, :], s_xo[:],
                          waits=[tevd])
        sb.reset(base)
        p.barrier()

    def phase4a(self):
        sb, p = self.sb, self.p
        self.s_bc = sb.alloc("m_bc", [128, T], BF16)
        self.s_cfm = sb.alloc("m_cfm", [64, T], BF16)
        self.s_btm = sb.alloc("m_btm", [128, NCH * 64], BF16)
        self.s_stm = sb.alloc("m_stm", [128, NCH * 64], F32)
        self.s_estm = sb.alloc("m_estm", [128, NCH * 64], F32)
        self.s_nstm = sb.alloc("m_nstm", [128, NCH * 64], F32)
        self.s_dttm = sb.alloc("m_dttm", [128, NCH * 64], F32)
        self.s_wtm = sb.alloc("m_wtm", [128, NCH * 64], F32)
        self.s_state = sb.alloc("m_state", [64, NH * 64], F32)
        self.s_stateb = sb.alloc("m_stateb", [64, NH * 64], BF16)
        self.off_dt = sb.mark()
        self.s_dt = sb.alloc("m_dt", [64, T], F32)
        self.off_ss = sb.mark()
        self.s_s = sb.alloc("m_s", [64, T], F32)
        self.s_z64 = sb.alloc("m_z64", [64, 128], F32)
        self.mpers_end = sb.mark()
        base = sb.mark()
        s_ufm = sb.alloc("m_ufm", [128, EC * T], BF16)
        s_w = sb.alloc("m_w", [128, 2 * EC * 512], BF16)
        off_mx = sb.mark()
        s_x = sb.alloc("m_x", [128, 2 * E], F32)
        off_u = sb.mark()
        s_u = sb.alloc("m_u", [128, 2 * E], BF16)
        off_sq = sb.mark()
        s_sq = sb.alloc("m_sq", [128, E], BF16)
        s_stat = sb.alloc("m_stat", [128, 32], F32)
        s_zev = sb.alloc("m_zev", [128, 2 * 512], BF16, off=self.off_ss)
        s_rmsb = sb.alloc("m_rmsb", [128, E], F32, off=self.off_dt)
        s_xbc = sb.alloc("m_xbc", [128, 3 + T + 1], BF16, off=off_mx)
        off_cv = off_mx + 4128
        s_cv = sb.alloc("m_cv", [128, T], F32, off=off_cv)
        s_cvb = sb.alloc("m_cvb", [128, T], BF16, off=off_u + 2048)
        s_xtev = sb.alloc("m_xtev", [128, 2 * 1024], BF16, off=off_sq)
        s_dtmp = sb.alloc("m_dtmp", [64, T], F32, off=off_cv)
        p.do("pool", lambda e: e.memset(self.s_z64[:], 0.0))
        trb = p.dma("pool", s_rmsb[:], self.rmsb[:, :])
        inp_r = self.inpT.rearrange("(c p) n -> p c n", p=128)

        def load_slab(sl, war):
            cols = 512 if sl < 16 else 192
            buf = (sl % 2) * EC * 512
            tl_ = None
            for cq in range(2):
                tl_ = p.dma("pool",
                            bass.AP(s_w, buf + cq * 8 * 512,
                                    [[2 * EC * 512, 128], [512, 8], [1, cols]]),
                            inp_r[:, cq * 8:(cq + 1) * 8, sl * 512: sl * 512 + cols],
                            waits=war)
            return tl_

        slab_war = {}
        wtoks = {0: load_slab(0, []), 1: load_slab(1, [])}
        zr = self.Zd.rearrange("(c p) n -> p c n", p=128)
        last_zev = {}
        cc_done = {}

        x_war = {}
        u_war = {}
        for tc in range(NTC):
            par = tc % 2
            st = s_stat[:, par * 16: par * 16 + 16]
            xb = s_x[:, par * E:(par + 1) * E]
            ub = s_u[:, par * E:(par + 1) * E]
            tx = p.dma("sp", xb, self.X2[tc * 128:(tc + 1) * 128, :],
                       waits=[x_war.get(par, ("pe", 0))])
            tb = self.ln_stats(xb, st[:, 0:1], st[:, 1:2], st[:, 2:3],
                               self.c_mr2[:, tc:tc + 1], st[:, 4:5], tx,
                               sq_scr=s_sq[:])
            t1 = p.do("dve", lambda e, tc=tc, st=st: e.tensor_scalar(
                out=self.c_r2p1[:, tc:tc + 1], in0=st[:, 2:3], scalar1=1.0,
                scalar2=None, op0=ALU.add), waits=[tb])
            t2 = p.do("dve", lambda e, st=st: e.tensor_tensor(
                out=st[:, 5:6], in0=st[:, 2:3], in1=st[:, 2:3],
                op=ALU.mult), waits=[tb])
            t3 = p.do("dve", lambda e, st=st: e.tensor_tensor(
                out=st[:, 5:6], in0=st[:, 5:6], in1=st[:, 1:2],
                op=ALU.mult), waits=[t2])
            t4 = p.do("act", lambda e, st=st: e.activation(
                out=st[:, 6:7], in_=st[:, 5:6], func=AF.Sqrt,
                bias=RMSEPS_B, scale=1.0), waits=[t3])
            t5 = p.do("dve", lambda e, st=st: e.reciprocal(
                out=st[:, 7:8], in_=st[:, 6:7]), waits=[t4])
            t6 = p.do("dve", lambda e, st=st: e.tensor_tensor(
                out=st[:, 8:9], in0=st[:, 2:3], in1=st[:, 7:8],
                op=ALU.mult), waits=[t5])
            t7 = p.do("dve", lambda e, st=st: e.scalar_tensor_tensor(
                out=st[:, 9:10], in0=st[:, 0:1], scalar=-1.0,
                in1=st[:, 8:9], op0=ALU.mult, op1=ALU.mult), waits=[t6])
            tu = p.do("dve", lambda e, xb=xb, ub=ub, st=st: e.tensor_scalar(
                out=ub, in0=xb, scalar1=st[:, 8:9],
                scalar2=st[:, 9:10], op0=ALU.mult, op1=ALU.add),
                waits=[t7, u_war.get(par, ("pe", 0))])
            x_war[par] = tu
            tur = p.do("dve", lambda e, ub=ub: e.tensor_tensor(
                out=ub, in0=ub, in1=s_rmsb[:], op=ALU.mult),
                waits=[tu] + ([trb] if tc == 0 else []))
            u_war[par] = self.transpose_tiles(
                lambda i, ub=ub: ub[:, i * 128:(i + 1) * 128],
                lambda i0, cnt, tc=tc: bass.AP(s_ufm, i0 * T + tc * 128,
                                               [[EC * T, 128], [T, cnt], [1, 128]]),
                EC, tur, batch=8)
            bank = tc % 2
            w0 = [u_war[par], wtoks[0], last_zev.get(bank, ("pe", 0))]
            for ec in range(EC):
                p.do("pe", lambda e, tc=tc, ec=ec, bank=bank: e.matmul(
                    self.PSF[:, bank * 512:(bank + 1) * 512],
                    s_ufm[:, ec * T + tc * 128: ec * T + (tc + 1) * 128],
                    s_w[:, ec * 512:(ec + 1) * 512],
                    start=(ec == 0), stop=(ec == EC - 1)),
                    waits=(w0 if ec == 0 else []), inc=(ec == EC - 1))
            slab_war[0] = p.tok("pe")
            tz = p.do("act", lambda e, bank=bank: e.activation(
                out=s_zev[:, bank * 512:(bank + 1) * 512],
                in_=self.PSF[:, bank * 512:(bank + 1) * 512],
                func=AF.Copy, scale=1.0), waits=[p.tok("pe")])
            last_zev[bank] = p.dma(
                "sp", zr[:, tc:tc + 1, 0:512],
                s_zev[:, bank * 512: bank * 512 + 512].rearrange(
                    "p (a n) -> p a n", a=1), waits=[tz])
        tuf = p.tok("dve")

        p.do("pool", lambda e: e.memset(s_xbc[:, 0:3], 0.0), waits=[tuf])
        for sl in range(1, 17):
            if sl + 1 < 17:
                wtoks[sl + 1] = load_slab(sl + 1, [slab_war.get((sl + 1) % 2, ("pe", 0))])
            buf = (sl % 2) * EC * 512
            if sl < 8:
                for tc in range(NTC):
                    bank = tc % 2
                    w0 = [tuf, wtoks[sl], last_zev.get(bank, ("pe", 0))]
                    for ec in range(EC):
                        p.do("pe", lambda e, tc=tc, ec=ec, bank=bank, buf=buf: e.matmul(
                            self.PSF[:, bank * 512:(bank + 1) * 512],
                            s_ufm[:, ec * T + tc * 128: ec * T + (tc + 1) * 128],
                            s_w[:, buf + ec * 512: buf + (ec + 1) * 512],
                            start=(ec == 0), stop=(ec == EC - 1)),
                            waits=(w0 if ec == 0 else []), inc=(ec == EC - 1))
                    slab_war[sl % 2] = p.tok("pe")
                    tz = p.do("act", lambda e, bank=bank: e.activation(
                        out=s_zev[:, bank * 512:(bank + 1) * 512],
                        in_=self.PSF[:, bank * 512:(bank + 1) * 512],
                        func=AF.Copy, scale=1.0), waits=[p.tok("pe")])
                    last_zev[bank] = p.dma(
                        "sp", zr[:, tc:tc + 1, sl * 512:(sl + 1) * 512],
                        s_zev[:, bank * 512: bank * 512 + 512].rearrange(
                            "p (a n) -> p a n", a=1), waits=[tz])
            else:
                ncc = 4 if sl < 16 else 1
                for ccl in range(ncc):
                    cc = (sl - 8) * 4 + ccl
                    w0 = [tuf, wtoks[sl], cc_done.get("psum", ("pe", 0)),
                          cc_done.get("psum_d", ("pe", 0))]
                    if cc == 0:
                        w0.append(p.tok("act"))
                    for st4 in range(4):
                        for ec in range(EC):
                            p.do("pe", lambda e, ccl=ccl, st4=st4, ec=ec, buf=buf: e.matmul(
                                self.PSF[:, st4 * 512:(st4 + 1) * 512],
                                s_w[:, buf + ec * 512 + ccl * 128: buf + ec * 512 + (ccl + 1) * 128],
                                s_ufm[:, ec * T + st4 * 512: ec * T + (st4 + 1) * 512],
                                start=(ec == 0), stop=(ec == EC - 1)),
                                waits=(w0 if (st4 == 0 and ec == 0) else []),
                                inc=(ec == EC - 1 and st4 == 3))
                    slab_war[sl % 2] = p.tok("pe")
                    tmm = p.tok("pe")
                    tcp1 = p.do("dve", lambda e: e.tensor_copy(
                        s_xbc[:, 3:3 + 1024], self.PSF[:, 0:1024]), waits=[tmm])
                    tcp2 = p.do("act", lambda e: e.activation(
                        out=s_xbc[:, 3 + 1024:3 + 2048], in_=self.PSF[:, 1024:2048],
                        func=AF.Copy, scale=1.0), waits=[tmm])
                    cc_done["psum"] = tcp2
                    cc_done["psum_d"] = tcp1
                    tdv, tpl = [tcp1, tcp2], [tcp1, tcp2]
                    for k in range(4):
                        wcol = self.c_convw[:, cc * 4 + k: cc * 4 + k + 1]
                        if k == 0:
                            tdv = [p.do("dve", lambda e, k=k, wcol=wcol: e.tensor_scalar(
                                out=s_cv[:, 0:1280], in0=s_xbc[:, k:k + 1280],
                                scalar1=wcol, scalar2=None, op0=ALU.mult),
                                waits=tdv + [cc_done.get("cv", ("pe", 0))])]
                            tpl = [p.do("dve", lambda e, k=k, wcol=wcol: e.tensor_scalar(
                                out=s_cv[:, 1280:2048], in0=s_xbc[:, 1280 + k:2048 + k],
                                scalar1=wcol, scalar2=None, op0=ALU.mult),
                                waits=tpl + [cc_done.get("cv", ("pe", 0))])]
                        else:
                            tdv = [p.do("dve", lambda e, k=k, wcol=wcol: e.scalar_tensor_tensor(
                                out=s_cv[:, 0:1280], in0=s_xbc[:, k:k + 1280],
                                scalar=wcol, in1=s_cv[:, 0:1280],
                                op0=ALU.mult, op1=ALU.add), waits=tdv)]
                            tpl = [p.do("dve", lambda e, k=k, wcol=wcol: e.scalar_tensor_tensor(
                                out=s_cv[:, 1280:2048], in0=s_xbc[:, 1280 + k:2048 + k],
                                scalar=wcol, in1=s_cv[:, 1280:2048],
                                op0=ALU.mult, op1=ALU.add), waits=tpl)]
                    tsil = p.do("act", lambda e, cc=cc: e.activation(
                        out=s_cvb[:], in_=s_cv[:], func=AF.Silu,
                        bias=self.c_vec[:, 152 + cc:153 + cc], scale=1.0),
                        waits=tdv + tpl)
                    cc_done["cv"] = tsil
                    if cc < 32:
                        for g in range(2):
                            half = self._tr_war.get("half", 0) % 2
                            self._tr_war["half"] = half + 1
                            psb = self.PBF[:, half * 1024: half * 1024 + 1024]
                            w0t = [tsil] + ([self._tr_war[half]] if half in self._tr_war else [])
                            for k in range(8):
                                p.do("pe", lambda e, g=g, k=k, psb=psb: e.transpose(
                                    psb[:, k * 128:(k + 1) * 128],
                                    s_cvb[:, (g * 8 + k) * 128:(g * 8 + k + 1) * 128],
                                    self.c_ident[:]), waits=(w0t if k == 0 else []),
                                    inc=(k == 7))
                            tev = p.do("dve", lambda e, g=g, psb=psb: e.tensor_copy(
                                s_xtev[:, g * 1024:(g + 1) * 1024], psb[:, 0:1024]),
                                waits=[p.tok("pe"), cc_done.get(("xt", g), ("pe", 0))])
                            self._tr_war[half] = tev
                            cc_done[("xt", g)] = p.dma(
                                "sp",
                                self.XTd.rearrange("(tc p) n -> p tc n", p=128)[
                                    :, g * 8:(g + 1) * 8, cc * 128:(cc + 1) * 128],
                                s_xtev[:, g * 1024:(g + 1) * 1024].rearrange(
                                    "p (a n) -> p a n", a=8), waits=[tev])
                    else:
                        tbc = p.do("dve", lambda e: e.tensor_copy(self.s_bc[:], s_cvb[:]),
                                   waits=[tsil])
                        tcd = p.dma("pool", self.s_cfm[0:64, :], self.s_bc[64:128, :],
                                    waits=[tbc])
                        for c in range(NCH):
                            half = self._tr_war.get("half", 0) % 2
                            self._tr_war["half"] = half + 1
                            psb = self.PBF[:, half * 1024: half * 1024 + 128]
                            w0t = [tbc] + ([self._tr_war[half]] if half in self._tr_war else [])
                            p.do("pe", lambda e, c=c, psb=psb: e.transpose(
                                psb[0:128, 0:64],
                                self.s_bc[0:64, c * 128:(c + 1) * 128],
                                self.c_ident[0:64, 0:64]), waits=w0t)
                            tev = p.do("dve", lambda e, c=c, psb=psb: e.tensor_copy(
                                self.s_btm[:, c * 64:(c + 1) * 64], psb[0:128, 0:64]),
                                waits=[p.tok("pe")])
                            self._tr_war[half] = tev
        buf = (16 % 2) * EC * 512
        for st4 in range(4):
            for ec in range(EC):
                p.do("pe", lambda e, st4=st4, ec=ec, buf=buf: e.matmul(
                    self.PSF[0:64, st4 * 512:(st4 + 1) * 512][:, 0:512],
                    s_w[:, buf + ec * 512 + 128: buf + ec * 512 + 192],
                    s_ufm[:, ec * T + st4 * 512: ec * T + (st4 + 1) * 512],
                    start=(ec == 0), stop=(ec == EC - 1)),
                    waits=([wtoks[16], cc_done.get("psum", ("pe", 0)),
                            p.tok("dve"), p.tok("act"), p.tok("pool")]
                           if (st4 == 0 and ec == 0) else []),
                    inc=(ec == EC - 1 and st4 == 3))
        tdt = p.tok("pe")
        for st4 in range(4):
            p.do("dve", lambda e, st4=st4: e.tensor_copy(
                s_dtmp[0:64, st4 * 512:(st4 + 1) * 512],
                self.PSF[0:64, st4 * 512:(st4 + 1) * 512]),
                waits=[tdt] if st4 == 0 else [])
        tev = p.do("dve", lambda e: e.tensor_scalar(
            out=s_dtmp[0:64, :], in0=s_dtmp[0:64, :],
            scalar1=self.c_small[0:NH, 0:1], scalar2=None, op0=ALU.add),
            waits=[p.tok("dve")])
        tab = p.do("act", lambda e: e.activation(
            out=self.s_dt[0:64, :], in_=s_dtmp[0:64, :], func=AF.Abs, scale=1.0),
            waits=[tev])
        te1 = p.do("act", lambda e: e.activation(
            out=self.s_s[0:64, :], in_=self.s_dt[0:64, :], func=AF.Exp, scale=-1.0),
            waits=[tab])
        te2 = p.do("act", lambda e: e.activation(
            out=self.s_s[0:64, :], in_=self.s_s[0:64, :], func=AF.Ln, bias=1.0,
            scale=1.0), waits=[te1])
        tr1 = p.do("dve", lambda e: e.tensor_scalar(
            out=s_dtmp[0:64, :], in0=s_dtmp[0:64, :], scalar1=0.0, scalar2=None,
            op0=ALU.max), waits=[te1])
        tsp = p.do("dve", lambda e: e.tensor_tensor(
            out=self.s_dt[0:64, :], in0=s_dtmp[0:64, :], in1=self.s_s[0:64, :],
            op=ALU.add), waits=[te2, tr1])
        tL = p.do("dve", lambda e: e.tensor_scalar(
            out=s_dtmp[0:64, :], in0=self.s_dt[0:64, :],
            scalar1=self.c_small[0:NH, 2:3], scalar2=None, op0=ALU.mult),
            waits=[tsp, self.tok_A])
        tsc = tL
        for c in range(NCH):
            tsc = p.do("dve", lambda e, c=c: e.tensor_tensor_scan(
                out=self.s_s[0:64, c * 128:(c + 1) * 128],
                data0=s_dtmp[0:64, c * 128:(c + 1) * 128],
                data1=self.s_z64[0:64, :], initial=0.0,
                op0=ALU.add, op1=ALU.add), waits=[tsc])
        tw = []
        for c in range(NCH):
            t1 = p.do("act", lambda e, c=c: e.activation(
                out=s_dtmp[0:64, c * 128:(c + 1) * 128],
                in_=self.s_s[0:64, c * 128:(c + 1) * 128], func=AF.Exp, scale=-1.0,
                bias=self.s_s[0:64, c * 128 + 127: c * 128 + 128]), waits=[tsc])
            tw.append(t1)
        tw2 = p.do("dve", lambda e: e.tensor_tensor(
            out=s_dtmp[0:64, :], in0=s_dtmp[0:64, :], in1=self.s_dt[0:64, :],
            op=ALU.mult), waits=tw)
        for (srcbuf, dstbuf, wtok) in [(self.s_s, self.s_stm, tsc),
                                       (self.s_dt, self.s_dttm, tsp),
                                       (s_dtmp, self.s_wtm, tw2)]:
            for c in range(NCH):
                tp = p.do("pe", lambda e, c=c, srcbuf=srcbuf: e.transpose(
                    self.PSF[0:128, 2048:2048 + 64],
                    srcbuf[0:64, c * 128:(c + 1) * 128], self.c_identf[0:64, 0:64]),
                    waits=[wtok, p.tok("dve")])
                p.do("dve", lambda e, c=c, dstbuf=dstbuf: e.tensor_copy(
                    dstbuf[:, c * 64:(c + 1) * 64], self.PSF[0:128, 2048:2048 + 64]),
                    waits=[tp])
        p.do("act", lambda e: e.activation(
            out=self.s_estm[:], in_=self.s_stm[:], func=AF.Exp, scale=1.0),
            waits=[p.tok("dve")])
        p.do("dve", lambda e: e.tensor_scalar(
            out=self.s_nstm[:], in0=self.s_stm[:], scalar1=-1.0, scalar2=None,
            op0=ALU.mult), waits=[p.tok("dve")])
        sb.reset(base)
        p.barrier()

    def phase4b(self):
        sb, p = self.sb, self.p
        base = sb.mark()
        s_xt = sb.alloc("b_xt", [128, 2 * DI], BF16)
        s_srow = sb.alloc("b_srow", [1, 2 * NH * 128], F32)
        s_y = sb.alloc("b_y", [128, 2 * DI], BF16)
        s_zc = sb.alloc("b_zc", [128, DI], BF16)
        s_mt = sb.alloc("b_mt", [128, 2 * 512], BF16)
        s_ed = sb.alloc("b_ed", [128, 2 * 512], BF16)
        s_yb = sb.alloc("b_yb", [128, 256], F32)
        s_t1 = sb.alloc("b_t1", [128, 256], F32)
        s_t2 = sb.alloc("b_t2", [128, 256], F32)
        s_st = sb.alloc("b_st", [64, 256], F32)
        s_esq = sb.alloc("b_esq", [64, 4], F32)
        s_esqr = sb.alloc("b_esqr", [1, 64], F32)
        s_decay = sb.alloc("b_decay", [64, 64], F32)
        ps_G = self.PSF[:, 2048:2176]
        ps_esq = self.PSF[:, 2560:2624]
        xt_r = self.XTd.rearrange("(tc p) n -> p tc n", p=128)

        tz = p.do("pool", lambda e: e.memset(self.s_state[:], 0.0))
        tz = p.do("pool", lambda e: e.memset(self.s_stateb[:], 0.0), waits=[tz])
        yr = self.Yd.rearrange("(tc p) n -> p tc n", p=128)
        zr4 = self.Zd.rearrange("(tc p) n -> p tc n", p=128)
        zc_tok = {0: p.dma("pool", s_zc[:], zr4[:, 0:1, :].rearrange("p a n -> p (a n)"))}
        xt_tok = {0: p.dma("sp", s_xt[:, 0:DI], xt_r[:, 0:1, :].rearrange("p a n -> p (a n)"))}
        last_y_dma = ("pe", 0)
        ywar = {"t": ("pe", 0)}
        st_evac = [tz]
        xt_mm = {}
        srow_mm = {}
        for c in range(NCH):
            if c + 1 < NCH:
                xt_tok[c + 1] = p.dma(
                    "sp", s_xt[:, ((c + 1) % 2) * DI: ((c + 1) % 2 + 1) * DI],
                    xt_r[:, c + 1:c + 2, :].rearrange("p a n -> p (a n)"),
                    waits=[("pe", 0) if c == 0 else xt_mm.get((c + 1) % 2, ("pe", 0))])
            xbuf = (c % 2) * DI
            srb = (c % 2) * NH * 128
            if c == 0:
                srow_tok = {0: p.dma("pool", s_srow[0:1, 0:NH * 128],
                                     self.s_s[0:64, 0:128], waits=[p.tok("pe")])}
            if c + 1 < NCH:
                srow_tok[c + 1] = p.dma(
                    "pool", s_srow[0:1, ((c + 1) % 2) * NH * 128: ((c + 1) % 2 + 1) * NH * 128],
                    self.s_s[0:64, (c + 1) * 128:(c + 2) * 128],
                    waits=[srow_mm.get((c + 1) % 2, ("pe", 0))])
            tst = srow_tok[c]
            te0 = p.do("act", lambda e, c=c: e.activation(
                out=s_esq[0:64, 0:1], in_=self.s_s[0:64, c * 128 + 127: c * 128 + 128],
                func=AF.Exp, scale=1.0), waits=[p.tok("dve")])
            tt0 = p.do("pe", lambda e: e.transpose(
                ps_esq[0:1, 0:64], s_esq[0:64, 0:1], self.c_identf[0:64, 0:64]),
                waits=[te0, st_evac[-1]])
            tt1 = p.do("dve", lambda e: e.tensor_copy(s_esqr[0:1, 0:64],
                                                      ps_esq[0:1, 0:64]), waits=[tt0])
            tt2 = p.do("pe", lambda e: e.matmul(
                self.PSF[0:64, 2560:2624], self.c_ones[0:1, 0:64], s_esqr[0:1, 0:64],
                start=True, stop=True), waits=[tt1])
            tdec = p.do("dve", lambda e: e.tensor_copy(s_decay[:], self.PSF[0:64, 2560:2624]),
                        waits=[tt2])
            tG = p.do("pe", lambda e, c=c: e.matmul(
                ps_G, self.s_bc[0:64, c * 128:(c + 1) * 128],
                self.s_cfm[0:64, c * 128:(c + 1) * 128], start=True, stop=True),
                waits=[p.tok("dve")])
            pend = None
            bank_tok = {}
            def stage_b(hg, par, tm2):
                for hh in range(4):
                    h = hg * 4 + hh
                    w_in = [tm2, xt_tok[c]] + [bank_tok.get("y", ("pe", 0))]
                    p.do("pe", lambda e, h=h, hh=hh, par=par, xbuf=xbuf: e.matmul(
                        self.PSF[:, 1024 + hh * 64: 1024 + (hh + 1) * 64],
                        s_mt[:, par * 512 + hh * 128: par * 512 + (hh + 1) * 128],
                        s_xt[:, xbuf + h * 64: xbuf + (h + 1) * 64],
                        start=True, stop=True),
                        waits=(w_in if hh == 0 else []), inc=False)
                    p.do("pe", lambda e, h=h, hh=hh, c=c: e.matmul(
                        self.PSF[:, 1536 + hh * 64: 1536 + (hh + 1) * 64],
                        self.s_cfm[0:64, c * 128:(c + 1) * 128],
                        self.s_stateb[0:64, h * 64:(h + 1) * 64],
                        start=True, stop=True),
                        waits=(st_evac[-1:] if hh == 0 else []), inc=(hh == 3))
                btm_rep = bass.AP(self.s_btm, c * 64,
                                  [[NCH * 64, 128], [0, 4], [1, 64]])
                wtm_b = bass.AP(self.s_wtm, c * 64 + hg * 4,
                                [[NCH * 64, 128], [1, 4], [0, 64]])
                tb2 = p.do("dve", lambda e, btm_rep=btm_rep, wtm_b=wtm_b, par=par: e.tensor_tensor(
                    out=s_ed[:, par * 512:par * 512 + 256], in0=btm_rep, in1=wtm_b,
                    op=ALU.mult), waits=[tm2])
                for hh in range(4):
                    h = hg * 4 + hh
                    p.do("pe", lambda e, hh=hh, h=h, par=par, xbuf=xbuf: e.matmul(
                        self.PSF[0:64, 2624 + hh * 64: 2624 + (hh + 1) * 64],
                        s_ed[:, par * 512 + hh * 64: par * 512 + hh * 64 + 64],
                        s_xt[:, xbuf + h * 64: xbuf + (h + 1) * 64],
                        start=True, stop=True),
                        waits=([tb2, tdec, st_evac[-1]] if hh == 0 else []),
                        inc=(hh == 3))
                tpe_all = p.tok("pe")
                esb = bass.AP(self.s_estm, c * 64 + hg * 4,
                              [[NCH * 64, 128], [1, 4], [0, 64]])
                ty1 = p.do("dve", lambda e, esb=esb: e.tensor_tensor(
                    out=s_yb[:], in0=self.PSF[:, 1536:1792], in1=esb, op=ALU.mult),
                    waits=[tpe_all])
                dbc = bass.AP(self.c_dbc, hg * 4, [[NH, 128], [1, 4], [0, 64]])
                ty2 = p.do("dve", lambda e, dbc=dbc, xbuf=xbuf, hg=hg: e.tensor_tensor(
                    out=s_t1[:], in0=s_xt[:, xbuf + hg * 256: xbuf + (hg + 1) * 256],
                    in1=dbc, op=ALU.mult), waits=[tpe_all])
                ty3 = p.do("dve", lambda e: e.tensor_tensor(
                    out=s_t2[:], in0=s_t1[:], in1=self.PSF[:, 1024:1280], op=ALU.add),
                    waits=[ty2])
                ybase = (c % 2) * DI
                p.do("dve", lambda e, hg=hg, ybase=ybase: e.tensor_tensor(
                    out=s_y[:, ybase + hg * 256: ybase + (hg + 1) * 256],
                    in0=s_t2[:], in1=s_yb[:], op=ALU.add),
                    waits=[ty3, ty1] + ([ywar.get(c % 2, ("pe", 0))] if hg == 0 else []))
                bank_tok["y"] = ty3
                dcb = bass.AP(s_decay, hg * 4, [[64, 64], [1, 4], [0, 64]])
                ts1 = p.do("dve", lambda e, dcb=dcb, hg=hg: e.tensor_tensor(
                    out=s_st[0:64, :], in0=self.s_state[0:64, hg * 256:(hg + 1) * 256],
                    in1=dcb, op=ALU.mult), waits=[tdec, tpe_all])
                ts2 = p.do("dve", lambda e, hg=hg: e.tensor_tensor(
                    out=self.s_state[0:64, hg * 256:(hg + 1) * 256], in0=s_st[0:64, :],
                    in1=self.PSF[0:64, 2624:2880], op=ALU.add), waits=[ts1])
                ts3 = p.do("dve", lambda e, hg=hg: e.tensor_copy(
                    self.s_stateb[0:64, hg * 256:(hg + 1) * 256],
                    self.s_state[0:64, hg * 256:(hg + 1) * 256]), waits=[ts2])
                st_evac.append(ts3)

            for hg in range(NH // 4):
                bank = hg % 2
                par = hg % 2
                psD = self.PSF[:, bank * 512:(bank + 1) * 512]
                w0 = [tst, p.tok("act")]
                p.do("pe", lambda e, hg=hg, psD=psD, srb=srb: e.matmul(
                    psD, self.c_ones[0:1, :], s_srow[0:1, srb + hg * 512: srb + (hg + 1) * 512],
                    start=True, stop=False), waits=w0, inc=False)
                for hh in range(4):
                    h = hg * 4 + hh
                    p.do("pe", lambda e, h=h, hh=hh, psD=psD, srb=srb: e.matmul(
                        psD[:, hh * 128:(hh + 1) * 128],
                        s_srow[0:1, srb + h * 128: srb + (h + 1) * 128],
                        self.c_negones[0:1, :], start=False, stop=False), inc=False)
                p.do("pe", lambda e, psD=psD: e.matmul(
                    psD, self.c_ident[:], self.c_mmask[:], start=False, stop=True),
                    inc=True)
                tD = p.tok("pe")
                ted = p.do("act", lambda e, par=par, psD=psD: e.activation(
                    out=s_ed[:, par * 512:(par + 1) * 512], in_=psD, func=AF.Exp),
                    waits=[tD])
                dtb = bass.AP(self.s_dttm, c * 64 + hg * 4,
                              [[NCH * 64, 128], [1, 4], [0, 128]])
                tm1 = p.do("dve", lambda e, dtb=dtb, par=par: e.tensor_tensor(
                    out=s_mt[:, par * 512:(par + 1) * 512],
                    in0=s_ed[:, par * 512:(par + 1) * 512], in1=dtb, op=ALU.mult),
                    waits=[ted, p.tok("pe")])
                gb = bass.AP(self.PSF, 2048, [[3072, 128], [0, 4], [1, 128]])
                tm2 = p.do("dve", lambda e, gb=gb, par=par: e.tensor_tensor(
                    out=s_mt[:, par * 512:(par + 1) * 512],
                    in0=s_mt[:, par * 512:(par + 1) * 512], in1=gb, op=ALU.mult),
                    waits=[tm1, tG])
                if pend is not None:
                    stage_b(*pend)
                pend = (hg, par, tm2)
            stage_b(*pend)
            tyd = p.tok("dve")
            xt_mm[c % 2] = p.tok("pe")
            srow_mm[c % 2] = p.tok("pe")
            yb = s_y[:, (c % 2) * DI: (c % 2 + 1) * DI]
            tsl = p.do("act", lambda e: e.activation(
                out=s_zc[:], in_=s_zc[:], func=AF.Silu, scale=1.0),
                waits=[zc_tok[c]])
            tys = p.do("dve", lambda e, yb=yb: e.tensor_tensor(
                out=yb, in0=yb, in1=s_zc[:], op=ALU.mult),
                waits=[tyd, tsl])
            tss = p.do("act", lambda e, c=c, yb=yb: e.activation(
                out=s_zc[:], in_=yb, func=AF.Square,
                accum_out=self.c_vec[:, 201 + c:202 + c]), waits=[tys])
            if c + 1 < NCH:
                zc_tok[c + 1] = p.dma(
                    "pool", s_zc[:],
                    zr4[:, c + 1:c + 2, :].rearrange("p a n -> p (a n)"),
                    waits=[tss])
            last_y_dma = p.dma("sp", yr[:, c:c + 1, :].rearrange("p a n -> p (a n)"),
                               yb, waits=[tss, last_y_dma])
            ywar[c % 2] = last_y_dma
        sb.reset(base)
        p.barrier()

    def phase4c(self):
        sb, p = self.sb, self.p
        sb.reset(self.pers_end)
        self.s_opt = sb.alloc("d_opt", [128, 32 * E], BF16)
        opt_r = self.outpT.rearrange("(c p) n -> p c n", p=128)
        ov = self.s_opt[:].rearrange("p (c n) -> p c n", c=32)
        for q in range(4):
            tw = p.dma("pool", ov[:, q * 8:(q + 1) * 8, :], opt_r[:, q * 8:(q + 1) * 8, :])
        twf = tw
        for cfold in range(32):
            twf = p.do("dve", lambda e, cfold=cfold: e.tensor_scalar(
                out=self.s_opt[:, cfold * E:(cfold + 1) * E],
                in0=self.s_opt[:, cfold * E:(cfold + 1) * E],
                scalar1=self.c_vec[:, 120 + cfold:121 + cfold], scalar2=None,
                op0=ALU.mult), waits=([tw] if cfold == 0 else []))
        self.opt_tok = twf
        base = sb.mark()
        s_opt = self.s_opt
        s_y = sb.alloc("c_y", [128, DI], BF16)
        s_ys = sb.alloc("c_ys", [128, 2 * DI], BF16)
        s_rst = sb.alloc("c_rst", [128, NTC], F32)
        s_ynt = sb.alloc("d_ynt", [128, 2 * 32 * 128], BF16)
        s_x2 = sb.alloc("d_x2", [128, E], F32)
        s_x3 = sb.alloc("d_x3", [128, E], F32)
        yr = self.Yd.rearrange("(tc p) n -> p tc n", p=128)
        tq = p.do("act", lambda e: e.activation(
            out=s_rst[:], in_=self.c_vec[:, 201:201 + NTC], func=AF.Sqrt,
            bias=RMSEPS_B, scale=1.0 / DI), waits=[p.tok("act")])
        tr = p.do("dve", lambda e: e.reciprocal(out=s_rst[:], in_=s_rst[:]), waits=[tq])
        last_ev = {}
        yn_mm = {}
        ys_war = {}
        y_war = ("pe", 0)
        for tc in range(NTC):
            par = tc % 2
            buf = par * 32 * 128
            ysb = s_ys[:, par * DI:(par + 1) * DI]
            ty = p.dma("sp", s_y[:], yr[:, tc:tc + 1, :].rearrange("p a n -> p (a n)"),
                       waits=[y_war])
            tyn = p.do("dve", lambda e, tc=tc, ysb=ysb: e.tensor_scalar(
                out=ysb, in0=s_y[:], scalar1=s_rst[:, tc:tc + 1], scalar2=None,
                op0=ALU.mult), waits=[ty, tr, ys_war.get(par, ("pe", 0))])
            y_war = tyn
            for g in range(4):
                half = self._tr_war.get("half", 0) % 2
                self._tr_war["half"] = half + 1
                psb = self.PBF[:, half * 1024: half * 1024 + 1024]
                w0 = [tyn] + ([self._tr_war[half]] if half in self._tr_war else [])
                for k in range(8):
                    p.do("pe", lambda e, g=g, k=k, psb=psb, ysb=ysb: e.transpose(
                        psb[:, k * 128:(k + 1) * 128],
                        ysb[:, (g * 8 + k) * 128:(g * 8 + k + 1) * 128],
                        self.c_ident[:]), waits=(w0 if k == 0 else []), inc=(k == 7))
                tev = p.do("dve", lambda e, g=g, psb=psb, buf=buf: e.tensor_copy(
                    s_ynt[:, buf + g * 1024: buf + (g + 1) * 1024], psb[:, 0:1024]),
                    waits=[p.tok("pe"), yn_mm.get(par, ("pe", 0))])
                self._tr_war[half] = tev
            tyn_done = p.tok("dve")
            ys_war[par] = tyn_done
            tx = p.dma("sp", s_x2[:], self.X2[tc * 128:(tc + 1) * 128, :],
                       waits=[p.tok("dve")])
            for eseg in range(4):
                bank = eseg % 2
                w0 = [tyn_done, twf, tx, last_ev.get(bank, ("pe", 0))]
                for xc in range(32):
                    p.do("pe", lambda e, xc=xc, eseg=eseg, bank=bank, buf=buf: e.matmul(
                        self.PSF[:, bank * 512:(bank + 1) * 512],
                        s_ynt[:, buf + xc * 128: buf + (xc + 1) * 128],
                        s_opt[:, xc * E + eseg * 512: xc * E + (eseg + 1) * 512],
                        start=(xc == 0), stop=(xc == 31)),
                        waits=(w0 if xc == 0 else []), inc=(xc == 31))
                tmm = p.tok("pe")
                yn_mm[par] = tmm
                t1 = p.do("dve", lambda e, eseg=eseg, bank=bank, tc=tc: e.scalar_tensor_tensor(
                    out=s_x3[:, eseg * 512:(eseg + 1) * 512],
                    in0=s_x2[:, eseg * 512:(eseg + 1) * 512],
                    scalar=self.c_r2p1[:, tc:tc + 1],
                    in1=self.PSF[:, bank * 512:(bank + 1) * 512],
                    op0=ALU.mult, op1=ALU.add), waits=[tmm])
                t2 = p.do("dve", lambda e, eseg=eseg, tc=tc: e.tensor_scalar(
                    out=s_x3[:, eseg * 512:(eseg + 1) * 512],
                    in0=s_x3[:, eseg * 512:(eseg + 1) * 512],
                    scalar1=self.c_mr2[:, tc:tc + 1], scalar2=None, op0=ALU.add),
                    waits=[t1])
                last_ev[bank] = t2
            tevd = p.tok("dve")
            p.dma("sp", self.X3[tc * 128:(tc + 1) * 128, :], s_x3[:], waits=[tevd])
            if "tap_x3" in self.tapd:
                p.dma("sp", self.tapd["tap_x3"][tc * 128:(tc + 1) * 128, :], s_x3[:],
                      waits=[tevd])
        sb.reset(self.pers_end)
        p.barrier()

    def tap_copy(self, name, src_ap):
        if name in self.tapd:
            t = p_split_dma(self.p, "sp", self.tapd[name][:, :], src_ap)
            return t
        return None

    def finish(self):
        self.p.barrier()


def p_split_dma(p, q, out_ap, in_ap, parts=2):
    n = out_ap.shape[0]
    tok = None
    step = (n + parts - 1) // parts
    for i in range(0, n, step):
        j = min(n, i + step)
        tok = p.dma(q, out_ap[i:j], in_ap[i:j])
    return tok


def build_mk(taps=(), phases=("p1a", "p1b", "p2", "p2b", "p3", "p4", "p5")):
    mk = MK(taps=taps)
    if "p1a" in phases:
        mk.phase1a()
    if "p1b" in phases:
        mk.phase1b()
    if "p2" in phases:
        mk.phase2()
    if "p2b" in phases:
        mk.phase2b()
    if "p3" in phases:
        mk.phase_ffn(mk.X1, mk.X2, tap="tap_x2")
    if "p4" in phases:
        mk.phase4a()
        mk.tap_copy("tap_z", mk.Zd)
        mk.tap_copy("tap_xt", mk.XTd)
        if "tap_dt" in mk.tapd:
            mk.p.dma("sp", mk.tapd["tap_dt"][0:NH, :], mk.s_dt[0:64, :])
        mk.phase4b()
        mk.tap_copy("tap_y", mk.Yd)
        mk.phase4c()
    if "p5" in phases:
        mk.phase_ffn(mk.X3, mk.out, tap=None)
    mk.tap_copy("tap_qn", mk.QN)
    mk.tap_copy("tap_qp", mk.QP)
    mk.tap_copy("tap_v", mk.Vd)
    if "tap_kn" in mk.tapd:
        mk.p.dma("sp", mk.tapd["tap_kn"].rearrange("(c p) t -> p c t", p=128),
                 mk.s_kn[:].rearrange("p (c t) -> p c t", c=H))
    if "tap_kp" in mk.tapd:
        mk.p.dma("sp", mk.tapd["tap_kp"][:, :], mk.s_kp[0:64, :])
    mk.finish()
    mk.p.emit()
    return mk


def kernel(**inputs):
    shared = prep_inputs(inputs)
    x = np.asarray(inputs["L0_attn_a_x"], np.float32)
    mk = build_mk()
    in_maps = []
    for b in range(8):
        m = dict(shared)
        m["x"] = np.ascontiguousarray(x[b])
        in_maps.append(m)
    res = run_bass_kernel_spmd(mk.nc, in_maps, core_ids=list(range(8)))
    return np.stack([r["out"] for r in res.results], axis=0)



# version 21
# speedup vs baseline: 1.0492x; 1.0199x over previous
import math
import numpy as np
import ml_dtypes

import concourse.bass as bass
import concourse.mybir as mybir
from concourse.bass_utils import run_bass_kernel_spmd

F32 = mybir.dt.float32
BF16 = mybir.dt.bfloat16
I32 = mybir.dt.int32
AF = mybir.ActivationFunctionType
ALU = mybir.AluOpType
AX = mybir.AxisListType
bfdt = ml_dtypes.bfloat16

T = 2048
E = 2048
EC = E // 128
NSTRIP = T // 512
NTC = T // 128
H = 16
QL, KL = 1536, 512
DN, DR, DV = 128, 64, 128
HID = 5632
HC = HID // 128
DI = 4096
DST = 64
NH = 64
HD = 64
CONV = 4224
NCH = 16
LNEPS = 1e-5
RMSEPS_A = 1e-6
RMSEPS_B = 1e-5
NEG = -1e30
SCALE_ATTN = 1.0 / math.sqrt(DN + DR)
FCONSTS = [1e-5, 1e-6, 1.0, 0.0, -1.0, 0.5, 2.0, -0.5]


class Prog:
    ENGS = ("pe", "act", "dve", "pool", "sp")
    QSUBS = 6

    def __init__(self, nc):
        self.nc = nc
        self.ops = []
        self.cnt = {e: 0 for e in self.ENGS}
        self.qidx = {}
        for q in ("sp", "pool", "act"):
            self.qidx[q] = 0
            for i in range(self.QSUBS):
                self.cnt[f"d_{q}_{i}"] = 0

    def _fix(self, waits):
        return tuple((s, self.cnt[s]) if s.startswith("d_") else (s, v)
                     for (s, v) in waits)

    def do(self, eng, fn, waits=(), inc=True):
        incs = []
        if inc:
            self.cnt[eng] += 1
            incs.append((eng, 1))
        self.ops.append((eng, fn, self._fix(waits), tuple(incs)))
        return (eng, self.cnt[eng])

    def wait(self, eng, waits):
        if waits:
            self.ops.append((eng, None, self._fix(waits), ()))

    def dma(self, q, out, in_, waits=(), **kw):
        fixed = self._fix(waits)
        sem = f"d_{q}_{self.qidx[q] % self.QSUBS}"
        self.qidx[q] += 1
        self.cnt[sem] += 16
        def fn(e, out=out, in_=in_, kw=kw):
            return e.dma_start(out=out, in_=in_, **kw)
        self.ops.append((q if q != "sp" else "sp", fn, fixed, ((sem, 16),)))
        return (sem, self.cnt[sem])

    def tok(self, key):
        return (key, self.cnt[key])

    def barrier(self):
        snap = tuple((s, v) for s, v in self.cnt.items() if v > 0)
        for e in self.ENGS:
            self.ops.append((e, None, snap, ()))

    def emit(self):
        nc = self.nc
        sems = {name: nc.alloc_semaphore(name) for name in self.cnt}
        attr = {"pe": "tensor", "act": "scalar", "dve": "vector",
                "pool": "gpsimd", "sp": "sync"}
        with nc.Block() as block:
            for key in self.ENGS:
                def make(key):
                    def run(eng):
                        seen = {}
                        for (e, fn, waits, incs) in self.ops:
                            if e != key:
                                continue
                            for (s, v) in waits:
                                if seen.get(s, 0) < v:
                                    eng.wait_ge(sems[s], v)
                                    seen[s] = v
                            if fn is None:
                                continue
                            inst = fn(eng)
                            for (s, n) in incs:
                                inst.then_inc(sems[s], n)
                    return run
                getattr(block, attr[key])(make(key))
        return sems


class SB:
    BASE = 16512
    TOP = 229344

    def __init__(self, nc):
        self.nc = nc
        self.off = self.BASE

    def alloc(self, name, shape, dtype, off=None):
        sz = int(np.prod(shape[1:])) * mybir.dt.size(dtype)
        if off is None:
            off = self.off
            self.off = off + ((sz + 31) // 32) * 32
        assert off + sz <= self.TOP, f"SBUF overflow {name}: {off}+{sz}"
        return self.nc.alloc_sbuf_tensor_at(name, shape, dtype, offset=off)

    def mark(self):
        return self.off

    def reset(self, off):
        self.off = off


def _rope_tables():
    inv_freq = 1.0 / 10000.0 ** (np.arange(0, DR, 2, dtype=np.float32) / DR)
    t = np.arange(T, dtype=np.float32)
    freqs = np.outer(t, inv_freq)
    return np.cos(freqs).astype(np.float32), np.sin(freqs).astype(np.float32)


def _pe_perm():
    p = np.empty(64, dtype=np.int64)
    p[:32] = 2 * np.arange(32)
    p[32:] = 2 * np.arange(32) + 1
    return p


def prep_inputs(kw):
    f32 = np.float32
    pe = _pe_perm()
    d = {}
    d["wqa"] = np.ascontiguousarray(kw["L0_attn_a_w_q_a"], dtype=f32)
    wqb = np.asarray(kw["L0_attn_a_w_q_b"], dtype=f32)
    cols = []
    for h in range(H):
        cols.append(np.arange(h * 192, h * 192 + 128))
    for h in range(H):
        cols.append(h * 192 + 128 + pe)
    d["wqb"] = np.ascontiguousarray(wqb[:, np.concatenate(cols)])
    wkva = np.asarray(kw["L0_attn_a_w_kv_a"], dtype=f32)
    d["wkva"] = np.ascontiguousarray(
        np.concatenate([wkva[:, :KL], wkva[:, KL:][:, pe]], axis=1))
    wkvb = np.asarray(kw["L0_attn_a_w_kv_b"], dtype=f32)
    cols = [np.arange(h * 256, h * 256 + 128) for h in range(H)]
    cols += [np.arange(h * 256 + 128, h * 256 + 256) for h in range(H)]
    d["wkvb"] = np.ascontiguousarray(wkvb[:, np.concatenate(cols)])
    d["wout"] = np.ascontiguousarray(kw["L0_attn_a_w_out"], dtype=f32)
    d["qaln"] = np.ascontiguousarray(
        np.asarray(kw["L0_attn_a_q_a_ln"], f32).reshape(QL // 128, 128).T)
    d["kvln"] = np.ascontiguousarray(
        np.asarray(kw["L0_attn_a_kv_a_ln"], f32).reshape(KL // 128, 128).T)
    d["wg"] = np.ascontiguousarray(np.asarray(kw["L0_ffn_wq_gate"], np.int32).T)
    d["wu"] = np.ascontiguousarray(np.asarray(kw["L0_ffn_wq_up"], np.int32).T)
    d["wd"] = np.ascontiguousarray(np.asarray(kw["L0_ffn_wq_down"], np.int32).T)
    d["sg"] = np.ascontiguousarray(np.asarray(kw["L0_ffn_s_gate"], f32).reshape(HC, 128).T)
    d["su"] = np.ascontiguousarray(np.asarray(kw["L0_ffn_s_up"], f32).reshape(HC, 128).T)
    d["sd"] = np.ascontiguousarray(np.asarray(kw["L0_ffn_s_down"], f32).reshape(EC, 128).T)
    d["sdb"] = np.ascontiguousarray(np.broadcast_to(np.asarray(kw["L0_ffn_s_down"], f32), (128, E)))
    d["inpT"] = np.ascontiguousarray(np.asarray(kw["L0_attn_b_in_proj_w"], f32).T)
    d["outpT"] = np.ascontiguousarray(np.asarray(kw["L0_attn_b_out_proj_w"], f32).T)
    d["rmsw"] = np.ascontiguousarray(
        np.asarray(kw["L0_attn_b_rms_weight"], f32).reshape(EC, 128).T)
    d["rmsb"] = np.ascontiguousarray(
        np.broadcast_to(np.asarray(kw["L0_attn_b_rms_weight"], f32), (128, E)))
    d["normw"] = np.ascontiguousarray(
        np.asarray(kw["L0_attn_b_norm_w"], f32).reshape(DI // 128, 128).T)
    d["convw"] = np.ascontiguousarray(
        np.asarray(kw["L0_attn_b_conv_w"], f32).reshape(CONV // 128, 128, 4)
        .transpose(1, 0, 2).reshape(128, -1))
    d["convb"] = np.ascontiguousarray(
        np.asarray(kw["L0_attn_b_conv_b"], f32).reshape(CONV // 128, 128).T)
    d["dtb"] = np.ascontiguousarray(np.asarray(kw["L0_attn_b_dt_bias"], f32).reshape(NH, 1))
    d["alog"] = np.ascontiguousarray(np.asarray(kw["L0_attn_b_A_log"], f32).reshape(NH, 1))
    d["dbc"] = np.ascontiguousarray(
        np.broadcast_to(np.asarray(kw["L0_attn_b_D"], f32), (128, NH)))
    d["ident"] = np.eye(128, dtype=bfdt)
    d["identf"] = np.eye(128, dtype=f32)
    d["onesf"] = np.ones((1, 128), dtype=f32)
    d["negonesf"] = np.full((1, 128), -1.0, dtype=f32)
    d["fconst"] = np.ascontiguousarray(np.broadcast_to(np.array(FCONSTS, dtype=f32), (128, len(FCONSTS))))
    cos, sin = _rope_tables()
    d["costm"] = np.ascontiguousarray(cos.reshape(NTC, 128, 32).transpose(1, 0, 2).reshape(128, -1))
    d["sintm"] = np.ascontiguousarray(sin.reshape(NTC, 128, 32).transpose(1, 0, 2).reshape(128, -1))
    masks = np.zeros((4, 128, 512), dtype=np.float32)
    i = np.arange(128)[:, None]
    j = np.arange(512)[None, :]
    for v in range(4):
        masks[v] = np.where(j <= v * 128 + i, 0.0, NEG)
    d["masks"] = np.ascontiguousarray(masks.transpose(1, 0, 2).reshape(128, -1).astype(bfdt))
    mm = np.where(i <= j[:, :128], 0.0, NEG).astype(np.float32)
    d["mmask"] = np.ascontiguousarray(np.tile(mm, (1, 4)).astype(bfdt))
    return d


class MK:
    def __init__(self, taps=()):
        self.taps = set(taps)
        nc = bass.Bass("TRN2", target_bir_lowering=False)
        self.nc = nc
        self.p = Prog(nc)
        self.sb = SB(nc)
        dr = lambda n, s, t, k="ExternalInput": nc.dram_tensor(n, s, t, kind=k).ap()
        self.x_in = dr("x", [T, E], F32)
        self.wqa = dr("wqa", [E, QL], F32)
        self.wqb = dr("wqb", [QL, 3072], F32)
        self.wkva = dr("wkva", [E, 576], F32)
        self.wkvb = dr("wkvb", [KL, 4096], F32)
        self.wout = dr("wout", [E, E], F32)
        self.qaln = dr("qaln", [128, QL // 128], F32)
        self.kvln = dr("kvln", [128, KL // 128], F32)
        self.wg = dr("wg", [E, HID], I32)
        self.wu = dr("wu", [E, HID], I32)
        self.wd = dr("wd", [HID, E], I32)
        self.sg = dr("sg", [128, HC], F32)
        self.su = dr("su", [128, HC], F32)
        self.sd = dr("sd", [128, EC], F32)
        self.sdb = dr("sdb", [128, E], F32)
        self.inpT = dr("inpT", [E, 8384], F32)
        self.outpT = dr("outpT", [DI, E], F32)
        self.rmsw = dr("rmsw", [128, EC], F32)
        self.rmsb = dr("rmsb", [128, E], F32)
        self.normw = dr("normw", [128, DI // 128], F32)
        self.convw = dr("convw", [128, (CONV // 128) * 4], F32)
        self.convb = dr("convb", [128, CONV // 128], F32)
        self.dtb = dr("dtb", [NH, 1], F32)
        self.alog = dr("alog", [NH, 1], F32)
        self.dbc = dr("dbc", [128, NH], F32)
        self.ident = dr("ident", [128, 128], BF16)
        self.identf = dr("identf", [128, 128], F32)
        self.negonesf = dr("negonesf", [1, 128], F32)
        self.fconst = dr("fconst", [128, len(FCONSTS)], F32)
        self.onesf = dr("onesf", [1, 128], F32)
        self.costm = dr("costm", [128, NTC * 32], F32)
        self.sintm = dr("sintm", [128, NTC * 32], F32)
        self.masks = dr("masks", [128, 4 * 512], BF16)
        self.mmask = dr("mmask", [128, 512], BF16)
        self.out = dr("out", [T, E], F32, k="ExternalOutput")
        dri = lambda n, s, t: nc.dram_tensor(n, s, t, kind="Internal").ap()
        self.QN = dri("QN", [H * 128, T], BF16)
        self.QP = dri("QP", [H * 64, T], BF16)
        self.Vd = dri("Vd", [T, H * 128], BF16)
        self.OT = dri("OT", [H * 128, T], BF16)
        self.X1 = dri("X1", [T, E], F32)
        self.X2 = dri("X2", [T, E], F32)
        self.X3 = dri("X3", [T, E], F32)
        self.Zd = dri("Zd", [T, DI], BF16)
        self.XTd = dri("XTd", [T, DI], BF16)
        self.Yd = dri("Yd", [T, DI], BF16)
        self.YN = dri("YN", [DI // 128, NTC, 128, 128], BF16)
        self.tapd = {}
        for tname, shape, dt in [
            ("tap_h", [T, E], BF16), ("tap_qn", [H * 128, T], BF16),
            ("tap_qp", [H * 64, T], BF16), ("tap_kn", [H * 128, T], BF16),
            ("tap_kp", [64, T], BF16), ("tap_v", [T, H * 128], BF16),
            ("tap_ot", [H * 128, T], BF16), ("tap_x1", [T, E], F32),
            ("tap_x2", [T, E], F32), ("tap_z", [T, DI], BF16),
            ("tap_xt", [T, DI], BF16), ("tap_dt", [NH, T], F32),
            ("tap_y", [T, DI], BF16), ("tap_x3", [T, E], F32),
        ]:
            if tname in self.taps:
                self.tapd[tname] = dr(tname, shape, dt, k="ExternalOutput")

        sb = self.sb
        self.c_fc = sb.alloc("c_fc", [128, len(FCONSTS)], F32)
        self.c_ident = sb.alloc("c_ident", [128, 128], BF16)
        self.c_identf = sb.alloc("c_identf", [128, 128], F32)
        self.c_ones = sb.alloc("c_ones", [1, 128], F32)
        self.c_negones = sb.alloc("c_negones", [1, 128], F32)
        self.c_cos = sb.alloc("c_cos", [128, NTC * 32], F32)
        self.c_sin = sb.alloc("c_sin", [128, NTC * 32], F32)
        self.c_masks = sb.alloc("c_masks", [128, 4 * 512], BF16)
        self.c_mmask = sb.alloc("c_mmask", [128, 512], BF16)
        self.c_vec = sb.alloc("c_vec", [128, 256], F32)
        self.c_convw = sb.alloc("c_convw", [128, (CONV // 128) * 4], F32)
        self.c_dbc = sb.alloc("c_dbc", [128, NH], F32)
        self.c_small = sb.alloc("c_small", [NH, 8], F32)
        self.c_r2p1 = sb.alloc("c_r2p1", [128, NTC], F32)
        self.c_mr2 = sb.alloc("c_mr2", [128, NTC], F32)
        self.pers_end = sb.mark()

        self.PBF = nc.alloc_psum_tensor("PBF", [128, 2048], BF16)
        self.PSF = nc.alloc_psum_tensor("PSF", [128, 3072], F32)
        self._tr_war = {}
        self._load_consts()

    def _load_consts(self):
        p = self.p
        toks = []
        toks.append(p.dma("sp", self.c_fc[:], self.fconst[:, :]))
        for i, v in enumerate(FCONSTS):
            self.nc.const_aps.register(F32, v, self.c_fc[:, i:i + 1])
        toks.append(p.dma("sp", self.c_ident[:], self.ident[:, :]))
        toks.append(p.dma("sp", self.c_identf[:], self.identf[:, :]))
        toks.append(p.dma("sp", self.c_negones[:], self.negonesf[:, :]))
        toks.append(p.dma("sp", self.c_ones[:], self.onesf[:, :]))
        toks.append(p.dma("sp", self.c_cos[:], self.costm[:, :]))
        toks.append(p.dma("sp", self.c_sin[:], self.sintm[:, :]))
        toks.append(p.dma("sp", self.c_masks[:], self.masks[:, :]))
        toks.append(p.dma("sp", self.c_mmask[:], self.mmask[:, :]))
        toks.append(p.dma("sp", self.c_vec[:, 0:44], self.sg[:, :]))
        toks.append(p.dma("sp", self.c_vec[:, 44:88], self.su[:, :]))
        toks.append(p.dma("sp", self.c_vec[:, 88:104], self.sd[:, :]))
        toks.append(p.dma("sp", self.c_vec[:, 104:120], self.rmsw[:, :]))
        toks.append(p.dma("sp", self.c_vec[:, 120:152], self.normw[:, :]))
        toks.append(p.dma("sp", self.c_vec[:, 152:185], self.convb[:, :]))
        toks.append(p.dma("sp", self.c_vec[:, 185:197], self.qaln[:, :]))
        toks.append(p.dma("sp", self.c_vec[:, 197:201], self.kvln[:, :]))
        toks.append(p.dma("sp", self.c_convw[:], self.convw[:, :]))
        toks.append(p.dma("sp", self.c_dbc[:], self.dbc[:, :]))
        toks.append(p.dma("sp", self.c_small[0:NH, 0:1], self.dtb[:, :]))
        toks.append(p.dma("sp", self.c_small[0:NH, 1:2], self.alog[:, :]))
        self.const_tok = toks[-1]
        p.do("act", lambda e: e.activation(out=self.c_small[0:NH, 2:3],
                                           in_=self.c_small[0:NH, 1:2],
                                           func=AF.Exp, scale=1.0),
             waits=[self.const_tok])
        self.tok_A = p.do("dve", lambda e: e.tensor_scalar(
            out=self.c_small[0:NH, 2:3], in0=self.c_small[0:NH, 2:3],
            scalar1=-1.0, scalar2=None, op0=ALU.mult), waits=[p.tok("act")])
        p.barrier()

    def ln_stats(self, s_x, s_sum, s_ss, s_sc, s_bi, s_tmp, wait, width=E,
                 eps=LNEPS, sq_scr=None):
        p = self.p
        t1 = p.do("dve", lambda e: e.reduce_sum(s_sum, s_x, axis=AX.X), waits=[wait])
        t2 = p.do("act", lambda e: e.activation(out=sq_scr, in_=s_x, func=AF.Square,
                                                accum_out=s_ss), waits=[wait])
        t3 = p.do("dve", lambda e: e.tensor_scalar(out=s_sum, in0=s_sum,
                                                   scalar1=1.0 / width, scalar2=None,
                                                   op0=ALU.mult), waits=[t1])
        t4 = p.do("dve", lambda e: e.tensor_tensor(out=s_tmp, in0=s_sum, in1=s_sum,
                                                   op=ALU.mult), waits=[t3])
        t5 = p.do("dve", lambda e: e.scalar_tensor_tensor(out=s_ss, in0=s_ss,
                                                          scalar=1.0 / width, in1=s_tmp,
                                                          op0=ALU.mult, op1=ALU.subtract),
                  waits=[t2, t4])
        ta = p.do("act", lambda e: e.activation(out=s_tmp, in_=s_ss, func=AF.Sqrt,
                                                bias=eps, scale=1.0), waits=[t5])
        tr = p.do("dve", lambda e: e.reciprocal(out=s_sc, in_=s_tmp), waits=[ta])
        tb = p.do("dve", lambda e: e.scalar_tensor_tensor(out=s_bi, in0=s_sum,
                                                          scalar=-1.0, in1=s_sc,
                                                          op0=ALU.mult, op1=ALU.mult),
                  waits=[tr])
        return tb

    def transpose_tiles(self, src_fn, dst_fn, n, wait_src, batch=8,
                        evac_eng="dve"):
        p = self.p
        tok = None
        i = 0
        while i < n:
            cnt = min(batch, n - i)
            half = (self._tr_war.get("half", 0)) % 2
            self._tr_war["half"] = half + 1
            ps = self.PBF[:, half * 1024: half * 1024 + cnt * 128]
            waits = [wait_src] if wait_src else []
            if half in self._tr_war:
                waits.append(self._tr_war[half])
            for k in range(cnt):
                w = waits if k == 0 else []
                p.do("pe", lambda e, i=i, k=k, ps=ps: e.transpose(
                    ps[:, k * 128:(k + 1) * 128], src_fn(i + k), self.c_ident[:]),
                    waits=w, inc=(k == cnt - 1))
            tpe = p.tok("pe")
            tok = p.do(evac_eng, lambda e, i=i, cnt=cnt, ps=ps: (
                e.tensor_copy(dst_fn(i, cnt), ps[:, 0:cnt * 128])
                if evac_eng == "dve" else
                e.activation(out=dst_fn(i, cnt), in_=ps[:, 0:cnt * 128],
                             func=AF.Copy, scale=1.0)), waits=[tpe])
            self._tr_war[half] = tok
            i += cnt
        return tok

    def phase1a(self):
        sb, p = self.sb, self.p
        base = sb.mark()
        s_wqa = sb.alloc("s_wqa", [128, EC * QL], BF16)
        s_wqb = sb.alloc("s_wqb", [128, 12 * 3072], BF16)
        s_x = sb.alloc("p1_x", [128, E], F32)
        s_h = sb.alloc("p1_h", [128, E], BF16)
        s_sq = sb.alloc("p1_sq", [128, E], BF16)
        s_hfm = sb.alloc("p1_hfm", [128, EC * 512], BF16)
        s_qan = sb.alloc("p1_qan", [128, QL], BF16)
        s_qanfm = sb.alloc("p1_qanfm", [128, 12 * 512], BF16)
        s_stat = sb.alloc("p1_stat", [128, 16], F32)
        s_qev = sb.alloc("p1_qev", [128, 2 * 512], BF16)
        s_qpe = sb.alloc("p1_qpe", [128, 1024], BF16)
        s_rt1 = sb.alloc("p1_rt1", [128, 512], F32)
        s_rt2 = sb.alloc("p1_rt2", [128, 512], F32)
        ps_qa = self.PSF[:, 0:1536]
        ps_pe = self.PSF[:, 0:1024]
        ps_qn = self.PSF[:, 2048:3072]

        wqa_r = self.wqa.rearrange("(c p) n -> p c n", p=128)
        wqb_r = self.wqb.rearrange("(c p) n -> p c n", p=128)
        for half in range(2):
            p.dma("pool", s_wqa[:].rearrange("p (c n) -> p c n", c=EC)[:, half * 8:(half + 1) * 8, :],
                  wqa_r[:, half * 8:(half + 1) * 8, :])
        for half in range(2):
            wtok = p.dma("pool", s_wqb[:].rearrange("p (c n) -> p c n", c=12)[:, half * 6:(half + 1) * 6, :],
                         wqb_r[:, half * 6:(half + 1) * 6, :])
        for c in range(12):
            p.do("dve", lambda e, c=c: e.tensor_scalar(
                out=s_wqb[:, c * 3072:(c + 1) * 3072],
                in0=s_wqb[:, c * 3072:(c + 1) * 3072],
                scalar1=self.c_vec[:, 185 + c:186 + c], scalar2=None, op0=ALU.mult),
                waits=[wtok, self.const_tok] if c == 0 else [])
        wfold = p.tok("dve")

        last_qn_dma = {}
        last_qp_dma = {}
        for s in range(NSTRIP):
            for tl in range(4):
                tc = s * 4 + tl
                tx = p.dma("sp", s_x[:], self.x_in[tc * 128:(tc + 1) * 128, :],
                           waits=[p.tok("act"), p.tok("dve")])
                tb = self.ln_stats(s_x[:], s_stat[:, 0:1], s_stat[:, 1:2],
                                   s_stat[:, 2:3], s_stat[:, 3:4], s_stat[:, 4:5],
                                   tx, sq_scr=s_sq[:])
                th = p.do("dve", lambda e: e.tensor_scalar(
                    out=s_h[:], in0=s_x[:], scalar1=s_stat[:, 2:3],
                    scalar2=s_stat[:, 3:4], op0=ALU.mult, op1=ALU.add),
                    waits=[tb, p.tok("pe")])
                if "tap_h" in self.tapd:
                    p.dma("sp", self.tapd["tap_h"][tc * 128:(tc + 1) * 128, :], s_h[:],
                          waits=[th])
                self.transpose_tiles(
                    lambda i: s_h[:, i * 128:(i + 1) * 128],
                    lambda i0, cnt, tl=tl: bass.AP(s_hfm, i0 * 512 + tl * 128,
                                                   [[EC * 512, 128], [512, cnt], [1, 128]]),
                    EC, th, batch=8)
            tstrip = p.tok("dve")
            for tl in range(4):
                for seg in range(3):
                    for ec in range(EC):
                        p.do("pe", lambda e, tl=tl, seg=seg, ec=ec: e.matmul(
                            ps_qa[:, seg * 512:(seg + 1) * 512],
                            s_hfm[:, ec * 512 + tl * 128: ec * 512 + (tl + 1) * 128],
                            s_wqa[:, ec * QL + seg * 512: ec * QL + (seg + 1) * 512],
                            start=(ec == 0), stop=(ec == EC - 1)),
                            waits=([tstrip, p.tok("act")] if (seg == 0 and ec == 0) else []),
                            inc=(ec == EC - 1))
                tqa = p.tok("pe")
                tsq = None
                for seg in range(3):
                    tsq = p.do("act", lambda e, seg=seg: e.activation(
                        out=s_sq[:, 0:512], in_=ps_qa[:, seg * 512:(seg + 1) * 512],
                        func=AF.Square, accum_out=s_stat[:, 5 + seg:6 + seg]),
                        waits=[tqa] if seg == 0 else [tsq])
                ta1 = p.do("dve", lambda e: e.tensor_tensor(
                    out=s_stat[:, 8:9], in0=s_stat[:, 5:6], in1=s_stat[:, 6:7],
                    op=ALU.add), waits=[tsq])
                ta2 = p.do("dve", lambda e: e.tensor_tensor(
                    out=s_stat[:, 8:9], in0=s_stat[:, 8:9], in1=s_stat[:, 7:8],
                    op=ALU.add), waits=[ta1])
                ts2 = p.do("act", lambda e: e.activation(
                    out=s_stat[:, 9:10], in_=s_stat[:, 8:9], func=AF.Sqrt,
                    bias=RMSEPS_A, scale=1.0 / QL), waits=[ta2])
                tr = p.do("dve", lambda e: e.reciprocal(
                    out=s_stat[:, 10:11], in_=s_stat[:, 9:10]), waits=[ts2])
                for seg in range(3):
                    p.do("act", lambda e, seg=seg: e.activation(
                        out=s_qan[:, seg * 512:(seg + 1) * 512],
                        in_=ps_qa[:, seg * 512:(seg + 1) * 512],
                        func=AF.Copy, scale=s_stat[:, 10:11]), waits=[tr])
                tqan = p.tok("act")
                self.transpose_tiles(
                    lambda i: s_qan[:, i * 128:(i + 1) * 128],
                    lambda i0, cnt, tl=tl: bass.AP(s_qanfm, i0 * 512 + tl * 128,
                                                   [[12 * 512, 128], [512, cnt], [1, 128]]),
                    12, tqan, batch=4)
            tqanfm = p.tok("dve")
            for hc in range(H):
                bank = hc % 2
                waits = [tqanfm, wfold]
                if bank in last_qn_dma:
                    waits.append(last_qn_dma[bank])
                for qc in range(12):
                    p.do("pe", lambda e, hc=hc, qc=qc, bank=bank: e.matmul(
                        ps_qn[:, bank * 512:(bank + 1) * 512],
                        s_wqb[:, qc * 3072 + hc * 128: qc * 3072 + (hc + 1) * 128],
                        s_qanfm[:, qc * 512:(qc + 1) * 512],
                        start=(qc == 0), stop=(qc == 11)),
                        waits=(waits if qc == 0 else []), inc=(qc == 11))
                tq = p.tok("pe")
                tev = p.do("act", lambda e, hc=hc, bank=bank: e.activation(
                    out=s_qev[:, bank * 512:(bank + 1) * 512],
                    in_=ps_qn[:, bank * 512:(bank + 1) * 512],
                    func=AF.Copy, scale=SCALE_ATTN), waits=[tq])
                last_qn_dma[bank] = p.dma(
                    "sp", self.QN[hc * 128:(hc + 1) * 128, s * 512:(s + 1) * 512],
                    s_qev[:, bank * 512:(bank + 1) * 512], waits=[tev])
            tqn_act = p.tok("act")
            for tl in range(4):
                tc = s * 4 + tl
                for seg in range(2):
                    for qc in range(12):
                        p.do("pe", lambda e, tl=tl, seg=seg, qc=qc: e.matmul(
                            ps_pe[:, seg * 512:(seg + 1) * 512],
                            s_qanfm[:, qc * 512 + tl * 128: qc * 512 + (tl + 1) * 128],
                            s_wqb[:, qc * 3072 + 2048 + seg * 512: qc * 3072 + 2048 + (seg + 1) * 512],
                            start=(qc == 0), stop=(qc == 11)),
                            waits=([p.tok("dve"), tqn_act] if (seg == 0 and qc == 0) else []),
                            inc=(qc == 11))
                tpe = p.tok("pe")
                pp = ps_pe.rearrange("p (h d) -> p h d", d=64)
                cosap = bass.AP(self.c_cos, tc * 32, [[NTC * 32, 128], [0, 16], [1, 32]])
                sinap = bass.AP(self.c_sin, tc * 32, [[NTC * 32, 128], [0, 16], [1, 32]])
                qq = s_qpe[:].rearrange("p (h d) -> p h d", d=64)
                r1 = s_rt1[:].rearrange("p (h d) -> p h d", d=32)
                r2 = s_rt2[:].rearrange("p (h d) -> p h d", d=32)
                p.do("dve", lambda e, pp=pp, cosap=cosap: e.scalar_tensor_tensor(
                    out=s_rt1[:], in0=pp[:, :, 0:32], scalar=SCALE_ATTN, in1=cosap,
                    op0=ALU.mult, op1=ALU.mult), waits=[tpe])
                p.do("dve", lambda e, pp=pp, sinap=sinap: e.scalar_tensor_tensor(
                    out=s_rt2[:], in0=pp[:, :, 32:64], scalar=SCALE_ATTN, in1=sinap,
                    op0=ALU.mult, op1=ALU.mult))
                p.do("dve", lambda e, qq=qq, r1=r1, r2=r2: e.tensor_tensor(
                    out=qq[:, :, 0:32], in0=r1, in1=r2, op=ALU.subtract))
                p.do("dve", lambda e, pp=pp, cosap=cosap: e.scalar_tensor_tensor(
                    out=s_rt1[:], in0=pp[:, :, 32:64], scalar=SCALE_ATTN, in1=cosap,
                    op0=ALU.mult, op1=ALU.mult))
                p.do("dve", lambda e, pp=pp, sinap=sinap: e.scalar_tensor_tensor(
                    out=s_rt2[:], in0=pp[:, :, 0:32], scalar=SCALE_ATTN, in1=sinap,
                    op0=ALU.mult, op1=ALU.mult))
                trope = p.do("dve", lambda e, qq=qq, r1=r1, r2=r2: e.tensor_tensor(
                    out=qq[:, :, 32:64], in0=r1, in1=r2, op=ALU.add))
                for g in range(2):
                    waits = [trope]
                    if g in last_qp_dma:
                        waits.append(last_qp_dma[g])
                    tev = self.transpose_tiles(
                        lambda i, g=g: s_qpe[:, (g * 4 + i) * 128:(g * 4 + i + 1) * 128],
                        lambda i0, cnt, g=g: s_qev[:, g * 512 + i0 * 128: g * 512 + (i0 + cnt) * 128],
                        4, None, batch=4) if False else None
                    half = self._tr_war.get("half", 0) % 2
                    self._tr_war["half"] = half + 1
                    ps = self.PBF[:, half * 1024: half * 1024 + 512]
                    if half in self._tr_war:
                        waits.append(self._tr_war[half])
                    for k in range(4):
                        p.do("pe", lambda e, g=g, k=k, ps=ps: e.transpose(
                            ps[:, k * 128:(k + 1) * 128],
                            s_qpe[:, (g * 4 + k) * 128:(g * 4 + k + 1) * 128],
                            self.c_ident[:]), waits=(waits if k == 0 else []),
                            inc=(k == 3))
                    tev = p.do("dve", lambda e, g=g, ps=ps: e.tensor_copy(
                        s_qev[:, g * 512:(g + 1) * 512], ps[:, 0:512]),
                        waits=[p.tok("pe")])
                    self._tr_war[half] = tev
                    last_qp_dma[g] = p.dma(
                        "sp",
                        bass.AP(self.QP.tensor, (g * 4 * 128) * T + s * 512 + tl * 128,
                                [[T, 128], [128 * T, 4], [1, 128]]),
                        s_qev[:, g * 512:(g + 1) * 512].rearrange(
                            "p (c n) -> p c n", c=4), waits=[tev])
        sb.reset(base)
        p.barrier()




    def phase1b(self):
        sb, p = self.sb, self.p
        self.s_kn = sb.alloc("s_kn", [128, H * T], BF16)
        self.s_kp = sb.alloc("s_kp", [128, T], BF16)
        self.kv_base = sb.mark()
        base = sb.mark()
        s_wkva = sb.alloc("s_wkva", [128, EC * 576], BF16)
        s_wkvb = sb.alloc("s_wkvb", [128, 4 * 4096], BF16)
        s_x = sb.alloc("pb_x", [128, E], F32)
        s_h = sb.alloc("pb_h", [128, E], BF16)
        s_sq = sb.alloc("pb_sq", [128, E], BF16)
        s_hfm = sb.alloc("pb_hfm", [128, EC * 128], BF16)
        s_ckv = sb.alloc("pb_ckv", [128, KL], BF16)
        s_ckvfm = sb.alloc("pb_ckvfm", [128, 4 * 512], BF16)
        s_stat = sb.alloc("pb_stat", [128, 16], F32)
        s_kpe = sb.alloc("pb_kpe", [128, 64], BF16)
        s_vev = sb.alloc("pb_vev", [128, 2 * 512], BF16)
        s_rt1 = sb.alloc("pb_rt1", [128, 32], F32)
        s_rt2 = sb.alloc("pb_rt2", [128, 32], F32)
        ps_ckv = self.PSF[:, 0:512]
        ps_kpe = self.PSF[:, 512:576]
        ps_kn = self.PSF[:, 1024:2048]
        ps_v = self.PSF[:, 2048:3072]

        wkva_r = self.wkva.rearrange("(c p) n -> p c n", p=128)
        wkvb_r = self.wkvb.rearrange("(c p) n -> p c n", p=128)
        for half in range(2):
            p.dma("pool", s_wkva[:].rearrange("p (c n) -> p c n", c=EC)[:, half * 8:(half + 1) * 8, :],
                  wkva_r[:, half * 8:(half + 1) * 8, :])
        wtok = p.dma("pool", s_wkvb[:].rearrange("p (c n) -> p c n", c=4), wkvb_r)
        for c in range(4):
            p.do("dve", lambda e, c=c: e.tensor_scalar(
                out=s_wkvb[:, c * 4096:(c + 1) * 4096],
                in0=s_wkvb[:, c * 4096:(c + 1) * 4096],
                scalar1=self.c_vec[:, 197 + c:198 + c], scalar2=None, op0=ALU.mult),
                waits=[wtok, self.const_tok] if c == 0 else [])
        wfold = p.tok("dve")

        last_v_dma = {}
        last_kn = {}
        for s in range(NSTRIP):
            for tl in range(4):
                tc = s * 4 + tl
                tx = p.dma("sp", s_x[:], self.x_in[tc * 128:(tc + 1) * 128, :],
                           waits=[p.tok("act"), p.tok("dve")])
                tb = self.ln_stats(s_x[:], s_stat[:, 0:1], s_stat[:, 1:2],
                                   s_stat[:, 2:3], s_stat[:, 3:4], s_stat[:, 4:5],
                                   tx, sq_scr=s_sq[:])
                th = p.do("dve", lambda e: e.tensor_scalar(
                    out=s_h[:], in0=s_x[:], scalar1=s_stat[:, 2:3],
                    scalar2=s_stat[:, 3:4], op0=ALU.mult, op1=ALU.add),
                    waits=[tb, p.tok("pe")])
                self.transpose_tiles(
                    lambda i: s_h[:, i * 128:(i + 1) * 128],
                    lambda i0, cnt: bass.AP(s_hfm, i0 * 128,
                                            [[EC * 128, 128], [128, cnt], [1, 128]]),
                    EC, th, batch=8)
                thfm = p.tok("dve")
                for ec in range(EC):
                    p.do("pe", lambda e, ec=ec: e.matmul(
                        ps_ckv[:, 0:512],
                        s_hfm[:, ec * 128:(ec + 1) * 128],
                        s_wkva[:, ec * 576: ec * 576 + 512],
                        start=(ec == 0), stop=(ec == EC - 1)),
                        waits=([thfm, p.tok("act"), wtok] if ec == 0 else []),
                        inc=(ec == EC - 1))
                for ec in range(EC):
                    p.do("pe", lambda e, ec=ec: e.matmul(
                        ps_kpe[:, 0:64],
                        s_hfm[:, ec * 128:(ec + 1) * 128],
                        s_wkva[:, ec * 576 + 512: (ec + 1) * 576],
                        start=(ec == 0), stop=(ec == EC - 1)),
                        inc=(ec == EC - 1))
                tckv = p.tok("pe")
                p.do("act", lambda e: e.activation(
                    out=s_sq[:, 0:512], in_=ps_ckv[:, 0:512],
                    func=AF.Square, accum_out=s_stat[:, 5:6]), waits=[tckv])
                ts2 = p.do("act", lambda e: e.activation(
                    out=s_stat[:, 9:10], in_=s_stat[:, 5:6], func=AF.Sqrt,
                    bias=RMSEPS_A, scale=1.0 / KL), waits=[p.tok("act")])
                trr = p.do("dve", lambda e: e.reciprocal(
                    out=s_stat[:, 10:11], in_=s_stat[:, 9:10]), waits=[ts2])
                tckvn = p.do("act", lambda e: e.activation(
                    out=s_ckv[:, 0:512], in_=ps_ckv[:, 0:512],
                    func=AF.Copy, scale=s_stat[:, 10:11]), waits=[trr])
                cosap = bass.AP(self.c_cos, tc * 32, [[NTC * 32, 128], [1, 32]])
                sinap = bass.AP(self.c_sin, tc * 32, [[NTC * 32, 128], [1, 32]])
                p.do("dve", lambda e, cosap=cosap: e.tensor_tensor(
                    out=s_rt1[:], in0=ps_kpe[:, 0:32], in1=cosap, op=ALU.mult),
                    waits=[tckv])
                p.do("dve", lambda e, sinap=sinap: e.tensor_tensor(
                    out=s_rt2[:], in0=ps_kpe[:, 32:64], in1=sinap, op=ALU.mult))
                p.do("dve", lambda e: e.tensor_tensor(
                    out=s_kpe[:, 0:32], in0=s_rt1[:], in1=s_rt2[:], op=ALU.subtract))
                p.do("dve", lambda e, cosap=cosap: e.tensor_tensor(
                    out=s_rt1[:], in0=ps_kpe[:, 32:64], in1=cosap, op=ALU.mult))
                p.do("dve", lambda e, sinap=sinap: e.tensor_tensor(
                    out=s_rt2[:], in0=ps_kpe[:, 0:32], in1=sinap, op=ALU.mult))
                tkpe = p.do("dve", lambda e: e.tensor_tensor(
                    out=s_kpe[:, 32:64], in0=s_rt1[:], in1=s_rt2[:], op=ALU.add))
                half = self._tr_war.get("half", 0) % 2
                self._tr_war["half"] = half + 1
                pstr = self.PBF[:, half * 1024: half * 1024 + 128]
                w = [tkpe] + ([self._tr_war[half]] if half in self._tr_war else [])
                p.do("pe", lambda e, pstr=pstr: e.transpose(
                    pstr[0:64, 0:128], s_kpe[:, 0:64], self.c_ident[:]), waits=w)
                tkp = p.do("dve", lambda e, pstr=pstr, tc=tc: e.tensor_copy(
                    self.s_kp[0:64, tc * 128:(tc + 1) * 128], pstr[0:64, 0:128]),
                    waits=[p.tok("pe")])
                self._tr_war[half] = tkp
                self.transpose_tiles(
                    lambda i: s_ckv[:, i * 128:(i + 1) * 128],
                    lambda i0, cnt, tl=tl: bass.AP(s_ckvfm, i0 * 512 + tl * 128,
                                                   [[4 * 512, 128], [512, cnt], [1, 128]]),
                    4, tckvn, batch=4)
            tstrip = p.tok("dve")
            for hc in range(H):
                bank = hc % 2
                for kc in range(4):
                    p.do("pe", lambda e, hc=hc, kc=kc, bank=bank: e.matmul(
                        ps_kn[:, bank * 512:(bank + 1) * 512],
                        s_wkvb[:, kc * 4096 + hc * 128: kc * 4096 + (hc + 1) * 128],
                        s_ckvfm[:, kc * 512:(kc + 1) * 512],
                        start=(kc == 0), stop=(kc == 3)),
                        waits=([tstrip, wfold] + ([last_kn[bank]] if bank in last_kn else [])
                               if kc == 0 else []),
                        inc=(kc == 3))
                last_kn[bank] = p.do("dve", lambda e, hc=hc, bank=bank, s=s: e.tensor_copy(
                    self.s_kn[:, hc * T + s * 512:(hc * T) + (s + 1) * 512],
                    ps_kn[:, bank * 512:(bank + 1) * 512]), waits=[p.tok("pe")])
            for tl in range(4):
                tc = s * 4 + tl
                for seg in range(4):
                    bank = seg % 2
                    w0 = [p.tok("dve")]
                    if (bank,) in last_v_dma:
                        w0.append(last_v_dma[(bank,)])
                    for kc in range(4):
                        p.do("pe", lambda e, tl=tl, seg=seg, kc=kc, bank=bank: e.matmul(
                            ps_v[:, bank * 512:(bank + 1) * 512],
                            s_ckvfm[:, kc * 512 + tl * 128: kc * 512 + (tl + 1) * 128],
                            s_wkvb[:, kc * 4096 + 2048 + seg * 512: kc * 4096 + 2048 + (seg + 1) * 512],
                            start=(kc == 0), stop=(kc == 3)),
                            waits=(w0 if kc == 0 else []), inc=(kc == 3))
                    tev = p.do("act", lambda e, bank=bank: e.activation(
                        out=s_vev[:, bank * 512: bank * 512 + 512],
                        in_=ps_v[:, bank * 512:(bank + 1) * 512],
                        func=AF.Copy, scale=1.0), waits=[p.tok("pe")])
                    last_v_dma[(bank,)] = p.dma(
                        "sp", self.Vd[tc * 128:(tc + 1) * 128, seg * 512:(seg + 1) * 512],
                        s_vev[:, bank * 512: bank * 512 + 512], waits=[tev])
        sb.reset(base)
        p.barrier()

    def phase2(self):
        sb, p = self.sb, self.p
        base = sb.mark()
        s_v = sb.alloc("p2_v", [128, NTC * 2048], BF16)
        s_qn = sb.alloc("p2_qn", [128, H * 512], BF16)
        s_qp = sb.alloc("p2_qp", [128, 8 * 512], BF16)
        s_p = sb.alloc("p2_p", [128, 2 * 4 * 2048], BF16)
        s_pt = sb.alloc("p2_pt", [128, 2 * 1024], BF16)
        s_otev = sb.alloc("p2_otev", [128, 2 * 512], BF16)
        s_st = sb.alloc("p2_st", [128, 64], F32)
        ps_O = self.PSF[:, 2048:2560]

        vr = self.Vd.rearrange("(c p) n -> p c n", p=128)
        vv = s_v[:].rearrange("p (c n) -> p c n", c=NTC)
        tv = []
        for qtr in range(2):
            tv.append(p.dma("sp", vv[:, qtr * 8:(qtr + 1) * 8, :], vr[:, qtr * 8:(qtr + 1) * 8, :]))
        tkp2 = p.dma("pool", self.s_kp[64:128, :], self.s_kp[0:64, :])
        qnr = self.QN.rearrange("(c p) t -> p c t", p=128)
        qpr = self.QP.rearrange("(c p) t -> p c t", p=128)

        pending = None
        state = {"oev": {}, "odma": {}, "gi": 0}

        def emit_ptpv(info):
            s, h, par, nseg, tnorm = info
            nkt = nseg * 4
            opar = state["gi"] % 2
            state["gi"] += 1
            psO = self.PSF[:, 2048 + opar * 512: 2048 + (opar + 1) * 512]
            for kp in range(nkt // 2):
                kt0 = kp * 2
                half = self._tr_war.get("half", 0) % 2
                self._tr_war["half"] = half + 1
                psb = self.PBF[:, half * 1024: half * 1024 + 1024]
                w0 = list(tnorm)
                if half in self._tr_war:
                    w0.append(self._tr_war[half])
                for j in range(8):
                    kt, qt = kt0 + j // 4, j % 4
                    p.do("pe", lambda e, kt=kt, qt=qt, j=j, psb=psb, par=par: e.transpose(
                        psb[:, j * 128:(j + 1) * 128],
                        s_p[:, (par * 4 + qt) * 2048 + kt * 128:
                            (par * 4 + qt) * 2048 + (kt + 1) * 128],
                        self.c_ident[:]),
                        waits=(w0 if j == 0 else []), inc=(j == 7))
                tev = p.do("dve", lambda e, half=half, psb=psb: e.tensor_copy(
                    s_pt[:, half * 1024:(half + 1) * 1024], psb[:]),
                    waits=[p.tok("pe")])
                self._tr_war[half] = tev
                for j2 in range(2):
                    kt = kt0 + j2
                    w1 = [tev]
                    if kt == 0 and opar in state["oev"]:
                        w1.append(state["oev"][opar])
                    p.do("pe", lambda e, h=h, kt=kt, half=half, j2=j2, nkt=nkt, psO=psO: e.matmul(
                        psO, s_v[:, kt * 2048 + h * 128: kt * 2048 + (h + 1) * 128],
                        s_pt[:, half * 1024 + j2 * 512: half * 1024 + (j2 + 1) * 512],
                        start=(kt == 0), stop=(kt == nkt - 1)),
                        waits=(w1 if j2 == 0 else []), inc=(kt == nkt - 1))
            tO = p.tok("pe")
            w2 = [tO] + ([state["odma"][opar]] if opar in state["odma"] else [])
            tevo = p.do("dve", lambda e, opar=opar, psO=psO: e.tensor_copy(
                s_otev[:, opar * 512:(opar + 1) * 512], psO), waits=w2)
            state["oev"][opar] = tevo
            state["odma"][opar] = p.dma(
                "sp", self.OT[h * 128:(h + 1) * 128, s * 512:(s + 1) * 512],
                s_otev[:, opar * 512:(opar + 1) * 512], waits=[tevo])

        for s in range(NSTRIP):
            nseg = s + 1
            tq1 = p.dma("sp", s_qn[:].rearrange("p (c t) -> p c t", c=H),
                        qnr[:, :, s * 512:(s + 1) * 512], waits=[p.tok("pe")])
            tq2 = p.dma("sp", s_qp[:].rearrange("p (c t) -> p c t", c=8),
                        qpr[:, :, s * 512:(s + 1) * 512])
            bank_free = {}
            strip_start = p.tok("act")
            first_mm = True
            for h in range(H):
                par = (s * H + h) % 2
                texp = []
                bsets = {1: 4, 2: 2, 3: 1, 4: 1}[nseg]
                for qt in range(4):
                    bset = qt % bsets
                    b0 = bset * nseg * 512
                    for seg in range(nseg):
                        w0 = [bank_free.get((bset, seg), strip_start)]
                        if first_mm:
                            w0 += [tq1, tq2, tkp2] + tv
                            first_mm = False
                        p.do("pe", lambda e, h=h, qt=qt, seg=seg, b0=b0: e.matmul(
                            self.PSF[:, b0 + seg * 512: b0 + (seg + 1) * 512],
                            s_qn[:, h * 512 + qt * 128: h * 512 + (qt + 1) * 128],
                            self.s_kn[:, h * T + seg * 512: h * T + (seg + 1) * 512],
                            start=True, stop=False),
                            waits=w0, inc=False)
                        p.do("pe", lambda e, h=h, qt=qt, seg=seg, nseg=nseg, b0=b0: e.matmul(
                            self.PSF[:, b0 + seg * 512: b0 + (seg + 1) * 512],
                            s_qp[(h % 2) * 64:(h % 2) * 64 + 64,
                                 (h // 2) * 512 + qt * 128: (h // 2) * 512 + (qt + 1) * 128],
                            self.s_kp[(h % 2) * 64:(h % 2) * 64 + 64,
                                      seg * 512:(seg + 1) * 512],
                            start=False, stop=(seg != nseg - 1)),
                            inc=(seg != nseg - 1))
                        if seg == nseg - 1:
                            p.do("pe", lambda e, qt=qt, nseg=nseg, b0=b0: e.matmul(
                                self.PSF[:, b0 + (nseg - 1) * 512: b0 + nseg * 512],
                                self.c_ident[:], self.c_masks[:, qt * 512:(qt + 1) * 512],
                                start=False, stop=True), inc=True)
                        tSseg = p.tok("pe")
                        pcol = par * 32 + 16 + qt * 4 + seg
                        te = p.do("act", lambda e, qt=qt, seg=seg, par=par, b0=b0, pcol=pcol: e.activation(
                            out=s_p[:, (par * 4 + qt) * 2048 + seg * 512:
                                    (par * 4 + qt) * 2048 + (seg + 1) * 512],
                            in_=self.PSF[:, b0 + seg * 512: b0 + (seg + 1) * 512],
                            func=AF.Exp, scale=1.0,
                            accum_out=s_st[:, pcol:pcol + 1]),
                            waits=[tSseg])
                        bank_free[(bset, seg)] = te
                        texp.append(te)
                    p.do("dve", lambda e, qt=qt, nseg=nseg, par=par: e.reduce_sum(
                        s_st[:, par * 32 + 4 + qt:par * 32 + 5 + qt],
                        s_st[:, par * 32 + 16 + qt * 4: par * 32 + 16 + qt * 4 + nseg],
                        axis=AX.X), waits=[texp[-1]])
                tsum = p.tok("dve")
                trec = p.do("dve", lambda e, par=par: e.reciprocal(
                    out=s_st[:, par * 32 + 8:par * 32 + 12],
                    in_=s_st[:, par * 32 + 4:par * 32 + 8]), waits=[tsum] + texp)
                tnorm = []
                for qt in range(4):
                    tn = p.do("dve", lambda e, qt=qt, nseg=nseg, par=par: e.tensor_scalar(
                        out=s_p[:, (par * 4 + qt) * 2048: (par * 4 + qt) * 2048 + nseg * 512],
                        in0=s_p[:, (par * 4 + qt) * 2048: (par * 4 + qt) * 2048 + nseg * 512],
                        scalar1=s_st[:, par * 32 + 8 + qt:par * 32 + 9 + qt],
                        scalar2=None, op0=ALU.mult), waits=[trec])
                    tnorm.append(tn)
                info = (s, h, par, nseg, tnorm)
                if pending is not None:
                    emit_ptpv(pending)
                pending = info
        if pending is not None:
            emit_ptpv(pending)
        sb.reset(base)
        p.barrier()

    def phase2b(self):
        sb, p = self.sb, self.p
        base = self.sb.BASE
        sb.reset(max(base, self.pers_end))
        base = sb.mark()
        s_otf = sb.alloc("p2b_ot", [128, H * T], BF16)
        s_wo = sb.alloc("p2b_wo", [128, EC * E], BF16)
        s_x = sb.alloc("p2b_x", [128, 2 * E], F32)
        s_x1 = sb.alloc("p2b_x1", [128, 2 * E], F32)
        otr = self.OT.rearrange("(c p) t -> p c t", p=128)
        ov = s_otf[:].rearrange("p (c t) -> p c t", c=H)
        for qtr in range(2):
            tot = p.dma("sp", ov[:, qtr * 8:(qtr + 1) * 8, :], otr[:, qtr * 8:(qtr + 1) * 8, :])
        wor = self.wout.rearrange("(c p) n -> p c n", p=128)
        wv = s_wo[:].rearrange("p (c n) -> p c n", c=EC)
        for qtr in range(2):
            tw = p.dma("pool", wv[:, qtr * 8:(qtr + 1) * 8, :], wor[:, qtr * 8:(qtr + 1) * 8, :])
        add_tok = {}
        xread = {}
        x1dma = {}
        for tc in range(NTC):
            par = tc % 2
            xb = s_x[:, par * E:(par + 1) * E]
            x1b = s_x1[:, par * E:(par + 1) * E]
            tx = p.dma("sp", xb, self.x_in[tc * 128:(tc + 1) * 128, :],
                       waits=[xread.get(par, ("pe", 0))])
            tadd = None
            for eseg in range(4):
                bank = eseg % 2
                w0 = [add_tok.get(bank, ("pe", 0))]
                if tc == 0 and eseg == 0:
                    w0 += [tot, tw]
                for dvc in range(EC):
                    p.do("pe", lambda e, tc=tc, eseg=eseg, dvc=dvc, bank=bank: e.matmul(
                        self.PSF[:, bank * 512:(bank + 1) * 512],
                        s_otf[:, dvc * T + tc * 128: dvc * T + (tc + 1) * 128],
                        s_wo[:, dvc * E + eseg * 512: dvc * E + (eseg + 1) * 512],
                        start=(dvc == 0), stop=(dvc == EC - 1)),
                        waits=(w0 if dvc == 0 else []),
                        inc=(dvc == EC - 1))
                tmm = p.tok("pe")
                tadd = p.do("dve", lambda e, eseg=eseg, bank=bank, xb=xb, x1b=x1b: e.tensor_tensor(
                    out=x1b[:, eseg * 512:(eseg + 1) * 512],
                    in0=self.PSF[:, bank * 512:(bank + 1) * 512],
                    in1=xb[:, eseg * 512:(eseg + 1) * 512], op=ALU.add),
                    waits=[tmm, tx] + ([x1dma.get(par, ("pe", 0))] if eseg == 0 else []))
                add_tok[bank] = tadd
            xread[par] = tadd
            x1dma[par] = p.dma("sp", self.X1[tc * 128:(tc + 1) * 128, :], x1b,
                               waits=[tadd])
            if "tap_x1" in self.tapd:
                p.dma("sp", self.tapd["tap_x1"][tc * 128:(tc + 1) * 128, :], x1b,
                      waits=[tadd])
        sb.reset(base)
        p.barrier()

    def phase_ffn(self, x_src, x_dst, tap=None):
        sb, p = self.sb, self.p
        base = sb.mark()
        off_hfm = sb.mark()
        s_hfm = sb.alloc("f_hfm", [128, EC * 1024], BF16)
        s_dfm = sb.alloc("f_dfm", [128, EC * 1024], BF16, off=off_hfm)
        s_act = sb.alloc("f_act", [128, HC * 1024], BF16)
        s_wgu = sb.alloc("f_wgu", [128, 4 * EC * 128], BF16)
        s_wd = sb.alloc("f_wd", [128, 2 * HC * 128], BF16, off=None)
        off_x = sb.mark()
        s_x = sb.alloc("f_x", [128, 2 * E], F32)
        off_h = sb.mark()
        s_h = sb.alloc("f_h", [128, 2 * E], BF16)
        s_sq = sb.alloc("f_sq", [128, E], BF16)
        s_stat = sb.alloc("f_stat", [128, 16], F32)
        s_gt = sb.alloc("f_gt", [128, 2 * 512], BF16)
        s_xr = sb.alloc("f_xr", [128, E], F32, off=off_x)
        s_xo = sb.alloc("f_xo", [128, E], F32, off=off_h)
        NSL = HC
        wg_r = self.wg.rearrange("(c p) n -> p c n", p=128)
        wu_r = self.wu.rearrange("(c p) n -> p c n", p=128)
        wd_r = self.wd.rearrange("(c p) n -> p c n", p=128)

        def load_gu_slab(si, war):
            toks = []
            for j, (src_r, bufo) in enumerate([(wg_r, 0), (wu_r, 2)]):
                buf = (bufo + si % 2) * EC * 128
                dst = bass.AP(s_wgu, buf, [[4 * EC * 128, 128], [128, EC], [1, 128]])
                tl_ = p.dma("pool", dst, src_r[:, :, si * 128:(si + 1) * 128],
                            waits=war)
                tk = p.do("dve", lambda e, buf=buf: e.tensor_scalar(
                    out=s_wgu[:, buf:buf + EC * 128], in0=s_wgu[:, buf:buf + EC * 128],
                    scalar1=-128.0, scalar2=None, op0=ALU.add), waits=[tl_])
                toks.append(tk)
            return toks

        def load_wd_blk(ec, war):
            buf = (ec % 2) * HC * 128
            dst = bass.AP(s_wd, buf, [[2 * HC * 128, 128], [128, HC], [1, 128]])
            for cq in range(2):
                tl_ = p.dma("pool",
                            bass.AP(s_wd, buf + cq * 22 * 128,
                                    [[2 * HC * 128, 128], [128, 22], [1, 128]]),
                            wd_r[:, cq * 22:(cq + 1) * 22, ec * 128:(ec + 1) * 128],
                            waits=war)
            tk = p.do("dve", lambda e, buf=buf: e.tensor_scalar(
                out=s_wd[:, buf:buf + HC * 128], in0=s_wd[:, buf:buf + HC * 128],
                scalar1=-128.0, scalar2=None, op0=ALU.add), waits=[tl_])
            return tk

        ln_h_tok = {}
        ln_tr_tok = {}
        for hf in range(2):
            for tl in range(8):
                tc = hf * 8 + tl
                par = tc % 2
                sc = s_stat[:, par * 8: par * 8 + 8]
                xb = s_x[:, par * E:(par + 1) * E]
                hb = s_h[:, par * E:(par + 1) * E]
                tx = p.dma("sp", xb, x_src[tc * 128:(tc + 1) * 128, :],
                           waits=[ln_h_tok.get(par, ("pe", 0))])
                tb = self.ln_stats(xb, sc[:, 0:1], sc[:, 1:2],
                                   sc[:, 2:3], sc[:, 3:4], sc[:, 4:5],
                                   tx, sq_scr=s_sq[:])
                th = p.do("dve", lambda e, xb=xb, hb=hb, sc=sc: e.tensor_scalar(
                    out=hb, in0=xb, scalar1=sc[:, 2:3],
                    scalar2=sc[:, 3:4], op0=ALU.mult, op1=ALU.add),
                    waits=[tb, ln_tr_tok.get(par, ("pe", 0))])
                ln_h_tok[par] = th
                ln_tr_tok[par] = self.transpose_tiles(
                    lambda i, hb=hb: hb[:, i * 128:(i + 1) * 128],
                    lambda i0, cnt, tl=tl: bass.AP(s_hfm, i0 * 1024 + tl * 128,
                                                   [[EC * 1024, 128], [1024, cnt], [1, 128]]),
                    EC, th, batch=8)
            thalf = p.tok("dve")
            wtoks = {0: load_gu_slab(0, [p.tok("pe")])}
            last_ev = {}
            slab_war = {}
            for hc in range(HC):
                si = hc
                if si + 1 < NSL and (si + 1) not in wtoks:
                    wtoks[si + 1] = load_gu_slab(
                        si + 1, [slab_war.get((si + 1) % 2, ("pe", 0))])
                gbuf = (si % 2) * EC * 128
                ubuf = (2 + si % 2) * EC * 128
                for seg in range(2):
                    w0 = [thalf] + wtoks[si] + [last_ev.get(("g", seg), ("pe", 0)),
                                                last_ev.get(("u", seg), ("pe", 0))]
                    for ec in range(EC):
                        p.do("pe", lambda e, seg=seg, ec=ec, gbuf=gbuf: e.matmul(
                            self.PSF[:, seg * 512:(seg + 1) * 512],
                            s_wgu[:, gbuf + ec * 128: gbuf + ec * 128 + 128],
                            s_hfm[:, ec * 1024 + seg * 512: ec * 1024 + (seg + 1) * 512],
                            start=(ec == 0), stop=(ec == EC - 1)),
                            waits=(w0 if ec == 0 else []), inc=False)
                        p.do("pe", lambda e, seg=seg, ec=ec, ubuf=ubuf: e.matmul(
                            self.PSF[:, 1024 + seg * 512: 1024 + (seg + 1) * 512],
                            s_wgu[:, ubuf + ec * 128: ubuf + ec * 128 + 128],
                            s_hfm[:, ec * 1024 + seg * 512: ec * 1024 + (seg + 1) * 512],
                            start=(ec == 0), stop=(ec == EC - 1)),
                            inc=(ec == EC - 1))
                    tmm = p.tok("pe")
                    slab_war[si % 2] = tmm
                    tg = p.do("act", lambda e, hc=hc, seg=seg: e.activation(
                        out=s_gt[:, seg * 512:(seg + 1) * 512],
                        in_=self.PSF[:, seg * 512:(seg + 1) * 512],
                        func=AF.Silu, scale=self.c_vec[:, hc:hc + 1]), waits=[tmm])
                    last_ev[("g", seg)] = tg
                    tu = p.do("dve", lambda e, hc=hc, seg=seg: e.scalar_tensor_tensor(
                        out=s_act[:, hc * 1024 + seg * 512: hc * 1024 + (seg + 1) * 512],
                        in0=self.PSF[:, 1024 + seg * 512: 1024 + (seg + 1) * 512],
                        scalar=self.c_vec[:, 44 + hc:45 + hc],
                        in1=s_gt[:, seg * 512:(seg + 1) * 512],
                        op0=ALU.mult, op1=ALU.mult), waits=[tg])
                    last_ev[("u", seg)] = tu
            tact_done = [p.tok("dve"), p.tok("act")]
            dtok = {0: load_wd_blk(0, [p.tok("pe")])}
            last_dev = {}
            for ec in range(EC):
                if ec + 1 < EC:
                    dtok[ec + 1] = load_wd_blk(ec + 1, [last_dev.get((ec + 1) % 2, ("pe", 0))])
                wbuf = (ec % 2) * HC * 128
                for seg in range(2):
                    w0 = tact_done + [dtok[ec], last_dev.get(("e", seg), ("pe", 0))]
                    for c in range(HC):
                        p.do("pe", lambda e, seg=seg, c=c, wbuf=wbuf: e.matmul(
                            self.PSF[:, seg * 512:(seg + 1) * 512],
                            s_wd[:, wbuf + c * 128: wbuf + (c + 1) * 128],
                            s_act[:, c * 1024 + seg * 512: c * 1024 + (seg + 1) * 512],
                            start=(c == 0), stop=(c == HC - 1)),
                            waits=(w0 if c == 0 else []), inc=(c == HC - 1))
                    tmm = p.tok("pe")
                    last_dev[(ec % 2)] = tmm
                    td = p.do("act", lambda e, ec=ec, seg=seg: e.activation(
                        out=s_dfm[:, ec * 1024 + seg * 512: ec * 1024 + (seg + 1) * 512],
                        in_=self.PSF[:, seg * 512:(seg + 1) * 512],
                        func=AF.Copy, scale=self.c_vec[:, 88 + ec:89 + ec]), waits=[tmm])
                    last_dev[("e", seg)] = td
            tdown = p.tok("act")
            last_store = ("pe", 0)
            for tl in range(8):
                tc = hf * 8 + tl
                txr = p.dma("sp", s_xr[:], x_src[tc * 128:(tc + 1) * 128, :],
                            waits=[p.tok("dve")])
                for ecg in range(2):
                    half = self._tr_war.get("half", 0) % 2
                    self._tr_war["half"] = half + 1
                    psb = self.PBF[:, half * 1024: half * 1024 + 1024]
                    w0 = [tdown] + ([self._tr_war[half]] if half in self._tr_war else [])
                    for k in range(8):
                        p.do("pe", lambda e, k=k, tl=tl, ecg=ecg, psb=psb: e.transpose(
                            psb[:, k * 128:(k + 1) * 128],
                            s_dfm[:, (ecg * 8 + k) * 1024 + tl * 128:
                                  (ecg * 8 + k) * 1024 + (tl + 1) * 128],
                            self.c_ident[:]), waits=(w0 if k == 0 else []),
                            inc=(k == 7))
                    tev = p.do("dve", lambda e, ecg=ecg, psb=psb: e.tensor_tensor(
                        out=s_xo[:, ecg * 1024:(ecg + 1) * 1024],
                        in0=psb[:, 0:1024],
                        in1=s_xr[:, ecg * 1024:(ecg + 1) * 1024], op=ALU.add),
                        waits=[p.tok("pe"), txr, last_store])
                    self._tr_war[half] = tev
                tevd = p.tok("dve")
                last_store = p.dma("sp", x_dst[tc * 128:(tc + 1) * 128, :], s_xo[:],
                                   waits=[tevd])
                if tap and tap in self.tapd:
                    p.dma("sp", self.tapd[tap][tc * 128:(tc + 1) * 128, :], s_xo[:],
                          waits=[tevd])
        sb.reset(base)
        p.barrier()

    def phase4a(self):
        sb, p = self.sb, self.p
        self.s_bc = sb.alloc("m_bc", [128, T], BF16)
        self.s_cfm = sb.alloc("m_cfm", [64, T], BF16)
        self.s_btm = sb.alloc("m_btm", [128, NCH * 64], BF16)
        self.s_stm = sb.alloc("m_stm", [128, NCH * 64], F32)
        self.s_estm = sb.alloc("m_estm", [128, NCH * 64], F32)
        self.s_nstm = sb.alloc("m_nstm", [128, NCH * 64], F32)
        self.s_dttm = sb.alloc("m_dttm", [128, NCH * 64], F32)
        self.s_wtm = sb.alloc("m_wtm", [128, NCH * 64], F32)
        self.s_state = sb.alloc("m_state", [64, NH * 64], F32)
        self.s_stateb = sb.alloc("m_stateb", [64, NH * 64], BF16)
        self.off_dt = sb.mark()
        self.s_dt = sb.alloc("m_dt", [64, T], F32)
        self.off_ss = sb.mark()
        self.s_s = sb.alloc("m_s", [64, T], F32)
        self.s_z64 = sb.alloc("m_z64", [64, 128], F32)
        self.mpers_end = sb.mark()
        base = sb.mark()
        s_ufm = sb.alloc("m_ufm", [128, EC * T], BF16)
        s_w = sb.alloc("m_w", [128, 2 * EC * 512], BF16)
        off_mx = sb.mark()
        s_x = sb.alloc("m_x", [128, 2 * E], F32)
        off_u = sb.mark()
        s_u = sb.alloc("m_u", [128, 2 * E], BF16)
        off_sq = sb.mark()
        s_sq = sb.alloc("m_sq", [128, E], BF16)
        s_stat = sb.alloc("m_stat", [128, 32], F32)
        s_zev = sb.alloc("m_zev", [128, 2 * 512], BF16, off=self.off_ss)
        s_rmsb = sb.alloc("m_rmsb", [128, E], F32, off=self.off_dt)
        s_xbc = sb.alloc("m_xbc", [128, 3 + T + 1], BF16, off=off_mx)
        off_cv = off_mx + 4128
        s_cv = sb.alloc("m_cv", [128, T], F32, off=off_cv)
        s_cvb = sb.alloc("m_cvb", [128, T], BF16, off=off_u + 2048)
        s_xtev = sb.alloc("m_xtev", [128, 2 * 1024], BF16, off=off_sq)
        s_dtmp = sb.alloc("m_dtmp", [64, T], F32, off=off_cv)
        p.do("pool", lambda e: e.memset(self.s_z64[:], 0.0))
        trb = p.dma("pool", s_rmsb[:], self.rmsb[:, :])
        inp_r = self.inpT.rearrange("(c p) n -> p c n", p=128)

        def load_slab(sl, war):
            cols = 512 if sl < 16 else 192
            buf = (sl % 2) * EC * 512
            tl_ = None
            for cq in range(2):
                tl_ = p.dma("pool",
                            bass.AP(s_w, buf + cq * 8 * 512,
                                    [[2 * EC * 512, 128], [512, 8], [1, cols]]),
                            inp_r[:, cq * 8:(cq + 1) * 8, sl * 512: sl * 512 + cols],
                            waits=war)
            return tl_

        slab_war = {}
        wtoks = {0: load_slab(0, []), 1: load_slab(1, [])}
        zr = self.Zd.rearrange("(c p) n -> p c n", p=128)
        last_zev = {}
        cc_done = {}

        x_war = {}
        u_war = {}
        for tc in range(NTC):
            par = tc % 2
            st = s_stat[:, par * 16: par * 16 + 16]
            xb = s_x[:, par * E:(par + 1) * E]
            ub = s_u[:, par * E:(par + 1) * E]
            tx = p.dma("sp", xb, self.X2[tc * 128:(tc + 1) * 128, :],
                       waits=[x_war.get(par, ("pe", 0))])
            tb = self.ln_stats(xb, st[:, 0:1], st[:, 1:2], st[:, 2:3],
                               self.c_mr2[:, tc:tc + 1], st[:, 4:5], tx,
                               sq_scr=s_sq[:])
            t1 = p.do("dve", lambda e, tc=tc, st=st: e.tensor_scalar(
                out=self.c_r2p1[:, tc:tc + 1], in0=st[:, 2:3], scalar1=1.0,
                scalar2=None, op0=ALU.add), waits=[tb])
            t2 = p.do("dve", lambda e, st=st: e.tensor_tensor(
                out=st[:, 5:6], in0=st[:, 2:3], in1=st[:, 2:3],
                op=ALU.mult), waits=[tb])
            t3 = p.do("dve", lambda e, st=st: e.tensor_tensor(
                out=st[:, 5:6], in0=st[:, 5:6], in1=st[:, 1:2],
                op=ALU.mult), waits=[t2])
            t4 = p.do("act", lambda e, st=st: e.activation(
                out=st[:, 6:7], in_=st[:, 5:6], func=AF.Sqrt,
                bias=RMSEPS_B, scale=1.0), waits=[t3])
            t5 = p.do("dve", lambda e, st=st: e.reciprocal(
                out=st[:, 7:8], in_=st[:, 6:7]), waits=[t4])
            t6 = p.do("dve", lambda e, st=st: e.tensor_tensor(
                out=st[:, 8:9], in0=st[:, 2:3], in1=st[:, 7:8],
                op=ALU.mult), waits=[t5])
            t7 = p.do("dve", lambda e, st=st: e.scalar_tensor_tensor(
                out=st[:, 9:10], in0=st[:, 0:1], scalar=-1.0,
                in1=st[:, 8:9], op0=ALU.mult, op1=ALU.mult), waits=[t6])
            tu = p.do("dve", lambda e, xb=xb, ub=ub, st=st: e.tensor_scalar(
                out=ub, in0=xb, scalar1=st[:, 8:9],
                scalar2=st[:, 9:10], op0=ALU.mult, op1=ALU.add),
                waits=[t7, u_war.get(par, ("pe", 0))])
            x_war[par] = tu
            tur = p.do("dve", lambda e, ub=ub: e.tensor_tensor(
                out=ub, in0=ub, in1=s_rmsb[:], op=ALU.mult),
                waits=[tu] + ([trb] if tc == 0 else []))
            u_war[par] = self.transpose_tiles(
                lambda i, ub=ub: ub[:, i * 128:(i + 1) * 128],
                lambda i0, cnt, tc=tc: bass.AP(s_ufm, i0 * T + tc * 128,
                                               [[EC * T, 128], [T, cnt], [1, 128]]),
                EC, tur, batch=8)
            bank = tc % 2
            w0 = [u_war[par], wtoks[0], last_zev.get(bank, ("pe", 0))]
            for ec in range(EC):
                p.do("pe", lambda e, tc=tc, ec=ec, bank=bank: e.matmul(
                    self.PSF[:, bank * 512:(bank + 1) * 512],
                    s_ufm[:, ec * T + tc * 128: ec * T + (tc + 1) * 128],
                    s_w[:, ec * 512:(ec + 1) * 512],
                    start=(ec == 0), stop=(ec == EC - 1)),
                    waits=(w0 if ec == 0 else []), inc=(ec == EC - 1))
            slab_war[0] = p.tok("pe")
            tz = p.do("act", lambda e, bank=bank: e.activation(
                out=s_zev[:, bank * 512:(bank + 1) * 512],
                in_=self.PSF[:, bank * 512:(bank + 1) * 512],
                func=AF.Copy, scale=1.0), waits=[p.tok("pe")])
            last_zev[bank] = p.dma(
                "sp", zr[:, tc:tc + 1, 0:512],
                s_zev[:, bank * 512: bank * 512 + 512].rearrange(
                    "p (a n) -> p a n", a=1), waits=[tz])
        tuf = p.tok("dve")

        p.do("pool", lambda e: e.memset(s_xbc[:, 0:3], 0.0), waits=[tuf])
        for sl in range(1, 17):
            if sl + 1 < 17:
                wtoks[sl + 1] = load_slab(sl + 1, [slab_war.get((sl + 1) % 2, ("pe", 0))])
            buf = (sl % 2) * EC * 512
            if sl < 8:
                for tc in range(NTC):
                    bank = tc % 2
                    w0 = [tuf, wtoks[sl], last_zev.get(bank, ("pe", 0))]
                    for ec in range(EC):
                        p.do("pe", lambda e, tc=tc, ec=ec, bank=bank, buf=buf: e.matmul(
                            self.PSF[:, bank * 512:(bank + 1) * 512],
                            s_ufm[:, ec * T + tc * 128: ec * T + (tc + 1) * 128],
                            s_w[:, buf + ec * 512: buf + (ec + 1) * 512],
                            start=(ec == 0), stop=(ec == EC - 1)),
                            waits=(w0 if ec == 0 else []), inc=(ec == EC - 1))
                    slab_war[sl % 2] = p.tok("pe")
                    tz = p.do("act", lambda e, bank=bank: e.activation(
                        out=s_zev[:, bank * 512:(bank + 1) * 512],
                        in_=self.PSF[:, bank * 512:(bank + 1) * 512],
                        func=AF.Copy, scale=1.0), waits=[p.tok("pe")])
                    last_zev[bank] = p.dma(
                        "sp", zr[:, tc:tc + 1, sl * 512:(sl + 1) * 512],
                        s_zev[:, bank * 512: bank * 512 + 512].rearrange(
                            "p (a n) -> p a n", a=1), waits=[tz])
            else:
                ncc = 4 if sl < 16 else 1
                for ccl in range(ncc):
                    cc = (sl - 8) * 4 + ccl
                    w0 = [tuf, wtoks[sl], cc_done.get("psum", ("pe", 0)),
                          cc_done.get("psum_d", ("pe", 0))]
                    if cc == 0:
                        w0.append(p.tok("act"))
                    for st4 in range(4):
                        for ec in range(EC):
                            p.do("pe", lambda e, ccl=ccl, st4=st4, ec=ec, buf=buf: e.matmul(
                                self.PSF[:, st4 * 512:(st4 + 1) * 512],
                                s_w[:, buf + ec * 512 + ccl * 128: buf + ec * 512 + (ccl + 1) * 128],
                                s_ufm[:, ec * T + st4 * 512: ec * T + (st4 + 1) * 512],
                                start=(ec == 0), stop=(ec == EC - 1)),
                                waits=(w0 if (st4 == 0 and ec == 0) else []),
                                inc=(ec == EC - 1 and st4 == 3))
                    slab_war[sl % 2] = p.tok("pe")
                    tmm = p.tok("pe")
                    tcp1 = p.do("dve", lambda e: e.tensor_copy(
                        s_xbc[:, 3:3 + 1024], self.PSF[:, 0:1024]), waits=[tmm])
                    tcp2 = p.do("act", lambda e: e.activation(
                        out=s_xbc[:, 3 + 1024:3 + 2048], in_=self.PSF[:, 1024:2048],
                        func=AF.Copy, scale=1.0), waits=[tmm])
                    cc_done["psum"] = tcp2
                    cc_done["psum_d"] = tcp1
                    tdv, tpl = [tcp1, tcp2], [tcp1, tcp2]
                    for k in range(4):
                        wcol = self.c_convw[:, cc * 4 + k: cc * 4 + k + 1]
                        if k == 0:
                            tdv = [p.do("dve", lambda e, k=k, wcol=wcol: e.tensor_scalar(
                                out=s_cv[:, 0:1280], in0=s_xbc[:, k:k + 1280],
                                scalar1=wcol, scalar2=None, op0=ALU.mult),
                                waits=tdv + [cc_done.get("cv", ("pe", 0))])]
                            tpl = [p.do("dve", lambda e, k=k, wcol=wcol: e.tensor_scalar(
                                out=s_cv[:, 1280:2048], in0=s_xbc[:, 1280 + k:2048 + k],
                                scalar1=wcol, scalar2=None, op0=ALU.mult),
                                waits=tpl + [cc_done.get("cv", ("pe", 0))])]
                        else:
                            tdv = [p.do("dve", lambda e, k=k, wcol=wcol: e.scalar_tensor_tensor(
                                out=s_cv[:, 0:1280], in0=s_xbc[:, k:k + 1280],
                                scalar=wcol, in1=s_cv[:, 0:1280],
                                op0=ALU.mult, op1=ALU.add), waits=tdv)]
                            tpl = [p.do("dve", lambda e, k=k, wcol=wcol: e.scalar_tensor_tensor(
                                out=s_cv[:, 1280:2048], in0=s_xbc[:, 1280 + k:2048 + k],
                                scalar=wcol, in1=s_cv[:, 1280:2048],
                                op0=ALU.mult, op1=ALU.add), waits=tpl)]
                    tsil = p.do("act", lambda e, cc=cc: e.activation(
                        out=s_cvb[:], in_=s_cv[:], func=AF.Silu,
                        bias=self.c_vec[:, 152 + cc:153 + cc], scale=1.0),
                        waits=tdv + tpl)
                    cc_done["cv"] = tsil
                    if cc < 32:
                        for g in range(2):
                            half = self._tr_war.get("half", 0) % 2
                            self._tr_war["half"] = half + 1
                            psb = self.PBF[:, half * 1024: half * 1024 + 1024]
                            w0t = [tsil] + ([self._tr_war[half]] if half in self._tr_war else [])
                            for k in range(8):
                                p.do("pe", lambda e, g=g, k=k, psb=psb: e.transpose(
                                    psb[:, k * 128:(k + 1) * 128],
                                    s_cvb[:, (g * 8 + k) * 128:(g * 8 + k + 1) * 128],
                                    self.c_ident[:]), waits=(w0t if k == 0 else []),
                                    inc=(k == 7))
                            tev = p.do("dve", lambda e, g=g, psb=psb: e.tensor_copy(
                                s_xtev[:, g * 1024:(g + 1) * 1024], psb[:, 0:1024]),
                                waits=[p.tok("pe"), cc_done.get(("xt", g), ("pe", 0))])
                            self._tr_war[half] = tev
                            cc_done[("xt", g)] = p.dma(
                                "sp",
                                self.XTd.rearrange("(tc p) n -> p tc n", p=128)[
                                    :, g * 8:(g + 1) * 8, cc * 128:(cc + 1) * 128],
                                s_xtev[:, g * 1024:(g + 1) * 1024].rearrange(
                                    "p (a n) -> p a n", a=8), waits=[tev])
                    else:
                        tbc = p.do("dve", lambda e: e.tensor_copy(self.s_bc[:], s_cvb[:]),
                                   waits=[tsil])
                        tcd = p.dma("pool", self.s_cfm[0:64, :], self.s_bc[64:128, :],
                                    waits=[tbc])
                        for c in range(NCH):
                            half = self._tr_war.get("half", 0) % 2
                            self._tr_war["half"] = half + 1
                            psb = self.PBF[:, half * 1024: half * 1024 + 128]
                            w0t = [tbc] + ([self._tr_war[half]] if half in self._tr_war else [])
                            p.do("pe", lambda e, c=c, psb=psb: e.transpose(
                                psb[0:128, 0:64],
                                self.s_bc[0:64, c * 128:(c + 1) * 128],
                                self.c_ident[0:64, 0:64]), waits=w0t)
                            tev = p.do("dve", lambda e, c=c, psb=psb: e.tensor_copy(
                                self.s_btm[:, c * 64:(c + 1) * 64], psb[0:128, 0:64]),
                                waits=[p.tok("pe")])
                            self._tr_war[half] = tev
        buf = (16 % 2) * EC * 512
        for st4 in range(4):
            for ec in range(EC):
                p.do("pe", lambda e, st4=st4, ec=ec, buf=buf: e.matmul(
                    self.PSF[0:64, st4 * 512:(st4 + 1) * 512][:, 0:512],
                    s_w[:, buf + ec * 512 + 128: buf + ec * 512 + 192],
                    s_ufm[:, ec * T + st4 * 512: ec * T + (st4 + 1) * 512],
                    start=(ec == 0), stop=(ec == EC - 1)),
                    waits=([wtoks[16], cc_done.get("psum", ("pe", 0)),
                            p.tok("dve"), p.tok("act"), p.tok("pool")]
                           if (st4 == 0 and ec == 0) else []),
                    inc=(ec == EC - 1 and st4 == 3))
        tdt = p.tok("pe")
        for st4 in range(4):
            p.do("dve", lambda e, st4=st4: e.tensor_copy(
                s_dtmp[0:64, st4 * 512:(st4 + 1) * 512],
                self.PSF[0:64, st4 * 512:(st4 + 1) * 512]),
                waits=[tdt] if st4 == 0 else [])
        tev = p.do("dve", lambda e: e.tensor_scalar(
            out=s_dtmp[0:64, :], in0=s_dtmp[0:64, :],
            scalar1=self.c_small[0:NH, 0:1], scalar2=None, op0=ALU.add),
            waits=[p.tok("dve")])
        tab = p.do("act", lambda e: e.activation(
            out=self.s_dt[0:64, :], in_=s_dtmp[0:64, :], func=AF.Abs, scale=1.0),
            waits=[tev])
        te1 = p.do("act", lambda e: e.activation(
            out=self.s_s[0:64, :], in_=self.s_dt[0:64, :], func=AF.Exp, scale=-1.0),
            waits=[tab])
        te2 = p.do("act", lambda e: e.activation(
            out=self.s_s[0:64, :], in_=self.s_s[0:64, :], func=AF.Ln, bias=1.0,
            scale=1.0), waits=[te1])
        tr1 = p.do("dve", lambda e: e.tensor_scalar(
            out=s_dtmp[0:64, :], in0=s_dtmp[0:64, :], scalar1=0.0, scalar2=None,
            op0=ALU.max), waits=[te1])
        tsp = p.do("dve", lambda e: e.tensor_tensor(
            out=self.s_dt[0:64, :], in0=s_dtmp[0:64, :], in1=self.s_s[0:64, :],
            op=ALU.add), waits=[te2, tr1])
        tL = p.do("dve", lambda e: e.tensor_scalar(
            out=s_dtmp[0:64, :], in0=self.s_dt[0:64, :],
            scalar1=self.c_small[0:NH, 2:3], scalar2=None, op0=ALU.mult),
            waits=[tsp, self.tok_A])
        tsc = tL
        for c in range(NCH):
            tsc = p.do("dve", lambda e, c=c: e.tensor_tensor_scan(
                out=self.s_s[0:64, c * 128:(c + 1) * 128],
                data0=s_dtmp[0:64, c * 128:(c + 1) * 128],
                data1=self.s_z64[0:64, :], initial=0.0,
                op0=ALU.add, op1=ALU.add), waits=[tsc])
        tw = []
        for c in range(NCH):
            t1 = p.do("act", lambda e, c=c: e.activation(
                out=s_dtmp[0:64, c * 128:(c + 1) * 128],
                in_=self.s_s[0:64, c * 128:(c + 1) * 128], func=AF.Exp, scale=-1.0,
                bias=self.s_s[0:64, c * 128 + 127: c * 128 + 128]), waits=[tsc])
            tw.append(t1)
        tw2 = p.do("dve", lambda e: e.tensor_tensor(
            out=s_dtmp[0:64, :], in0=s_dtmp[0:64, :], in1=self.s_dt[0:64, :],
            op=ALU.mult), waits=tw)
        for (srcbuf, dstbuf, wtok) in [(self.s_s, self.s_stm, tsc),
                                       (self.s_dt, self.s_dttm, tsp),
                                       (s_dtmp, self.s_wtm, tw2)]:
            for c in range(NCH):
                tp = p.do("pe", lambda e, c=c, srcbuf=srcbuf: e.transpose(
                    self.PSF[0:128, 2048:2048 + 64],
                    srcbuf[0:64, c * 128:(c + 1) * 128], self.c_identf[0:64, 0:64]),
                    waits=[wtok, p.tok("dve")])
                p.do("dve", lambda e, c=c, dstbuf=dstbuf: e.tensor_copy(
                    dstbuf[:, c * 64:(c + 1) * 64], self.PSF[0:128, 2048:2048 + 64]),
                    waits=[tp])
        p.do("act", lambda e: e.activation(
            out=self.s_estm[:], in_=self.s_stm[:], func=AF.Exp, scale=1.0),
            waits=[p.tok("dve")])
        p.do("dve", lambda e: e.tensor_scalar(
            out=self.s_nstm[:], in0=self.s_stm[:], scalar1=-1.0, scalar2=None,
            op0=ALU.mult), waits=[p.tok("dve")])
        sb.reset(base)
        p.barrier()

    def phase4b(self):
        sb, p = self.sb, self.p
        base = sb.mark()
        s_xt = sb.alloc("b_xt", [128, 2 * DI], BF16)
        s_srow = sb.alloc("b_srow", [1, 2 * NH * 128], F32)
        s_y = sb.alloc("b_y", [128, 2 * DI], BF16)
        s_zc = sb.alloc("b_zc", [128, DI], BF16)
        s_mt = sb.alloc("b_mt", [128, 2 * 512], BF16)
        s_ed = sb.alloc("b_ed", [128, 2 * 512], BF16)
        s_yb = sb.alloc("b_yb", [128, 256], F32)
        s_t1 = sb.alloc("b_t1", [128, 256], F32)
        s_t2 = sb.alloc("b_t2", [128, 256], F32)
        s_st = sb.alloc("b_st", [64, 256], F32)
        s_esq = sb.alloc("b_esq", [64, 4], F32)
        s_esqr = sb.alloc("b_esqr", [1, 64], F32)
        s_decay = sb.alloc("b_decay", [64, 64], F32)
        ps_G = self.PSF[:, 2048:2176]
        ps_esq = self.PSF[:, 2560:2624]
        xt_r = self.XTd.rearrange("(tc p) n -> p tc n", p=128)

        tz = p.do("pool", lambda e: e.memset(self.s_state[:], 0.0))
        tz = p.do("pool", lambda e: e.memset(self.s_stateb[:], 0.0), waits=[tz])
        yr = self.Yd.rearrange("(tc p) n -> p tc n", p=128)
        zr4 = self.Zd.rearrange("(tc p) n -> p tc n", p=128)
        zc_tok = {0: p.dma("pool", s_zc[:], zr4[:, 0:1, :].rearrange("p a n -> p (a n)"))}
        xt_tok = {0: p.dma("sp", s_xt[:, 0:DI], xt_r[:, 0:1, :].rearrange("p a n -> p (a n)"))}
        last_y_dma = ("pe", 0)
        ywar = {"t": ("pe", 0)}
        st_evac = [tz]
        xt_mm = {}
        srow_mm = {}
        for c in range(NCH):
            if c + 1 < NCH:
                xt_tok[c + 1] = p.dma(
                    "sp", s_xt[:, ((c + 1) % 2) * DI: ((c + 1) % 2 + 1) * DI],
                    xt_r[:, c + 1:c + 2, :].rearrange("p a n -> p (a n)"),
                    waits=[("pe", 0) if c == 0 else xt_mm.get((c + 1) % 2, ("pe", 0))])
            xbuf = (c % 2) * DI
            srb = (c % 2) * NH * 128
            if c == 0:
                srow_tok = {0: p.dma("pool", s_srow[0:1, 0:NH * 128],
                                     self.s_s[0:64, 0:128], waits=[p.tok("pe")])}
            if c + 1 < NCH:
                srow_tok[c + 1] = p.dma(
                    "pool", s_srow[0:1, ((c + 1) % 2) * NH * 128: ((c + 1) % 2 + 1) * NH * 128],
                    self.s_s[0:64, (c + 1) * 128:(c + 2) * 128],
                    waits=[srow_mm.get((c + 1) % 2, ("pe", 0))])
            tst = srow_tok[c]
            te0 = p.do("act", lambda e, c=c: e.activation(
                out=s_esq[0:64, 0:1], in_=self.s_s[0:64, c * 128 + 127: c * 128 + 128],
                func=AF.Exp, scale=1.0), waits=[p.tok("dve")])
            tt0 = p.do("pe", lambda e: e.transpose(
                ps_esq[0:1, 0:64], s_esq[0:64, 0:1], self.c_identf[0:64, 0:64]),
                waits=[te0, st_evac[-1]])
            tt1 = p.do("dve", lambda e: e.tensor_copy(s_esqr[0:1, 0:64],
                                                      ps_esq[0:1, 0:64]), waits=[tt0])
            tt2 = p.do("pe", lambda e: e.matmul(
                self.PSF[0:64, 2560:2624], self.c_ones[0:1, 0:64], s_esqr[0:1, 0:64],
                start=True, stop=True), waits=[tt1])
            tdec = p.do("dve", lambda e: e.tensor_copy(s_decay[:], self.PSF[0:64, 2560:2624]),
                        waits=[tt2])
            tG = p.do("pe", lambda e, c=c: e.matmul(
                ps_G, self.s_bc[0:64, c * 128:(c + 1) * 128],
                self.s_cfm[0:64, c * 128:(c + 1) * 128], start=True, stop=True),
                waits=[p.tok("dve")])
            pend = None
            bank_tok = {}
            def stage_b(hg, par, tm2):
                for hh in range(4):
                    h = hg * 4 + hh
                    w_in = [tm2, xt_tok[c]] + [bank_tok.get("y", ("pe", 0))]
                    p.do("pe", lambda e, h=h, hh=hh, par=par, xbuf=xbuf: e.matmul(
                        self.PSF[:, 1024 + hh * 64: 1024 + (hh + 1) * 64],
                        s_mt[:, par * 512 + hh * 128: par * 512 + (hh + 1) * 128],
                        s_xt[:, xbuf + h * 64: xbuf + (h + 1) * 64],
                        start=True, stop=True),
                        waits=(w_in if hh == 0 else []), inc=False)
                    p.do("pe", lambda e, h=h, hh=hh, c=c: e.matmul(
                        self.PSF[:, 1536 + hh * 64: 1536 + (hh + 1) * 64],
                        self.s_cfm[0:64, c * 128:(c + 1) * 128],
                        self.s_stateb[0:64, h * 64:(h + 1) * 64],
                        start=True, stop=True),
                        waits=(st_evac[-1:] if hh == 0 else []), inc=(hh == 3))
                btm_rep = bass.AP(self.s_btm, c * 64,
                                  [[NCH * 64, 128], [0, 4], [1, 64]])
                wtm_b = bass.AP(self.s_wtm, c * 64 + hg * 4,
                                [[NCH * 64, 128], [1, 4], [0, 64]])
                tb2 = p.do("dve", lambda e, btm_rep=btm_rep, wtm_b=wtm_b, par=par: e.tensor_tensor(
                    out=s_ed[:, par * 512:par * 512 + 256], in0=btm_rep, in1=wtm_b,
                    op=ALU.mult), waits=[tm2])
                for hh in range(4):
                    h = hg * 4 + hh
                    p.do("pe", lambda e, hh=hh, h=h, par=par, xbuf=xbuf: e.matmul(
                        self.PSF[0:64, 2624 + hh * 64: 2624 + (hh + 1) * 64],
                        s_ed[:, par * 512 + hh * 64: par * 512 + hh * 64 + 64],
                        s_xt[:, xbuf + h * 64: xbuf + (h + 1) * 64],
                        start=True, stop=True),
                        waits=([tb2, tdec, st_evac[-1]] if hh == 0 else []),
                        inc=(hh == 3))
                tpe_all = p.tok("pe")
                esb = bass.AP(self.s_estm, c * 64 + hg * 4,
                              [[NCH * 64, 128], [1, 4], [0, 64]])
                ty1 = p.do("dve", lambda e, esb=esb: e.tensor_tensor(
                    out=s_yb[:], in0=self.PSF[:, 1536:1792], in1=esb, op=ALU.mult),
                    waits=[tpe_all])
                dbc = bass.AP(self.c_dbc, hg * 4, [[NH, 128], [1, 4], [0, 64]])
                ty2 = p.do("dve", lambda e, dbc=dbc, xbuf=xbuf, hg=hg: e.tensor_tensor(
                    out=s_t1[:], in0=s_xt[:, xbuf + hg * 256: xbuf + (hg + 1) * 256],
                    in1=dbc, op=ALU.mult), waits=[tpe_all])
                ty3 = p.do("dve", lambda e: e.tensor_tensor(
                    out=s_t2[:], in0=s_t1[:], in1=self.PSF[:, 1024:1280], op=ALU.add),
                    waits=[ty2])
                ybase = (c % 2) * DI
                p.do("dve", lambda e, hg=hg, ybase=ybase: e.tensor_tensor(
                    out=s_y[:, ybase + hg * 256: ybase + (hg + 1) * 256],
                    in0=s_t2[:], in1=s_yb[:], op=ALU.add),
                    waits=[ty3, ty1] + ([ywar.get(c % 2, ("pe", 0))] if hg == 0 else []))
                bank_tok["y"] = ty3
                dcb = bass.AP(s_decay, hg * 4, [[64, 64], [1, 4], [0, 64]])
                ts1 = p.do("dve", lambda e, dcb=dcb, hg=hg: e.tensor_tensor(
                    out=s_st[0:64, :], in0=self.s_state[0:64, hg * 256:(hg + 1) * 256],
                    in1=dcb, op=ALU.mult), waits=[tdec, tpe_all])
                ts2 = p.do("dve", lambda e, hg=hg: e.tensor_tensor(
                    out=self.s_state[0:64, hg * 256:(hg + 1) * 256], in0=s_st[0:64, :],
                    in1=self.PSF[0:64, 2624:2880], op=ALU.add), waits=[ts1])
                ts3 = p.do("dve", lambda e, hg=hg: e.tensor_copy(
                    self.s_stateb[0:64, hg * 256:(hg + 1) * 256],
                    self.s_state[0:64, hg * 256:(hg + 1) * 256]), waits=[ts2])
                st_evac.append(ts3)

            for hg in range(NH // 4):
                bank = hg % 2
                par = hg % 2
                psD = self.PSF[:, bank * 512:(bank + 1) * 512]
                w0 = [tst, p.tok("act")]
                p.do("pe", lambda e, hg=hg, psD=psD, srb=srb: e.matmul(
                    psD, self.c_ones[0:1, :], s_srow[0:1, srb + hg * 512: srb + (hg + 1) * 512],
                    start=True, stop=False), waits=w0, inc=False)
                for hh in range(4):
                    h = hg * 4 + hh
                    p.do("pe", lambda e, h=h, hh=hh, psD=psD, srb=srb: e.matmul(
                        psD[:, hh * 128:(hh + 1) * 128],
                        s_srow[0:1, srb + h * 128: srb + (h + 1) * 128],
                        self.c_negones[0:1, :], start=False, stop=False), inc=False)
                p.do("pe", lambda e, psD=psD: e.matmul(
                    psD, self.c_ident[:], self.c_mmask[:], start=False, stop=True),
                    inc=True)
                tD = p.tok("pe")
                ted = p.do("act", lambda e, par=par, psD=psD: e.activation(
                    out=s_ed[:, par * 512:(par + 1) * 512], in_=psD, func=AF.Exp),
                    waits=[tD])
                dtb = bass.AP(self.s_dttm, c * 64 + hg * 4,
                              [[NCH * 64, 128], [1, 4], [0, 128]])
                tm1 = p.do("dve", lambda e, dtb=dtb, par=par: e.tensor_tensor(
                    out=s_mt[:, par * 512:(par + 1) * 512],
                    in0=s_ed[:, par * 512:(par + 1) * 512], in1=dtb, op=ALU.mult),
                    waits=[ted, p.tok("pe")])
                gb = bass.AP(self.PSF, 2048, [[3072, 128], [0, 4], [1, 128]])
                tm2 = p.do("dve", lambda e, gb=gb, par=par: e.tensor_tensor(
                    out=s_mt[:, par * 512:(par + 1) * 512],
                    in0=s_mt[:, par * 512:(par + 1) * 512], in1=gb, op=ALU.mult),
                    waits=[tm1, tG])
                if pend is not None:
                    stage_b(*pend)
                pend = (hg, par, tm2)
            stage_b(*pend)
            tyd = p.tok("dve")
            xt_mm[c % 2] = p.tok("pe")
            srow_mm[c % 2] = p.tok("pe")
            yb = s_y[:, (c % 2) * DI: (c % 2 + 1) * DI]
            tsl = p.do("act", lambda e: e.activation(
                out=s_zc[:], in_=s_zc[:], func=AF.Silu, scale=1.0),
                waits=[zc_tok[c]])
            tys = p.do("dve", lambda e, yb=yb: e.tensor_tensor(
                out=yb, in0=yb, in1=s_zc[:], op=ALU.mult),
                waits=[tyd, tsl])
            tss = p.do("act", lambda e, c=c, yb=yb: e.activation(
                out=s_zc[:], in_=yb, func=AF.Square,
                accum_out=self.c_vec[:, 201 + c:202 + c]), waits=[tys])
            if c + 1 < NCH:
                zc_tok[c + 1] = p.dma(
                    "pool", s_zc[:],
                    zr4[:, c + 1:c + 2, :].rearrange("p a n -> p (a n)"),
                    waits=[tss])
            last_y_dma = p.dma("sp", yr[:, c:c + 1, :].rearrange("p a n -> p (a n)"),
                               yb, waits=[tss, last_y_dma])
            ywar[c % 2] = last_y_dma
        sb.reset(base)
        p.barrier()

    def phase4c(self):
        sb, p = self.sb, self.p
        sb.reset(self.pers_end)
        self.s_opt = sb.alloc("d_opt", [128, 32 * E], BF16)
        opt_r = self.outpT.rearrange("(c p) n -> p c n", p=128)
        ov = self.s_opt[:].rearrange("p (c n) -> p c n", c=32)
        for q in range(4):
            tw = p.dma("pool", ov[:, q * 8:(q + 1) * 8, :], opt_r[:, q * 8:(q + 1) * 8, :])
        twf = tw
        for cfold in range(32):
            twf = p.do("dve", lambda e, cfold=cfold: e.tensor_scalar(
                out=self.s_opt[:, cfold * E:(cfold + 1) * E],
                in0=self.s_opt[:, cfold * E:(cfold + 1) * E],
                scalar1=self.c_vec[:, 120 + cfold:121 + cfold], scalar2=None,
                op0=ALU.mult), waits=([tw] if cfold == 0 else []))
        self.opt_tok = twf
        base = sb.mark()
        s_opt = self.s_opt
        s_y = sb.alloc("c_y", [128, DI], BF16)
        s_ys = sb.alloc("c_ys", [128, 2 * DI], BF16)
        s_rst = sb.alloc("c_rst", [128, NTC], F32)
        s_ynt = sb.alloc("d_ynt", [128, 2 * 32 * 128], BF16)
        s_x2 = sb.alloc("d_x2", [128, E], F32)
        s_x3 = sb.alloc("d_x3", [128, E], F32)
        yr = self.Yd.rearrange("(tc p) n -> p tc n", p=128)
        tq = p.do("act", lambda e: e.activation(
            out=s_rst[:], in_=self.c_vec[:, 201:201 + NTC], func=AF.Sqrt,
            bias=RMSEPS_B, scale=1.0 / DI), waits=[p.tok("act")])
        tr = p.do("dve", lambda e: e.reciprocal(out=s_rst[:], in_=s_rst[:]), waits=[tq])
        last_ev = {}
        yn_mm = {}
        ys_war = {}
        y_war = ("pe", 0)
        for tc in range(NTC):
            par = tc % 2
            buf = par * 32 * 128
            ysb = s_ys[:, par * DI:(par + 1) * DI]
            ty = p.dma("sp", s_y[:], yr[:, tc:tc + 1, :].rearrange("p a n -> p (a n)"),
                       waits=[y_war])
            tyn = p.do("dve", lambda e, tc=tc, ysb=ysb: e.tensor_scalar(
                out=ysb, in0=s_y[:], scalar1=s_rst[:, tc:tc + 1], scalar2=None,
                op0=ALU.mult), waits=[ty, tr, ys_war.get(par, ("pe", 0))])
            y_war = tyn
            for g in range(4):
                half = self._tr_war.get("half", 0) % 2
                self._tr_war["half"] = half + 1
                psb = self.PBF[:, half * 1024: half * 1024 + 1024]
                w0 = [tyn] + ([self._tr_war[half]] if half in self._tr_war else [])
                for k in range(8):
                    p.do("pe", lambda e, g=g, k=k, psb=psb, ysb=ysb: e.transpose(
                        psb[:, k * 128:(k + 1) * 128],
                        ysb[:, (g * 8 + k) * 128:(g * 8 + k + 1) * 128],
                        self.c_ident[:]), waits=(w0 if k == 0 else []), inc=(k == 7))
                tev = p.do("dve", lambda e, g=g, psb=psb, buf=buf: e.tensor_copy(
                    s_ynt[:, buf + g * 1024: buf + (g + 1) * 1024], psb[:, 0:1024]),
                    waits=[p.tok("pe"), yn_mm.get(par, ("pe", 0))])
                self._tr_war[half] = tev
            tyn_done = p.tok("dve")
            ys_war[par] = tyn_done
            tx = p.dma("sp", s_x2[:], self.X2[tc * 128:(tc + 1) * 128, :],
                       waits=[p.tok("dve")])
            for eseg in range(4):
                bank = eseg % 2
                w0 = [tyn_done, twf, tx, last_ev.get(bank, ("pe", 0))]
                for xc in range(32):
                    p.do("pe", lambda e, xc=xc, eseg=eseg, bank=bank, buf=buf: e.matmul(
                        self.PSF[:, bank * 512:(bank + 1) * 512],
                        s_ynt[:, buf + xc * 128: buf + (xc + 1) * 128],
                        s_opt[:, xc * E + eseg * 512: xc * E + (eseg + 1) * 512],
                        start=(xc == 0), stop=(xc == 31)),
                        waits=(w0 if xc == 0 else []), inc=(xc == 31))
                tmm = p.tok("pe")
                yn_mm[par] = tmm
                t1 = p.do("dve", lambda e, eseg=eseg, bank=bank, tc=tc: e.scalar_tensor_tensor(
                    out=s_x3[:, eseg * 512:(eseg + 1) * 512],
                    in0=s_x2[:, eseg * 512:(eseg + 1) * 512],
                    scalar=self.c_r2p1[:, tc:tc + 1],
                    in1=self.PSF[:, bank * 512:(bank + 1) * 512],
                    op0=ALU.mult, op1=ALU.add), waits=[tmm])
                t2 = p.do("dve", lambda e, eseg=eseg, tc=tc: e.tensor_scalar(
                    out=s_x3[:, eseg * 512:(eseg + 1) * 512],
                    in0=s_x3[:, eseg * 512:(eseg + 1) * 512],
                    scalar1=self.c_mr2[:, tc:tc + 1], scalar2=None, op0=ALU.add),
                    waits=[t1])
                last_ev[bank] = t2
            tevd = p.tok("dve")
            p.dma("sp", self.X3[tc * 128:(tc + 1) * 128, :], s_x3[:], waits=[tevd])
            if "tap_x3" in self.tapd:
                p.dma("sp", self.tapd["tap_x3"][tc * 128:(tc + 1) * 128, :], s_x3[:],
                      waits=[tevd])
        sb.reset(self.pers_end)
        p.barrier()

    def tap_copy(self, name, src_ap):
        if name in self.tapd:
            t = p_split_dma(self.p, "sp", self.tapd[name][:, :], src_ap)
            return t
        return None

    def finish(self):
        self.p.barrier()


def p_split_dma(p, q, out_ap, in_ap, parts=2):
    n = out_ap.shape[0]
    tok = None
    step = (n + parts - 1) // parts
    for i in range(0, n, step):
        j = min(n, i + step)
        tok = p.dma(q, out_ap[i:j], in_ap[i:j])
    return tok


def build_mk(taps=(), phases=("p1a", "p1b", "p2", "p2b", "p3", "p4", "p5")):
    mk = MK(taps=taps)
    if "p1a" in phases:
        mk.phase1a()
    if "p1b" in phases:
        mk.phase1b()
    if "p2" in phases:
        mk.phase2()
    if "p2b" in phases:
        mk.phase2b()
    if "p3" in phases:
        mk.phase_ffn(mk.X1, mk.X2, tap="tap_x2")
    if "p4" in phases:
        mk.phase4a()
        mk.tap_copy("tap_z", mk.Zd)
        mk.tap_copy("tap_xt", mk.XTd)
        if "tap_dt" in mk.tapd:
            mk.p.dma("sp", mk.tapd["tap_dt"][0:NH, :], mk.s_dt[0:64, :])
        mk.phase4b()
        mk.tap_copy("tap_y", mk.Yd)
        mk.phase4c()
    if "p5" in phases:
        mk.phase_ffn(mk.X3, mk.out, tap=None)
    mk.tap_copy("tap_qn", mk.QN)
    mk.tap_copy("tap_qp", mk.QP)
    mk.tap_copy("tap_v", mk.Vd)
    if "tap_kn" in mk.tapd:
        mk.p.dma("sp", mk.tapd["tap_kn"].rearrange("(c p) t -> p c t", p=128),
                 mk.s_kn[:].rearrange("p (c t) -> p c t", c=H))
    if "tap_kp" in mk.tapd:
        mk.p.dma("sp", mk.tapd["tap_kp"][:, :], mk.s_kp[0:64, :])
    mk.finish()
    mk.p.emit()
    return mk


def kernel(**inputs):
    shared = prep_inputs(inputs)
    x = np.asarray(inputs["L0_attn_a_x"], np.float32)
    mk = build_mk()
    in_maps = []
    for b in range(8):
        m = dict(shared)
        m["x"] = np.ascontiguousarray(x[b])
        in_maps.append(m)
    res = run_bass_kernel_spmd(mk.nc, in_maps, core_ids=list(range(8)))
    return np.stack([r["out"] for r in res.results], axis=0)

